# version 01
# baseline (speedup 1.0000x reference)
import numpy as np
from contextlib import ExitStack

import concourse.bass as bass
import concourse.mybir as mybir
from concourse.bass import AP
from concourse.bass_utils import run_bass_kernel_spmd

F32 = mybir.dt.float32
BF16 = mybir.dt.bfloat16
AF = mybir.ActivationFunctionType
ALU = mybir.AluOpType
AX = mybir.AxisListType

NT = 2048
NCH = 16
D = 1024
DH = 64
NH = 16
NE = 64
EH = 256
SH = 2048
BS = [4, 8, 16, 32]
NN = [8, 4, 2, 1]
RL = [4, 8, 8, 8]
PAIRS = [[0, 1], [2, 3], [4, 5], [6, 7]]
L1_PARTS = "all"
L1_DEBUG_SHTAP = False

TAPS = {
    "xnb": ([128, 8, NT], BF16),
    "pqb": ([64, NT], BF16),
    "kb": ([64, NT], BF16),
    "vb": ([64, NT], BF16),
    "ctot": ([64, 65], F32),
    "crun": ([64, 65], F32),
    "attF": ([64, NT], BF16),
    "wct": ([128, 8, D], BF16),
    "gext": ([64, 65], BF16),
    "Bmat": ([64, 16, 64], BF16),
    "cwF": ([64, NT], BF16),
    "stot": ([128, 8, 65], F32),
    "pkF": ([128, 8, NT], BF16),
    "vta1": ([128, NCH, NH, 64], BF16),
    "S_run": ([128, 8, 65], F32),
    "attFt": ([128, 8, 256], BF16),
    "gsil": ([128, 2, 256], BF16),
    "gtil": ([128, 2, 256], BF16),
    "cwbS": ([128, 2, 256], BF16),
    "sb2sum": ([128, 8], F32),
    "sw1j": ([128, 8, SH], BF16),
    "sw2j": ([128, 16, D], BF16),
    "shdbg": ([128, 32, 128], F32),
}


class Sem:
    def __init__(self, ctx, nc, name):
        self.h = ctx.enter_context(nc.semaphore(name))
        self.v = 0
        self.name = name

    def at(self):
        return (self, self.v)


class Sched:
    ENGS = ("T", "V", "A", "G", "S")

    def __init__(self, P):
        self.P = P
        self.q = {e: [] for e in self.ENGS}

    def op(self, eng, emit, waits=(), inc=None, inc_n=1):
        self.q[eng].append((emit, [(s, v) for (s, v) in waits if v > 0], inc, inc_n))
        if inc is not None:
            inc.v += inc_n

    def dma(self, eng, out, in_, waits=(), sem=None):
        sem = sem if sem is not None else (self.P.sW if eng == "G" else self.P.sX)
        self.op(eng, lambda e, o=out, i=in_: e.dma_start(out=o, in_=i), waits, sem, 16)

    def flush(self):
        P = self.P
        secs = {"T": P.blk.tensor, "V": P.blk.vector, "A": P.blk.scalar,
                "G": P.blk.gpsimd, "S": P.blk.sync}
        for eng in self.ENGS:
            if not self.q[eng]:
                continue
            ops = self.q[eng]

            def body(e, ops=ops):
                waited = {}
                for emit, waits, inc, inc_n in ops:
                    for s, v in waits:
                        if waited.get(s.name, 0) < v:
                            e.wait_ge(s.h, v)
                            waited[s.name] = v
                    ins = emit(e)
                    if inc is not None:
                        ins.then_inc(inc.h, inc_n)

            secs[eng](body)
            self.q[eng] = []


class Prog:
    def __init__(self, nc, ctx):
        self.nc = nc
        self.ctx = ctx
        self.blk = ctx.enter_context(nc.Block())
        for n in ("sT", "sV", "sA", "sG", "sW", "sX", "sC", "sO"):
            setattr(self, n, Sem(ctx, nc, n))

    def sb(self, name, shape, dtype):
        t = self.ctx.enter_context(self.nc.sbuf_tensor(name, shape, dtype))
        setattr(self, name, t)
        return t

    def ps(self, name, shape, dtype):
        t = self.ctx.enter_context(self.nc.psum_tensor(name, shape, dtype))
        setattr(self, name, t)
        return t

    def carve(self, name, shape, dtype, off_bytes):
        import math
        elems = 1
        for d in shape[1:]:
            elems *= d
        nbytes = elems * (4 if dtype == F32 else 2)
        assert off_bytes % 4 == 0
        v = self.AR[0:shape[0], off_bytes // 2:(off_bytes + nbytes) // 2]
        if dtype == F32:
            v = v.bitcast(F32)
        if len(shape) == 3:
            v = v.rearrange("p (a b) -> p a b", a=shape[1])
        elif len(shape) == 4:
            v = v.rearrange("p (a b c) -> p a b c", a=shape[1], b=shape[2])
        setattr(self, name, v)
        return v


def bcast_free(t_ap, pdim, pcount, reps, inner):
    return AP(t_ap.tensor, t_ap.offset, [[pdim, pcount], [0, reps], [1, inner]])


def free_bcast(ap2d, reps):
    a = [list(x) for x in ap2d.ap]
    assert len(a) == 2
    return AP(ap2d.tensor, ap2d.offset, [a[0], a[1], [0, reps]])



def host_consts():
    c = {}
    c["c_idn"] = np.eye(128, dtype=np.float32)
    c["c_tri"] = np.triu(np.ones((128, 128), dtype=np.float32))
    hm = np.zeros((128, 2), np.float32)
    hm[0:64, 0] = 1.0
    hm[64:128, 1] = 1.0
    c["c_hmask"] = hm
    nm = np.zeros((64, 46), np.float32)
    col = 0
    for n in range(16):
        nm[4 * n:4 * n + 4, col] = 1.0
        col += 1
    for l in range(3):
        bs = BS[l]
        for n in range(NN[l]):
            nm[n * 2 * bs + bs:(n + 1) * 2 * bs, col] = 1.0
            col += 1
    for l in range(3):
        bs = BS[l]
        for n in range(NN[l]):
            nm[n * 2 * bs:n * 2 * bs + bs, col] = 1.0
            col += 1
    assert col == 44
    c["c_nodemask"] = nm
    pm = np.zeros((64, 9, 64), np.float32)
    for n in range(16):
        pm[4 * n:4 * n + 4, 0, 4 * n:4 * n + 4] = 1.0
    for l in range(4):
        bs = BS[l]
        for n in range(NN[l]):
            r0 = n * 2 * bs
            pm[r0:r0 + bs, 1 + l, r0 + bs:r0 + 2 * bs] = 1.0
            pm[r0 + bs:r0 + 2 * bs, 5 + l, r0:r0 + bs] = 1.0
    c["c_projmask"] = pm
    return c



def phase_ln(P, S, x_ready):
    pf = P.pfb
    for c8 in range(8):
        prev_mm = P.sT.at()
        S.op("V", lambda e, c=c8: e.tensor_copy(P.xbch[:], P.xst[:, c, :]),
             [x_ready, prev_mm], P.sV)
        xb_ok = P.sV.at()
        S.op("V", lambda e: e.tensor_tensor(P.xsq[:], P.xbch[:], P.xbch[:], ALU.mult),
             [], P.sV)
        sq_ok = P.sV.at()
        for q4 in range(4):
            b1, b2 = pf[q4 % 2], pf[2 + q4 % 2]
            prev_acc = P.sV.at()
            S.op("T", lambda e, q=q4, b=b1: e.matmul(
                b[0:1, :], P.onesk[:], P.xbch[:, 512 * q:512 * (q + 1)],
                start=True, stop=True), [xb_ok, prev_acc], P.sT)
            mm1 = P.sT.at()
            S.op("T", lambda e, q=q4, b=b2: e.matmul(
                b[0:1, :], P.onesk[:], P.xsq[:, 512 * q:512 * (q + 1)],
                start=True, stop=True), [sq_ok], P.sT)
            mm2 = P.sT.at()
            if c8 == 0:
                S.op("V", lambda e, q=q4, b=b1: e.tensor_copy(
                    P.accm[0:1, 512 * q:512 * (q + 1)], b[0:1, :]), [mm1], P.sV)
                S.op("V", lambda e, q=q4, b=b2: e.tensor_copy(
                    P.accv[0:1, 512 * q:512 * (q + 1)], b[0:1, :]), [mm2], P.sV)
            else:
                S.op("V", lambda e, q=q4, b=b1: e.tensor_tensor(
                    P.accm[0:1, 512 * q:512 * (q + 1)], P.accm[0:1, 512 * q:512 * (q + 1)],
                    b[0:1, :], ALU.add), [mm1], P.sV)
                S.op("V", lambda e, q=q4, b=b2: e.tensor_tensor(
                    P.accv[0:1, 512 * q:512 * (q + 1)], P.accv[0:1, 512 * q:512 * (q + 1)],
                    b[0:1, :], ALU.add), [mm2], P.sV)
    S.op("V", lambda e: e.tensor_scalar(P.accm, P.accm, 1.0 / D, None, ALU.mult),
         [], P.sV)
    S.op("V", lambda e: e.tensor_scalar(P.accv, P.accv, 1.0 / D, None, ALU.mult),
         [], P.sV)
    S.op("V", lambda e: e.tensor_tensor(P.rrow, P.accm, P.accm, ALU.mult), [], P.sV)
    S.op("V", lambda e: e.tensor_tensor(P.accv, P.accv, P.rrow, ALU.subtract),
         [], P.sV)
    S.op("V", lambda e: e.tensor_scalar(P.accv, P.accv, 1e-5, None, ALU.add),
         [], P.sV)
    var_ok = P.sV.at()
    S.op("A", lambda e: e.activation(P.accv, P.accv, AF.Ln), [var_ok], P.sA)
    S.op("A", lambda e: e.activation(P.rrow, P.accv, AF.Exp, scale=-0.5), [], P.sA)
    rstd_ok = P.sA.at()
    for q4 in range(4):
        prev_v = P.sV.at()
        S.op("T", lambda e, q=q4: e.matmul(
            pf[4][:], P.ones1f[:], P.accm[0:1, 512 * q:512 * (q + 1)],
            start=True, stop=True), [rstd_ok, prev_v], P.sT)
        S.op("T", lambda e, q=q4: e.matmul(
            pf[5][:], P.ones1f[:], P.rrow[0:1, 512 * q:512 * (q + 1)],
            start=True, stop=True), [], P.sT)
        bc_mm = P.sT.at()
        for c8 in range(8):
            sl = slice(512 * q4, 512 * (q4 + 1))
            S.op("V", lambda e, c=c8, s=sl: e.tensor_tensor(
                P.xsc32[:], P.xst[:, c, s], pf[4][:], ALU.subtract),
                [bc_mm], P.sV)
            S.op("V", lambda e, c=c8, s=sl: e.tensor_tensor(
                P.xnb[:, c, s], P.xsc32[:], pf[5][:], ALU.mult), [], P.sV)
    return P.sV.at()


ELU_MODE = "copyfirst"


def elu1_evac(P, S, src_psum, dst, tmp_a, tmp_r, waits):
    if ELU_MODE == "copyfirst":
        S.op("V", lambda e: e.tensor_copy(tmp_r, src_psum), waits, P.sV)
        c_ok = P.sV.at()
        S.op("A", lambda e: e.activation(tmp_a, tmp_r, AF.Exp), [c_ok], P.sA)
        a_ok = P.sA.at()
        S.op("V", lambda e: e.tensor_scalar(tmp_r, tmp_r, 0.0, None, ALU.max),
             [a_ok], P.sV)
        S.op("V", lambda e: e.scalar_tensor_tensor(dst, tmp_a, 1.0, tmp_r,
                                                   ALU.min, ALU.add), [], P.sV)
        return P.sV.at()
    if ELU_MODE == "split64":
        np_ = src_psum.shape[0]
        hp = np_ // 2
        for ph in range(2):
            ps = slice(hp * ph, hp * (ph + 1))
            S.op("A", lambda e, ps=ps: e.activation(tmp_a[ps], src_psum[ps],
                                                    AF.Exp), waits, P.sA)
            a_ok = P.sA.at()
            S.op("V", lambda e, ps=ps: e.tensor_scalar(
                tmp_r[ps], src_psum[ps], 0.0, None, ALU.max),
                list(waits) + [a_ok], P.sV)
        S.op("V", lambda e: e.scalar_tensor_tensor(dst, tmp_a, 1.0, tmp_r,
                                                   ALU.min, ALU.add), [], P.sV)
        return P.sV.at()
    S.op("A", lambda e: e.activation(tmp_a, src_psum, AF.Exp), waits, P.sA)
    a_ok = P.sA.at()
    S.op("V", lambda e: e.tensor_scalar(tmp_r, src_psum, 0.0, None, ALU.max),
         list(waits) + [a_ok], P.sV)
    S.op("V", lambda e: e.scalar_tensor_tensor(dst, tmp_a, 1.0, tmp_r, ALU.min, ALU.add),
         [], P.sV)
    return P.sV.at()



def build_l0(P, S, H, cc0_in, cc0_out, xn_ok, w_ld, x_ld):
    pf = P.pfb

    for tt in range(4):
        sl = slice(512 * tt, 512 * (tt + 1))
        prev_ev = P.sV.at()
        prev_ac = P.sA.at()
        for wi, (wname, prow, bank) in enumerate(
                (("l0wq", 0, pf[0]), ("l0wk", 64, pf[0]), ("l0wv", 0, pf[1]))):
            for ic in range(8):
                S.op("T", lambda e, w=wname, p=prow, b=bank, i=ic, s=sl: e.matmul(
                    b[p:p + 64, :], getattr(P, w)[:, i, :], P.xnb[:, i, s],
                    start=(i == 0), stop=(i == 7)),
                    [xn_ok, w_ld, prev_ev, prev_ac] if ic == 0 else [], P.sT)
        mm_ok = P.sT.at()
        elu1_evac(P, S, pf[0][0:64, :], P.pqb[0:64, sl], P.tmpa,
                  P.tmpr, [mm_ok])
        elu1_evac(P, S, pf[0][64:128, :], P.kb[0:64, sl], P.etmp2[0:64, 0:512],
                  P.etmp2[0:64, 512:1024], [mm_ok])
        S.op("V", lambda e, s=sl: e.tensor_copy(P.vb[0:64, s], pf[1][0:64, :]),
             [mm_ok], P.sV)
    qkv_ok = P.sV.at()

    S.op("V", lambda e: e.memset(P.vexta[:, :, 64], 1.0), [], P.sV)
    st_done = None
    for c in range(NCH):
        sl = slice(128 * c, 128 * (c + 1))
        pbk = P.pbt if c % 2 == 0 else P.pbt2
        prev_ev = P.sV.at()
        S.op("T", lambda e, s=sl, b=pbk: e.transpose(
            b[:, 0:64], P.kb[0:64, s], P.idn[0:64, 0:64]), [qkv_ok, prev_ev], P.sT)
        S.op("T", lambda e, s=sl, b=pbk: e.transpose(
            b[:, 64:128], P.vb[0:64, s], P.idn[0:64, 0:64]), [], P.sT)
        tr_ok = P.sT.at()
        S.op("V", lambda e, c=c, b=pbk: e.tensor_copy(P.kTa[:, c, :], b[:, 0:64]),
             [tr_ok], P.sV)
        S.op("V", lambda e, c=c, b=pbk: e.tensor_copy(P.vexta[:, c, 0:64], b[:, 64:128]),
             [], P.sV)
        ev_ok = P.sV.at()
        S.op("T", lambda e, c=c: e.matmul(
            pf[2][0:64, 0:65], P.kTa[:, c, :], P.vexta[:, c, :],
            start=(c == 0), stop=(c == NCH - 1)), [ev_ok], P.sT)
        st_done = P.sT.at()
    S.op("V", lambda e: e.tensor_copy(P.ctot[:], pf[2][0:64, 0:65]), [st_done], P.sV)
    ctot_ok = P.sV.at()
    S.dma("S", cc0_in[:], P.ctot[:], [ctot_ok])
    stg_ok = P.sX.at()
    S.op("G", lambda e: e.collective_compute(
        "AllGather", ALU.bypass, replica_groups=PAIRS,
        ins=[cc0_in[:]], outs=[cc0_out[:]]), [stg_ok], P.sC)
    cc_ok = P.sC.at()
    if L1_PARTS == "cc1":
        S.flush()
        return cc_ok

    S.op("V", lambda e: e.memset(P.Bmat[:], 0.0), [], P.sV)
    bm_z = P.sV.at()
    for l in range(4):
        for (wa, wb2, idx) in ((f"l0ULT{l}", f"l0ULT{l}", 0),
                               (f"l0VRT{l}", f"l0VRT{l}", 1),
                               (f"l0ULT{l}", f"l0VLT{l}", 2),
                               (f"l0VRT{l}", f"l0URT{l}", 3)):
            prev = P.sV.at()
            S.op("T", lambda e, a=wa, b=wb2, l=l: e.matmul(
                pf[3][0:BS[l], 0:BS[l]], getattr(P, a)[:], getattr(P, b)[:],
                start=True, stop=True), [x_ld, prev], P.sT)
            mm_ok = P.sT.at()
            S.op("V", lambda e, l=l: e.tensor_copy(
                P.bstg[0:BS[l], 0:BS[l]], pf[3][0:BS[l], 0:BS[l]]), [mm_ok], P.sV)
            ev_ok = P.sV.at()
            bs = BS[l]
            for n in range(NN[l]):
                r0 = n * 2 * bs + (bs if idx in (1, 3) else 0)
                S.dma("S", P.Bmat[r0:r0 + bs, 4 * l + idx, r0:r0 + bs],
                      P.bstg[0:bs, 0:bs], [ev_ok, bm_z])
    bm_ok = P.sX.at()

    prev = P.sV.at()
    S.op("T", lambda e: e.matmul(pf[3][0:4, 0:512], P.l0ldT[:], P.l0da[:, 0:512],
                                 start=True, stop=True), [x_ld, w_ld, prev], P.sT)
    S.op("T", lambda e: e.matmul(pf[4][0:4, 0:512], P.l0ldT[:], P.l0da[:, 512:1024],
                                 start=True, stop=True), [], P.sT)
    amm = P.sT.at()
    S.op("V", lambda e: e.tensor_copy(P.amat[0:4, 0:512], pf[3][0:4, 0:512]), [amm], P.sV)
    S.op("V", lambda e: e.tensor_copy(P.amat[0:4, 512:1024], pf[4][0:4, 0:512]),
         [], P.sV)
    a_ok = P.sV.at()
    S.op("T", lambda e: e.matmul(pf[3][0:50, 0:512], P.l0luT[:], P.amat[0:4, 0:512],
                                 start=True, stop=True), [a_ok], P.sT)
    S.op("T", lambda e: e.matmul(pf[4][0:50, 0:512], P.l0luT[:], P.amat[0:4, 512:1024],
                                 start=True, stop=True), [], P.sT)
    bmm = P.sT.at()
    S.op("V", lambda e: e.tensor_copy(P.bmat2[0:50, 0:512], pf[3][0:50, 0:512]),
         [bmm], P.sV)
    S.op("V", lambda e: e.tensor_copy(P.bmat2[0:50, 512:1024], pf[4][0:50, 0:512]),
         [], P.sV)
    b_ok = P.sV.at()
    for ic in range(8):
        S.dma("S", P.wtb[:, 0, :], H["l0_WT"][ic], [P.sV.at()])
        wt_ld = P.sX.at()
        prev = P.sV.at()
        for half in range(2):
            S.op("T", lambda e, i=ic, h=half: e.matmul(
                pf[3 + h][:, :], P.bmat2[0:50, 128 * i:128 * (i + 1)],
                P.l0uaT[:, 512 * h:512 * (h + 1)], start=True, stop=True),
                [b_ok, prev], P.sT)
        t3_ok = P.sT.at()
        for half in range(2):
            S.op("V", lambda e, i=ic, h=half: e.tensor_tensor(
                P.wct[:, i, 512 * h:512 * (h + 1)],
                P.wtb[:, 0, 512 * h:512 * (h + 1)],
                pf[3 + h][:, :], ALU.add), [t3_ok, wt_ld], P.sV)
    wct_ok = P.sV.at()
    for oc in range(8):
        for tt in range(4):
            sl = slice(512 * tt, 512 * (tt + 1))
            bank = pf[5] if (oc * 4 + tt) % 2 == 0 else pf[2]
            prev = P.sV.at()
            for ic in range(8):
                S.op("T", lambda e, i=ic, o=oc, s=sl, b=bank: e.matmul(
                    b[:], P.wct[:, i, 128 * o:128 * (o + 1)], P.xnb[:, i, s],
                    start=(i == 0), stop=(i == 7)),
                    [wct_ok, prev] if ic == 0 else [], P.sT)
            mm_ok = P.sT.at()
            S.op("V", lambda e, o=oc, s=sl, b=bank: e.scalar_tensor_tensor(
                P.xst[:, o, s], b[:], P.l0b[:, o:o + 1], P.xst[:, o, s],
                ALU.add, ALU.add), [mm_ok], P.sV)
    ffn_ok = P.sV.at()

    S.dma("S", P.gth[:], cc0_out.rearrange("(r p) n -> p r n", r=2), [cc_ok])
    gth_ok = P.sX.at()
    S.op("V", lambda e: e.tensor_scalar(P.crun[:], P.gth[:, 0, :], P.pmask[0:64, 0:1],
                                        None, ALU.mult), [gth_ok], P.sV)
    P.crun_w = P.sV.at()

    build_l0_attn2(P, S, bm_ok, qkv_ok)
    att_ok = P.sV.at()

    for oc in range(8):
        for tt in range(4):
            sl = slice(512 * tt, 512 * (tt + 1))
            bank = pf[0] if (oc * 4 + tt) % 2 == 0 else pf[1]
            prev = P.sV.at()
            S.op("T", lambda e, o=oc, s=sl, b=bank: e.matmul(
                b[:], P.l0wo[0:64, 128 * o:128 * (o + 1)], P.attF[0:64, s],
                start=True, stop=True), [att_ok, ffn_ok, prev], P.sT)
            mm_ok = P.sT.at()
            S.op("V", lambda e, o=oc, s=sl, b=bank: e.tensor_tensor(
                P.xst[:, o, s], P.xst[:, o, s], b[:], ALU.add), [mm_ok], P.sV)
    S.flush()
    return P.sV.at()


def build_l0_attn2(P, S, bm_ok, qkv_ok):
    pf = P.pfb
    comps = []
    for n in range(16):
        comps.append(("leaf", None, n, (4 * n, 4 * n + 4)))
    ci = 16
    for l in range(3):
        bs = BS[l]
        for n in range(NN[l]):
            comps.append(("tr", l, ci, (n * 2 * bs, n * 2 * bs + bs)))
            ci += 1
    comps.append(("tr", 3, None, (0, 32)))
    for l in range(3):
        bs = BS[l]
        for n in range(NN[l]):
            comps.append(("bl", l, ci, (n * 2 * bs + bs, n * 2 * bs + 2 * bs)))
            ci += 1
    comps.append(("bl", 3, None, (32, 64)))
    comps.append(("den", None, None, (64, 65)))
    assert len(comps) == 47 and ci == 44

    last_y = P.sV.at()
    last_grp_ev = {0: P.sV.at(), 1: P.sV.at()}
    P.mskslot = {s: (P.sT, 0) for s in range(8)}
    P.amslot = {0: (P.sT, 0), 1: (P.sT, 0)}

    for c in range(NCH):
        sl = slice(128 * c, 128 * (c + 1))
        prev_tf = P.sV.at()
        for l in range(4):
            S.op("T", lambda e, l=l, s=sl: e.matmul(
                pf[2][0:64, 128 * l:128 * (l + 1)], P.Bmat[:, 4 * l + 1, :],
                P.vb[0:64, s], start=True, stop=True),
                [qkv_ok, bm_ok, prev_tf], P.sT)
        tfv_mm = P.sT.at()
        S.op("V", lambda e: e.tensor_copy(
            P.tfF[:, 0:4, :], pf[2][0:64, 0:512].rearrange("p (l n) -> p l n", l=4)),
            [tfv_mm], P.sV)
        tfv_ok = P.sV.at()
        for l in range(4):
            S.op("T", lambda e, l=l, s=sl: e.matmul(
                pf[2][0:64, 128 * l:128 * (l + 1)], P.Bmat[:, 4 * l + 2, :],
                P.kb[0:64, s], start=True, stop=True), [tfv_ok], P.sT)
        tfk_mm = P.sT.at()
        S.op("V", lambda e: e.tensor_copy(
            P.tfF[:, 4:8, :], pf[2][0:64, 0:512].rearrange("p (l n) -> p l n", l=4)),
            [tfk_mm], P.sV)
        tfF_ok = P.sV.at()
        for l in range(4):
            S.op("T", lambda e, l=l, s=sl: e.matmul(
                pf[3][:, 64 * l:64 * (l + 1)], P.kb[0:64, s], P.Bmat[:, 4 * l + 0, :],
                start=True, stop=True), [tfF_ok], P.sT)
            S.op("T", lambda e, l=l, s=sl: e.matmul(
                pf[3][:, 256 + 64 * l:256 + 64 * (l + 1)], P.vb[0:64, s],
                P.Bmat[:, 4 * l + 3, :], start=True, stop=True), [], P.sT)
        tft_mm = P.sT.at()
        S.op("V", lambda e: e.tensor_copy(
            P.tfT[:], pf[3][:, 0:512].rearrange("p (l n) -> p l n", l=8)),
            [tft_mm], P.sV)
        tfT_ok = P.sV.at()
        S.op("V", lambda e: e.tensor_copy(P.crb[:], P.crun[:]), [P.crun_w], P.sV)
        crb_ok = P.sV.at()
        pbk = P.pbt if c % 2 == 0 else P.pbt2
        S.op("T", lambda e, b=pbk: e.transpose(b[0:64, 896:960], P.crb[0:64, 0:64],
                                               P.idn[0:64, 0:64]), [crb_ok], P.sT)
        ctr_mm = P.sT.at()
        S.op("V", lambda e, b=pbk: e.tensor_copy(P.cts[:], b[0:64, 896:960]),
             [ctr_mm], P.sV)
        cts_ok = P.sV.at()
        for l in range(4):
            S.op("V", lambda e, l=l: e.tensor_tensor(
                P.el[:, l, :], P.crb[:, 0:64], P.s_projmask[:, 1 + l, :], ALU.mult),
                [crb_ok], P.sV)
            S.op("V", lambda e, l=l: e.tensor_tensor(
                P.elt[:, l, :], P.cts[:], P.s_projmask[:, 5 + l, :], ALU.mult),
                [cts_ok], P.sV)
        S.op("V", lambda e: e.tensor_tensor(P.eleaf[:], P.cts[:], P.s_projmask[:, 0, :],
                                            ALU.mult), [cts_ok], P.sV)
        e_ok = P.sV.at()
        prev2 = P.sV.at()
        for l in range(4):
            S.op("T", lambda e, l=l: e.matmul(
                pf[2][0:64, 64 * l:64 * (l + 1)], P.el[:, l, :], P.Bmat[:, 4 * l + 0, :],
                start=True, stop=True), [e_ok, prev2], P.sT)
            S.op("T", lambda e, l=l: e.matmul(
                pf[3][0:64, 64 * l:64 * (l + 1)], P.elt[:, l, :], P.Bmat[:, 4 * l + 3, :],
                start=True, stop=True), [e_ok], P.sT)
        stg_mm = P.sT.at()
        S.op("V", lambda e: e.tensor_copy(P.s1b[:, 0:4, :], pf[2][0:64, 0:256]
                                          .rearrange("p (l n) -> p l n", l=4)),
             [stg_mm], P.sV)
        S.op("V", lambda e: e.tensor_copy(P.s1b[:, 4:8, :], pf[3][0:64, 0:256]
                                          .rearrange("p (l n) -> p l n", l=4)),
             [stg_mm], P.sV)
        s1_ok = P.sV.at()
        for l in range(4):
            S.op("T", lambda e, l=l: e.matmul(
                pf[4][0:64, 0:64], P.Bmat[:, 4 * l + 1, :], P.s1b[:, l, :],
                start=(l == 0), stop=False), [s1_ok], P.sT)
            S.op("T", lambda e, l=l: e.matmul(
                pf[4][0:64, 0:64], P.Bmat[:, 4 * l + 2, :], P.s1b[:, 4 + l, :],
                start=False, stop=(l == 3)), [], P.sT)
        gt_mm = P.sT.at()
        S.op("V", lambda e: e.tensor_tensor(P.gext[:, 0:64], pf[4][0:64, 0:64],
                                            P.eleaf[:], ALU.add), [gt_mm], P.sV)
        S.op("V", lambda e: e.tensor_copy(P.gext[:, 64:65], P.crun[:, 64:65]),
             [P.crun_w], P.sV)
        gext_ok = P.sV.at()

        S.op("T", lambda e, c=c: e.matmul(
            pf[4][0:64, 64:129], P.kTa[:, c, :], P.vexta[:, c, :],
            start=True, stop=True), [gext_ok], P.sT)
        st_mm = P.sT.at()
        S.op("V", lambda e: e.tensor_tensor(P.crun[:], P.crun[:], pf[4][0:64, 64:129],
                                            ALU.add), [st_mm, P.crun_w], P.sV)
        P.crun_w = P.sV.at()

        for g in range(12):
            bank = pf[g % 2]
            n_in_g = min(4, 47 - 4 * g)
            for j in range(n_in_g):
                i = 4 * g + j
                kind, l, mcol, cols = comps[i]
                if mcol is None:
                    continue
                if kind == "leaf":
                    srcf = lambda e, s=sl: P.vb[0:64, s]
                elif kind == "tr":
                    srcf = lambda e, l=l: P.tfF[:, l, :]
                else:
                    srcf = lambda e, l=l: P.tfF[:, 4 + l, :]
                S.op("V", lambda e, f=srcf, m=mcol, i=i: e.tensor_scalar(
                    P.msk[:, i % 8, :], f(e), P.s_nodemask[:, m:m + 1], None,
                    ALU.mult), [tfF_ok, P.mskslot[i % 8]], P.sV)
            mcp_ok = P.sV.at()
            for j in range(n_in_g):
                i = 4 * g + j
                kind, l, mcol, cols = comps[i]
                if mcol is not None:
                    lhsT_f = lambda e, i=i: P.msk[:, i % 8, :]
                elif kind == "tr":
                    lhsT_f = lambda e: P.tfF[:, 3, :]
                elif kind == "bl":
                    lhsT_f = lambda e: P.tfF[:, 7, :]
                else:
                    lhsT_f = lambda e, s=sl: P.kb[0:64, s]
                S.op("T", lambda e, f=lhsT_f, b=bank, j=j, s=sl: e.matmul(
                    b[:, 128 * j:128 * (j + 1)], f(e), P.pqb[0:64, s],
                    start=True, stop=True),
                    [mcp_ok, last_grp_ev[g % 2]] if j == 0 else [], P.sT)
            g_mm = P.sT.at()
            for j in range(n_in_g):
                P.mskslot[(4 * g + j) % 8] = g_mm
            S.op("V", lambda e, b=bank, n=n_in_g, g=g: e.tensor_tensor(
                P.Am[:, 4 * g:4 * g + n, :], b[:, 0:128 * n],
                bcast_free(P.tri[:], 128, 128, n, 128), ALU.mult),
                [g_mm], P.sV)
            last_grp_ev[g % 2] = P.sV.at()
        sc_ok = P.sV.at()

        S.op("T", lambda e, s=sl: e.matmul(pf[5][:, 0:65], P.pqb[0:64, s], P.gext[:],
                                           start=True, stop=False),
             [gext_ok, sc_ok, tfT_ok, last_y], P.sT)
        for i, (kind, l, mcol, cols) in enumerate(comps):
            if kind == "leaf":
                rhs_f = lambda e, c=c, co=cols: P.kTa[:, c, co[0]:co[1]]
            elif kind == "tr":
                rhs_f = lambda e, l=l, co=cols: P.tfT[:, l, co[0]:co[1]]
            elif kind == "bl":
                rhs_f = lambda e, l=l, co=cols: P.tfT[:, 4 + l, co[0]:co[1]]
            else:
                rhs_f = lambda e: P.onesc[:]
            S.op("T", lambda e, r=rhs_f, co=cols, i=i: e.matmul(
                pf[5][:, co[0]:co[1]], P.Am[:, i, :], r(e), start=False,
                stop=(i == 46)), [], P.sT)
        y_mm = P.sT.at()

        S.op("V", lambda e: e.tensor_scalar(P.dcol[:, 0:1], pf[5][:, 64:65], 1e-6,
                                            None, ALU.max), [y_mm], P.sV)
        d_ok = P.sV.at()
        S.op("V", lambda e: e.reciprocal(P.dcol[:, 1:2], P.dcol[:, 0:1]), [d_ok], P.sV)
        r_ok = P.sV.at()
        S.op("V", lambda e: e.tensor_scalar(P.att_tok[:], pf[5][:, 0:64],
                                            P.dcol[:, 1:2], None, ALU.mult),
             [r_ok], P.sV)
        atok_ok = P.sV.at()
        last_y = atok_ok
        S.op("T", lambda e, b=pbk: e.transpose(b[0:64, 768:896], P.att_tok[:],
                                               P.idn[:]), [atok_ok], P.sT)
        atr_mm = P.sT.at()
        S.op("V", lambda e, s=sl, b=pbk: e.tensor_copy(P.attF[0:64, s],
                                                       b[0:64, 768:896]),
             [atr_mm], P.sV)





def l1_carves(P):
    P.carve("pkF", [128, 8, NT], BF16, 0)
    P.carve("vta1", [128, NCH, NH, 64], BF16, 32768)
    P.carve("wbig", [128, 8, D], BF16, 65536)
    P.carve("S_run", [128, 8, 65], F32, 81920)
    P.carve("S_runb", [128, 8, 65], BF16, 84032)
    P.carve("stot", [128, 8, 65], F32, 85088)
    P.carve("wk2", [128, NCH, NE], F32, 85088)
    P.carve("lg", [128, NCH, NE], F32, 89184)
    P.carve("sel", [128, NCH, NE], BF16, 93280)
    P.carve("mtmp", [128, NCH, NE], BF16, 95328)
    P.carve("gth1", [128, 8, 65], F32, 85088)
    P.carve("pqt", [128, 8, 128], BF16, 89344)
    P.carve("attFt", [128, 8, 256], BF16, 91392)
    P.carve("att_tok1", [128, NH, 64], BF16, 95488)
    P.carve("pkTs", [128, 8, 128], BF16, 97536)
    P.carve("Am1", [128, 4, 128], BF16, 99584)
    P.carve("cwF", [64, NT], BF16, 100608)
    P.carve("dden", [128, 32], F32, 104704)
    P.carve("eluA", [128, 256], F32, 85088)
    P.carve("eluB", [128, 256], F32, 86112)
    P.carve("eluC", [128, 256], F32, 87136)
    P.carve("eluD", [128, 256], F32, 88160)
    P.u_w1, P.u_w2 = [], []
    for u in range(3):
        P.u_w1.append(P.carve(f"u{u}w1", [128, 4, 8, EH], BF16, 32768 * u))
        P.u_w2.append(P.carve(f"u{u}w2", [128, 4, 2, D], BF16, 32768 * u + 16384))
    P.carve("gsil", [128, 2, 256], BF16, 104832)
    P.carve("gtil", [128, 2, 256], BF16, 105856)
    P.carve("cwbS", [128, 2, 256], BF16, 98304)
    P.carve("pqtZ", [128, 16, 128], BF16, 106880)
    P.carve("sw1j", [128, 8, SH], BF16, 0)
    P.carve("sw2j", [128, 16, D], BF16, 32768)
    if L1_DEBUG_SHTAP:
        P.carve("shdbg", [128, 32, 128], F32, 65536)
    P.tmpa1 = P.eluA[:]
    P.tmpr1 = P.eluB[:]


def build_l1(P, S, H, cc1_in, cc1_out, cw_dram, xn_ok, x_ld):
    pf = P.pfb
    l1_carves(P)
    if L1_PARTS == "ln":
        S.flush()
        return xn_ok

    S.dma("G", P.wbig[:], H["l1_wkT"].rearrange("c p n -> p c n"), [xn_ok])
    wk_ld = P.sW.at()
    if L1_PARTS == "p1w":
        S.flush()
        return wk_ld
    for oc in range(8):
        for tt in range(4):
            sl = slice(512 * tt, 512 * (tt + 1))
            bank = pf[0] if (oc * 4 + tt) % 2 == 0 else pf[1]
            prev = P.sV.at()
            prev_a = P.sA.at()
            for ic in range(8):
                S.op("T", lambda e, i=ic, o=oc, s=sl, b=bank: e.matmul(
                    b[:], P.wbig[:, i, 128 * o:128 * (o + 1)], P.xnb[:, i, s],
                    start=(i == 0), stop=(i == 7)),
                    [xn_ok, wk_ld, prev, prev_a] if ic == 0 else [], P.sT)
            mm_ok = P.sT.at()
            if L1_PARTS == "p1k":
                continue
            for half in range(2):
                hs = slice(512 * tt + 256 * half, 512 * tt + 256 * (half + 1))
                if L1_PARTS in ("p1x", "p1y", "p1z"):
                    ta = P.eluA[:] if half == 0 else P.eluC[:]
                    tr = P.eluB[:] if half == 0 else P.eluD[:]
                    S.op("A", lambda e, b=bank, h=half, ta=ta: e.activation(
                        ta, b[:, 256 * h:256 * (h + 1)], AF.Exp), [mm_ok], P.sA)
                    a_ok = P.sA.at()
                    if L1_PARTS == "p1y":
                        S.op("V", lambda e, b=bank, h=half, tr=tr: e.tensor_scalar(
                            tr, b[:, 256 * h:256 * (h + 1)], 0.0, None, ALU.max),
                            [mm_ok, a_ok], P.sV)
                    elif L1_PARTS == "p1z":
                        S.op("V", lambda e, o=oc, hs=hs, ta=ta: e.scalar_tensor_tensor(
                            P.pkF[:, o, hs], ta, 1.0, ta, ALU.min, ALU.add),
                            [a_ok], P.sV)
                    continue
                elu1_evac(P, S, bank[:, 256 * half:256 * (half + 1)],
                          P.pkF[:, oc, hs],
                          P.eluA[:] if half == 0 else P.eluC[:],
                          P.eluB[:] if half == 0 else P.eluD[:], [mm_ok])
    if L1_PARTS == "p1k":
        S.flush()
        return P.sT.at()
    if L1_PARTS == "p1x":
        S.flush()
        return P.sA.at()
    if L1_PARTS in ("p1y", "p1z"):
        S.flush()
        return P.sV.at()
    pk_ok = P.sV.at()
    if L1_PARTS == "p1a":
        S.flush()
        return pk_ok
    S.dma("G", P.wbig[:], H["l1_wvT"].rearrange("c p n -> p c n"), [pk_ok])
    wv_ld = P.sW.at()
    for c in range(NCH):
        slc = slice(128 * c, 128 * (c + 1))
        for half in range(2):
            bank = pf[0] if (c * 2 + half) % 2 == 0 else pf[1]
            prev = P.sV.at()
            for ic in range(8):
                S.op("T", lambda e, i=ic, h=half, s=slc, b=bank: e.matmul(
                    b[:], P.xnb[:, i, s], P.wbig[:, i, 512 * h:512 * (h + 1)],
                    start=(i == 0), stop=(i == 7)),
                    [wv_ld, prev] if ic == 0 else [], P.sT)
            mm_ok = P.sT.at()
            S.op("V", lambda e, c=c, h=half, b=bank: e.tensor_copy(
                P.vta1[:, c, 8 * h:8 * (h + 1), :].rearrange("p h n -> p (h n)"),
                b[:]), [mm_ok], P.sV)
    v_ok = P.sV.at()
    if L1_PARTS == "p1b":
        S.flush()
        return v_ok
    S.op("V", lambda e: e.memset(P.stot.rearrange("p a b -> p (a b)"), 0.0),
         [], P.sV)
    for c in range(NCH):
        slc = slice(128 * c, 128 * (c + 1))
        pbk = P.pbt if c % 2 == 0 else P.pbt2
        prev_ev = P.sV.at()
        for hp in range(8):
            S.op("T", lambda e, hp=hp, s=slc, b=pbk: e.transpose(
                b[:, 128 * hp:128 * (hp + 1)], P.pkF[:, hp, s], P.idn[:]),
                [pk_ok, v_ok, prev_ev] if hp == 0 else [], P.sT)
        tr_ok = P.sT.at()
        S.op("V", lambda e, b=pbk: e.tensor_copy(
            P.pkTs[:], b[:, 0:1024].rearrange("p (a b) -> p a b", a=8)),
            [tr_ok], P.sV)
        ev_ok = P.sV.at()
        for h in range(NH):
            hp, half = h // 2, h % 2
            bank = pf[4] if hp < 4 else pf[5]
            col = (hp % 4) * 65
            S.op("T", lambda e, hp=hp, hf=half, c=c, b=bank, col=col: e.matmul(
                b[64 * hf:64 * (hf + 1), col:col + 64],
                P.pkTs[:, hp, 64 * hf:64 * (hf + 1)], P.vta1[:, c, 2 * hp + hf, :],
                start=True, stop=False), [ev_ok], P.sT)
            S.op("T", lambda e, hp=hp, hf=half, b=bank, col=col: e.matmul(
                b[64 * hf:64 * (hf + 1), col + 64:col + 65],
                P.pkTs[:, hp, 64 * hf:64 * (hf + 1)], P.onesc[:],
                start=False, stop=True), [], P.sT)
        st_mm = P.sT.at()
        S.op("V", lambda e: e.tensor_tensor(
            P.stot[:, 0:4, :].rearrange("p a b -> p (a b)"),
            P.stot[:, 0:4, :].rearrange("p a b -> p (a b)"), pf[4][:, 0:260],
            ALU.add), [st_mm], P.sV)
        S.op("V", lambda e: e.tensor_tensor(
            P.stot[:, 4:8, :].rearrange("p a b -> p (a b)"),
            P.stot[:, 4:8, :].rearrange("p a b -> p (a b)"), pf[5][:, 0:260],
            ALU.add), [], P.sV)
    stot_ok = P.sV.at()
    if L1_PARTS == "p1c":
        S.flush()
        return stot_ok
    S.dma("S", cc1_in.rearrange("(hp p) n -> p hp n", hp=8), P.stot[:], [stot_ok])
    stg_ok = P.sX.at()
    if L1_PARTS == "p1":
        S.flush()
        return stg_ok
    S.op("G", lambda e: e.collective_compute(
        "AllGather", ALU.bypass, replica_groups=PAIRS,
        ins=[cc1_in[:]], outs=[cc1_out[:]]), [stg_ok], P.sC)
    cc_ok = P.sC.at()
    if L1_PARTS == "cc1":
        S.flush()
        return cc_ok

    for t16 in range(NCH):
        slc = slice(128 * t16, 128 * (t16 + 1))
        bank = pf[2] if t16 % 2 == 0 else pf[3]
        prev = P.sV.at()
        for ic in range(8):
            S.op("T", lambda e, i=ic, s=slc, b=bank: e.matmul(
                b[:, 0:NE], P.xnb[:, i, s], P.l1rw[:, i, :],
                start=(i == 0), stop=False),
                [prev, stg_ok] if ic == 0 else [], P.sT)
        S.op("T", lambda e, b=bank: e.matmul(
            b[:, 0:NE], P.ones1[:], P.l1rb[:], start=False, stop=True), [], P.sT)
        mm_ok = P.sT.at()
        S.op("V", lambda e, t=t16, b=bank: e.tensor_copy(P.lg[:, t, :], b[:, 0:NE]),
             [mm_ok, stot_ok], P.sV)
    lg_ok = P.sV.at()
    mx = P.dden[:, 0:16]
    mxb = free_bcast(P.dden[:, 0:16], NE)
    r8b = free_bcast(P.dden[:, 16:32], NE)
    S.op("V", lambda e: e.tensor_reduce(mx, P.lg[:], AX.X, ALU.max), [lg_ok], P.sV)
    S.op("V", lambda e: e.drain(), [], None)
    S.op("V", lambda e: e.tensor_tensor(P.wk2[:], P.lg[:], mxb, ALU.subtract),
         [], P.sV)
    sub_ok = P.sV.at()
    S.op("A", lambda e: e.activation(P.lg.rearrange("p a b -> p (a b)"),
                                     P.wk2.rearrange("p a b -> p (a b)"), AF.Exp),
         [sub_ok], P.sA)
    exp_ok = P.sA.at()
    S.op("V", lambda e: e.tensor_copy(P.wk2.rearrange("p a b -> p (a b)"),
                                      P.lg.rearrange("p a b -> p (a b)")),
         [exp_ok], P.sV)
    S.op("V", lambda e: e.memset(P.sel.rearrange("p a b -> p (a b)"), 0.0), [], P.sV)
    for it in range(8):
        S.op("V", lambda e: e.tensor_reduce(mx, P.wk2[:], AX.X, ALU.max), [], P.sV)
        S.op("V", lambda e: e.drain(), [], None)
        S.op("V", lambda e: e.tensor_tensor(P.mtmp[:], P.wk2[:], mxb, ALU.is_ge),
             [], P.sV)
        S.op("V", lambda e: e.tensor_tensor(P.sel[:], P.sel[:], P.mtmp[:], ALU.max),
             [], P.sV)
        S.op("V", lambda e: e.scalar_tensor_tensor(
            P.wk2.rearrange("p a b -> p (a b)"), P.mtmp.rearrange("p a b -> p (a b)"),
            -1e30, P.wk2.rearrange("p a b -> p (a b)"), ALU.mult, ALU.add), [], P.sV)
    S.op("V", lambda e: e.tensor_tensor(P.wk2[:], P.lg[:], P.sel[:], ALU.mult),
         [], P.sV)
    S.op("V", lambda e: e.tensor_reduce(mx, P.wk2[:], AX.X, ALU.add), [], P.sV)
    S.op("V", lambda e: e.drain(), [], None)
    S.op("V", lambda e: e.reciprocal(P.dden[:, 16:32], mx), [], P.sV)
    S.op("V", lambda e: e.drain(), [], None)
    S.op("V", lambda e: e.tensor_tensor(P.mtmp[:], P.wk2[:], r8b, ALU.mult), [], P.sV)
    cw_ok = P.sV.at()
    for t16 in range(NCH):
        pbk = P.pbt if t16 % 2 == 0 else P.pbt2
        prev_ev = P.sV.at()
        S.op("T", lambda e, t=t16, b=pbk: e.transpose(
            b[0:64, 896:1024], P.mtmp[:, t, :], P.idn[:]), [cw_ok, prev_ev], P.sT)
        tr_ok = P.sT.at()
        S.op("V", lambda e, t=t16, b=pbk: e.tensor_copy(
            P.cwF[0:64, 128 * t:128 * (t + 1)], b[0:64, 896:1024]), [tr_ok], P.sV)
    cwF_ok = P.sV.at()
    if L1_PARTS == "rt":
        S.flush()
        return cwF_ok

    S.dma("S", P.gth1[:], cc1_out[0:NH * DH].rearrange("(hp p) n -> p hp n", hp=8),
          [cc_ok, cwF_ok])
    gth_ok = P.sX.at()
    S.op("V", lambda e: e.tensor_scalar(
        P.S_run[:], P.gth1[:],
        P.pmask[:, 0:1], None, ALU.mult), [gth_ok], P.sV)
    P.srun_w = P.sV.at()
    S.op("V", lambda e: e.tensor_copy(P.S_runb.rearrange("p a b -> p (a b)"),
                                      P.S_run.rearrange("p a b -> p (a b)")),
         [P.srun_w], P.sV)
    srunb_ok = P.sV.at()
    if L1_PARTS == "cc2":
        S.flush()
        return srunb_ok

    A2LVL = {"a2q": 1, "a2sm": 2, "a2sc": 2, "a2s": 2, "a2y": 3, "a2d": 4,
             "a2u": 5, "a2t": 6, "a2sm2": 2, "a2sm3": 2, "a2sm4": 2}.get(
                 L1_PARTS, 7)
    l1grp = {0: P.sV.at(), 1: P.sV.at()}
    l1am = P.sT.at()
    last_wo_mm = P.sT.at()
    for sub in range(8):
        ssl = slice(256 * sub, 256 * (sub + 1))
        S.dma("G", P.wbig[:], H["l1_wqT"].rearrange("c p n -> p c n"),
              [last_wo_mm, v_ok])
        wq_ld = P.sW.at()
        attf_ok = []
        for c2 in range(2):
            c = 2 * sub + c2
            slc = slice(128 * c, 128 * (c + 1))
            for oc in range(8):
                bank = pf[2] if oc % 2 == 0 else pf[3]
                prev = P.sV.at()
                prev_a = P.sA.at()
                for ic in range(8):
                    S.op("T", lambda e, i=ic, o=oc, s=slc, b=bank: e.matmul(
                        b[:, 0:128], P.wbig[:, i, 128 * o:128 * (o + 1)],
                        P.xnb[:, i, s], start=(i == 0), stop=(i == 7)),
                        [wq_ld, prev, prev_a, srunb_ok] if ic == 0 else [], P.sT)
                mm_ok = P.sT.at()
                elu1_evac(P, S, bank[:, 0:128], P.pqt[:, oc, :],
                          P.eluA[:, 0:128], P.eluB[:, 0:128], [mm_ok])
                for k in range(2):
                    S.op("V", lambda e, hp=oc, k=k: e.tensor_scalar(
                        P.pqtZ[:, 2 * hp + k, :], P.pqt[:, hp, :],
                        P.hmask[:, k:k + 1], None, ALU.mult),
                        [(P.sT, l1am[1])] if oc == 0 and k == 0 else [], P.sV)
            pq_ok = P.sV.at()
            if A2LVL < 2:
                continue
            for sg in range(4):
                bank = pf[0] if sg % 2 == 0 else pf[1]
                for j in range(4):
                    h = 4 * sg + j
                    hp = h // 2
                    S.op("T", lambda e, hp=hp, h=h, j=j, s=slc, b=bank:
                         e.matmul(b[:, 128 * j:128 * (j + 1)],
                                  P.pkF[:, hp, s],
                                  P.pqtZ[:, h, :],
                                  start=True, stop=True),
                         [pq_ok, l1grp[sg % 2]] if j == 0 else [], P.sT)
                g_mm = P.sT.at()
                if L1_PARTS in ("a2sm", "a2sm2", "a2sm3", "a2sm4"):
                    l1grp[sg % 2] = (P.sV, l1grp[sg % 2][1])
                    continue
                if L1_PARTS == "a2sc":
                    S.op("V", lambda e, b=bank: e.tensor_copy(
                        P.Am1[:], b[:]), [g_mm, l1am], P.sV)
                else:
                    S.op("V", lambda e, b=bank: e.tensor_tensor(
                        P.Am1[:], b[:], bcast_free(P.tri[:], 128, 128, 4, 128),
                        ALU.mult), [g_mm, l1am], P.sV)
                am_ok = P.sV.at()
                l1grp[sg % 2] = am_ok
                if A2LVL < 3:
                    continue
                for j in range(4):
                    h = 4 * sg + j
                    hp = h // 2
                    yb = pf[2 + h // 4]
                    col = (h % 4) * 65
                    S.op("T", lambda e, hp=hp, h=h, b=yb, col=col:
                         e.matmul(b[:, col:col + 65],
                                  P.pqtZ[:, h, :],
                                  P.S_runb[:, hp, :],
                                  start=True, stop=False),
                         [am_ok] if j == 0 else [], P.sT)
                    S.op("T", lambda e, j=j, h=h, c=c, b=yb, col=col: e.matmul(
                        b[:, col:col + 64], P.Am1[:, j, :], P.vta1[:, c, h, :],
                        start=False, stop=False), [], P.sT)
                    S.op("T", lambda e, j=j, b=yb, col=col: e.matmul(
                        b[:, col + 64:col + 65], P.Am1[:, j, :], P.onesc[:],
                        start=False, stop=True), [], P.sT)
                l1am = P.sT.at()
            y_mm = P.sT.at()
            if A2LVL < 4:
                continue
            for ybk in range(4):
                yb = pf[2 + ybk]
                ybv = yb[:, 0:260].rearrange("p (h n) -> p h n", h=4)
                S.op("V", lambda e, v=ybv, k=ybk: e.tensor_scalar(
                    P.dden[:, 4 * k:4 * k + 4], v[:, :, 64], 1e-5, None, ALU.add),
                    [y_mm], P.sV)
            dn_ok = P.sV.at()
            S.op("V", lambda e: e.reciprocal(P.dden[:, 16:32], P.dden[:, 0:16]),
                 [dn_ok], P.sV)
            S.op("V", lambda e: e.drain(), [], None)
            for ybk in range(4):
                yb = pf[2 + ybk]
                ybv = yb[:, 0:260].rearrange("p (h n) -> p h n", h=4)
                rb = free_bcast(P.dden[:, 16 + 4 * ybk:16 + 4 * ybk + 4], 64)
                S.op("V", lambda e, v=ybv, k=ybk, rb=rb: e.tensor_tensor(
                    P.att_tok1[:, 4 * k:4 * k + 4, :], v[:, :, 0:64], rb, ALU.mult),
                    [], P.sV)
            att_ok = P.sV.at()
            if A2LVL < 5:
                continue
            pbk = P.pbt if c % 2 == 0 else P.pbt2
            for hp in range(8):
                S.op("T", lambda e, hp=hp, s=slc, b=pbk: e.transpose(
                    b[:, 128 * hp:128 * (hp + 1)], P.pkF[:, hp, s], P.idn[:]),
                    [att_ok] if hp == 0 else [], P.sT)
            tr_ok = P.sT.at()
            S.op("V", lambda e, b=pbk: e.tensor_copy(
                P.pkTs[:], b[:, 0:1024].rearrange("p (a b) -> p a b", a=8)),
                [tr_ok], P.sV)
            ev_ok = P.sV.at()
            for h in range(NH):
                hp, hf = h // 2, h % 2
                bank = pf[0] if hp < 4 else pf[1]
                col = (hp % 4) * 65
                S.op("T", lambda e, hp=hp, hf=hf, c=c, b=bank, col=col: e.matmul(
                    b[64 * hf:64 * (hf + 1), col:col + 64],
                    P.pkTs[:, hp, 64 * hf:64 * (hf + 1)],
                    P.vta1[:, c, 2 * hp + hf, :],
                    start=True, stop=False), [ev_ok], P.sT)
                S.op("T", lambda e, hp=hp, hf=hf, b=bank, col=col,
                     last=True: e.matmul(
                    b[64 * hf:64 * (hf + 1), col + 64:col + 65],
                    P.pkTs[:, hp, 64 * hf:64 * (hf + 1)], P.onesc[:],
                    start=False, stop=last), [], P.sT)
            m_mm = P.sT.at()
            S.op("V", lambda e: e.tensor_tensor(
                P.S_run[:, 0:4, :].rearrange("p a b -> p (a b)"),
                P.S_run[:, 0:4, :].rearrange("p a b -> p (a b)"),
                pf[0][:, 0:260], ALU.add), [m_mm, P.srun_w], P.sV)
            S.op("V", lambda e: e.tensor_tensor(
                P.S_run[:, 4:8, :].rearrange("p a b -> p (a b)"),
                P.S_run[:, 4:8, :].rearrange("p a b -> p (a b)"),
                pf[1][:, 0:260], ALU.add), [], P.sV)
            P.srun_w = P.sV.at()
            S.op("V", lambda e: e.tensor_copy(
                P.S_runb.rearrange("p a b -> p (a b)"),
                P.S_run.rearrange("p a b -> p (a b)")), [P.srun_w], P.sV)
            srunb_ok = P.sV.at()
            if A2LVL < 6:
                continue
            pbk2 = P.pbt2 if c % 2 == 0 else P.pbt
            prev_ev2 = P.sV.at()
            for hp in range(8):
                S.op("T", lambda e, hp=hp, b=pbk2: e.transpose(
                    b[:, 128 * hp:128 * (hp + 1)],
                    P.att_tok1[:, 2 * hp:2 * hp + 2, :]
                    .rearrange("p a b -> p (a b)"), P.idn[:]),
                    [att_ok, prev_ev2] if hp == 0 else [], P.sT)
            atr_ok = P.sT.at()
            S.op("V", lambda e, b=pbk2, c2=c2: e.tensor_copy(
                P.attFt[:, :, 128 * c2:128 * (c2 + 1)],
                b[:, 0:1024].rearrange("p (a b) -> p a b", a=8)),
                [atr_ok], P.sV)
            attf_ok.append(P.sV.at())
        if A2LVL < 7:
            continue
        S.dma("G", P.wbig[:], H["l1_woT"].rearrange("c p n -> p c n"),
              [(P.sT, y_mm[1])])
        wo_ld = P.sW.at()
        for oc in range(8):
            bank = pf[4] if oc % 2 == 0 else pf[5]
            prev = P.sV.at()
            for ec in range(8):
                S.op("T", lambda e, ec=ec, o=oc, b=bank: e.matmul(
                    b[:, 0:256], P.wbig[:, ec, 128 * o:128 * (o + 1)],
                    P.attFt[:, ec, :], start=(ec == 0), stop=(ec == 7)),
                    [wo_ld, attf_ok[-1], prev] if ec == 0 else [], P.sT)
            mm_ok = P.sT.at()
            S.op("V", lambda e, o=oc, s=ssl, b=bank: e.scalar_tensor_tensor(
                P.xst[:, o, s], b[:, 0:256], P.l1bo[:, o:o + 1], P.xst[:, o, s],
                ALU.add, ALU.add), [mm_ok], P.sV)
        last_wo_mm = P.sT.at()
    att1_ok = P.sV.at()
    if L1_PARTS == "att":
        S.flush()
        return att1_ok

    S.dma("G", P.b2s[:], H["l1_b2"][:], [att1_ok])
    b2_ld = P.sW.at()
    slot_free = {0: P.sT.at(), 1: P.sT.at(), 2: P.sT.at()}
    unit_ld = {}

    def load_unit(u):
        slot = u % 3
        S.dma("G", P.u_w1[slot].rearrange("p e c n -> p (e c) n"),
              H["l1_w1"][u].rearrange("m p n -> p m n"),
              [slot_free[slot], att1_ok])
        S.dma("G", P.u_w2[slot].rearrange("p e c n -> p (e c) n"),
              H["l1_w2"][u].rearrange("m p n -> p m n"),
              [slot_free[slot]])
        unit_ld[u] = P.sW.at()

    if L1_PARTS != "nomoe":
        load_unit(0)
        load_unit(1)
        load_unit(2)
    cwb_free = {0: P.sV.at(), 1: P.sV.at()}
    P.partial_free = P.sV.at()
    for grp in ([] if L1_PARTS == "nomoe" else range(8)):
        if 2 * grp + 3 < 16 and grp > 0:
            load_unit(2 * grp + 2)
        for tt in range(8):
            tsl = slice(256 * tt, 256 * (tt + 1))
            if tt == 0 and grp == 0:
                P.mm1_bank_free = {0: P.sA.at(), 1: P.sA.at()}
                P.gsil_free = P.sV.at()
                P.gtil_free = P.sT.at()
            for ei in range(8):
                e_glob = 8 * grp + ei
                u = 2 * grp + ei // 4
                eu = ei % 4
                bank = pf[0] if ei % 2 == 0 else pf[1]
                prev_g = P.mm1_bank_free[ei % 2]
                for hc in range(2):
                    for ic in range(8):
                        S.op("T", lambda e, u=u % 3, eu=eu, i=ic, hc=hc, t=tsl,
                             b=bank: e.matmul(
                                 b[:, 256 * hc:256 * (hc + 1)],
                                 P.u_w1[u][:, eu, i, 128 * hc:128 * (hc + 1)],
                                 P.xnb[:, i, t], start=(i == 0), stop=(i == 7)),
                             [unit_ld[u], prev_g] if (hc == 0 and ic == 0) else [],
                             P.sT)
                mm1_ok = P.sT.at()
                for hc in range(2):
                    S.op("A", lambda e, hc=hc, eg=e_glob, b=bank: e.activation(
                        P.gsil[:, hc, :], b[:, 256 * hc:256 * (hc + 1)], AF.Silu,
                        bias=P.b1s[:, eg, hc:hc + 1]), [mm1_ok, P.gsil_free], P.sA)
                sil_ok = P.sA.at()
                P.mm1_bank_free[ei % 2] = sil_ok
                cwb = (P.pbt if ei % 2 == 0 else P.pbt2)[:, 0:512].bitcast(F32)
                if L1_PARTS != "nocwns":
                    S.op("T", lambda e, eg=e_glob, t=tsl, o=cwb: e.matmul(
                        o[:, 0:256],
                        free_bcast(P.idn[0:64, eg % 64:eg % 64 + 1], 128),
                        P.cwF[0:64, t], start=True, stop=True),
                        [cwF_ok, cwb_free[ei % 2]], P.sT)
                    cwb_mm = P.sT.at()
                for hc in range(2):
                    if L1_PARTS == "nocwns":
                        S.op("V", lambda e, hc=hc: e.tensor_copy(
                            P.gtil[:, hc, :], P.gsil[:, hc, :]),
                            [sil_ok, P.gtil_free], P.sV)
                    else:
                        S.op("V", lambda e, hc=hc, o=cwb: e.tensor_tensor(
                            P.gtil[:, hc, :], P.gsil[:, hc, :],
                            o[:, 0:256],
                            ALU.mult), [sil_ok, cwb_mm, P.gtil_free], P.sV)
                gt_ok = P.sV.at()
                P.gsil_free = gt_ok
                cwb_free[ei % 2] = gt_ok
                for oc in range(8):
                    pb_i, colh = divmod(oc, 2)
                    bank2 = pf[2 + pb_i]
                    for hc in range(2):
                        st = (ei == 0 and oc % 2 == 0 and hc == 0)
                        sp = (grp > 0 and ei == 7 and oc % 2 == 1 and hc == 1)
                        S.op("T", lambda e, u=u % 3, eu=eu, hc=hc, o=oc, b=bank2,
                             ch=colh, st=st, sp=sp: e.matmul(
                                 b[:, 256 * ch:256 * (ch + 1)],
                                 P.u_w2[u][:, eu, hc, 128 * o:128 * (o + 1)],
                                 P.gtil[:, hc, :], start=st, stop=sp),
                             [gt_ok, P.partial_free] if (oc == 0 and hc == 0)
                             else [], P.sT)
                P.gtil_free = P.sT.at()
            for oc in range(8):
                pb_i, colh = divmod(oc, 2)
                bank2 = pf[2 + pb_i]
                if grp == 0:
                    S.op("T", lambda e, o=oc, b=bank2, ch=colh, t=tsl,
                         last=(oc % 2 == 1): e.matmul(
                        b[:, 256 * ch:256 * (ch + 1)],
                        P.b2s[0:64, 128 * o:128 * (o + 1)], P.cwF[0:64, t],
                        start=False, stop=last), [cwF_ok, b2_ld], P.sT)
                else:
                    pass
            close_mm = P.sT.at()
            for oc in range(8):
                pb_i, colh = divmod(oc, 2)
                bank2 = pf[2 + pb_i]
                S.op("V", lambda e, o=oc, b=bank2, ch=colh, t=tsl: e.tensor_tensor(
                    P.xst[:, o, t], P.xst[:, o, t], b[:, 256 * ch:256 * (ch + 1)],
                    ALU.add), [close_mm], P.sV)
            P.partial_free = P.sV.at()
        slot_free[(2 * grp) % 3] = P.sT.at()
        slot_free[(2 * grp + 1) % 3] = P.sT.at()
        if 2 * grp + 3 < 16:
            load_unit(2 * grp + 3)
        elif grp == 0:
            load_unit(3)
    moe_ok = P.sV.at()
    if L1_PARTS in ("nosh", "nocwns"):
        S.flush()
        return moe_ok

    if L1_PARTS == "nomoe":
        moe_ok = att1_ok
    for j in range(2):
        S.dma("G", P.sw1j,
              H["l1_sw1"][j].rearrange("c p n -> p c n"),
              [moe_ok if j == 0 else P.sV.at()])
        S.dma("G", P.sw2j,
              H["l1_sw2"][j].rearrange("c p n -> p c n"), [])
        sw_ld = P.sW.at()
        for tt in range(8):
            tsl = slice(256 * tt, 256 * (tt + 1))
            if tt == 0:
                sh_bank_free = {0: P.sA.at(), 1: P.sA.at()}
                sh_gsil_free = {0: P.sT.at(), 1: P.sT.at()}
            for hc in range(16):
                bank = pf[0] if hc % 2 == 0 else pf[1]
                for ic in range(8):
                    S.op("T", lambda e, i=ic, hc=hc, t=tsl, b=bank: e.matmul(
                        b[:, 0:256], P.sw1j[:, i, 128 * hc:128 * (hc + 1)],
                        P.xnb[:, i, t], start=(i == 0), stop=(i == 7)),
                        [sw_ld, sh_bank_free[hc % 2]] if ic == 0 else [], P.sT)
                mm1_ok = P.sT.at()
                S.op("A", lambda e, hc=hc, j=j, b=bank: e.activation(
                    P.gsil[:, hc % 2, :], b[:, 0:256], AF.Silu,
                    bias=P.sb1s[:, j, hc:hc + 1]), [mm1_ok, sh_gsil_free[hc % 2]],
                     P.sA)
                sil_ok = P.sA.at()
                sh_bank_free[hc % 2] = sil_ok
                for oc in range(8):
                    pb_i, colh = divmod(oc, 2)
                    bank2 = pf[2 + pb_i]
                    S.op("T", lambda e, hc=hc, o=oc, b=bank2, ch=colh,
                         st=(hc == 0 and oc % 2 == 0),
                         sp=(hc == 15 and oc % 2 == 1): e.matmul(
                             b[:, 256 * ch:256 * (ch + 1)],
                             P.sw2j[:, hc, 128 * o:128 * (o + 1)],
                             P.gsil[:, hc % 2, :], start=st, stop=sp),
                         [(P.sA, sil_ok[1]), P.partial_free]
                         if (oc == 0 and hc == 0) else
                         ([(P.sA, sil_ok[1])] if oc == 0 else []), P.sT)
                sh_gsil_free[hc % 2] = P.sT.at()
            close_mm = P.sT.at()
            if L1_DEBUG_SHTAP and ((j == 0 and tt == 0) or (j == 1 and tt == 7)):
                dbase = 0 if j == 0 else 16
                for oc in range(8):
                    pb_i, colh = divmod(oc, 2)
                    bank2 = pf[2 + pb_i]
                    for half in range(2):
                        S.op("V", lambda e, o=oc, b=bank2, ch=colh, h2=half,
                             ba=dbase: e.tensor_copy(
                                 P.shdbg[:, ba + 2 * o + h2, :],
                                 b[:, 256 * ch + 128 * h2:256 * ch + 128 * (h2 + 1)]),
                             [close_mm], P.sV)
            for oc in range(8):
                pb_i, colh = divmod(oc, 2)
                bank2 = pf[2 + pb_i]
                if j == 1:
                    S.op("V", lambda e, o=oc, b=bank2, ch=colh, t=tsl:
                         e.scalar_tensor_tensor(
                             P.xst[:, o, t], b[:, 256 * ch:256 * (ch + 1)],
                             P.sb2sum[:, o:o + 1], P.xst[:, o, t], ALU.add,
                             ALU.add), [close_mm], P.sV)
                else:
                    S.op("V", lambda e, o=oc, b=bank2, ch=colh, t=tsl:
                         e.tensor_tensor(
                             P.xst[:, o, t], P.xst[:, o, t],
                             b[:, 256 * ch:256 * (ch + 1)], ALU.add),
                         [close_mm], P.sV)
            P.partial_free = P.sV.at()
    sh_ok = P.sV.at()
    S.flush()
    return sh_ok




def build_l1_moe_only(P, S, H, cw_dram, xn_ok, x_ld):
    pf = P.pfb
    for t16 in range(NCH):
        slc = slice(128 * t16, 128 * (t16 + 1))
        bank = pf[2] if t16 % 2 == 0 else pf[3]
        prev = P.sV.at()
        for ic in range(8):
            S.op("T", lambda e, i=ic, s=slc, b=bank: e.matmul(
                b[:, 0:NE], P.xnb[:, i, s], P.l1rw[:, i, :],
                start=(i == 0), stop=False),
                [prev, xn_ok, x_ld] if ic == 0 else [], P.sT)
        S.op("T", lambda e, b=bank: e.matmul(
            b[:, 0:NE], P.ones1[:], P.l1rb[:], start=False, stop=True), [], P.sT)
        mm_ok = P.sT.at()
        S.op("V", lambda e, t=t16, b=bank: e.tensor_copy(P.lg[:, t, :], b[:, 0:NE]),
             [mm_ok], P.sV)
    lg_ok = P.sV.at()
    mx = P.dden[:, 0:16]
    mxb = free_bcast(P.dden[:, 0:16], NE)
    r8b = free_bcast(P.dden[:, 16:32], NE)
    S.op("V", lambda e: e.tensor_reduce(mx, P.lg[:], AX.X, ALU.max), [lg_ok], P.sV)
    S.op("V", lambda e: e.drain(), [], None)
    S.op("V", lambda e: e.tensor_tensor(P.wk2[:], P.lg[:], mxb, ALU.subtract),
         [], P.sV)
    sub_ok = P.sV.at()
    S.op("A", lambda e: e.activation(P.lg.rearrange("p a b -> p (a b)"),
                                     P.wk2.rearrange("p a b -> p (a b)"), AF.Exp),
         [sub_ok], P.sA)
    exp_ok = P.sA.at()
    S.op("V", lambda e: e.tensor_copy(P.wk2.rearrange("p a b -> p (a b)"),
                                      P.lg.rearrange("p a b -> p (a b)")),
         [exp_ok], P.sV)
    S.op("V", lambda e: e.memset(P.sel.rearrange("p a b -> p (a b)"), 0.0), [], P.sV)
    for it in range(8):
        S.op("V", lambda e: e.tensor_reduce(mx, P.wk2[:], AX.X, ALU.max), [], P.sV)
        S.op("V", lambda e: e.drain(), [], None)
        S.op("V", lambda e: e.tensor_tensor(P.mtmp[:], P.wk2[:], mxb, ALU.is_ge),
             [], P.sV)
        S.op("V", lambda e: e.tensor_tensor(P.sel[:], P.sel[:], P.mtmp[:], ALU.max),
             [], P.sV)
        S.op("V", lambda e: e.scalar_tensor_tensor(
            P.wk2.rearrange("p a b -> p (a b)"), P.mtmp.rearrange("p a b -> p (a b)"),
            -1e30, P.wk2.rearrange("p a b -> p (a b)"), ALU.mult, ALU.add), [], P.sV)
    S.op("V", lambda e: e.tensor_tensor(P.wk2[:], P.lg[:], P.sel[:], ALU.mult),
         [], P.sV)
    S.op("V", lambda e: e.tensor_reduce(mx, P.wk2[:], AX.X, ALU.add), [], P.sV)
    S.op("V", lambda e: e.drain(), [], None)
    S.op("V", lambda e: e.reciprocal(P.dden[:, 16:32], mx), [], P.sV)
    S.op("V", lambda e: e.drain(), [], None)
    S.op("V", lambda e: e.tensor_tensor(P.mtmp[:], P.wk2[:], r8b, ALU.mult), [], P.sV)
    cw_ok = P.sV.at()
    for t16 in range(NCH):
        pbk = P.pbt if t16 % 2 == 0 else P.pbt2
        prev_ev = P.sV.at()
        S.op("T", lambda e, t=t16, b=pbk: e.transpose(
            b[0:64, 896:1024], P.mtmp[:, t, :], P.idn[:]), [cw_ok, prev_ev], P.sT)
        tr_ok = P.sT.at()
        S.op("V", lambda e, t=t16, b=pbk: e.tensor_copy(
            P.cwF[0:64, 128 * t:128 * (t + 1)], b[0:64, 896:1024]), [tr_ok], P.sV)
    cwF_ok = P.sV.at()
    S.dma("S", cw_dram[:], P.cwF[:], [cwF_ok])
    cwd_ok = P.sX.at()
    att1_ok = P.sV.at()
    P.moe_inline = (cwF_ok, cwd_ok, att1_ok)
    return None



def build_program(part="l0", taps=()):
    nc = bass.Bass("TRN2", target_bir_lowering=False)
    want_l0 = part in ("l0", "fused")
    want_l1 = part in ("l1", "fused")

    H = {}

    def hin(name, shape, dtype=F32):
        H[name] = nc.dram_tensor(name, list(shape), dtype, kind="ExternalInput").ap()
        return H[name]

    hin("x_in", (8, 128, NT))
    x_out = nc.dram_tensor("x_out", [8, 128, NT], F32, kind="ExternalOutput").ap()
    hin("c_idn", (128, 128))
    hin("c_tri", (128, 128))
    hin("c_pmask", (128, 1))
    hin("c_hmask", (128, 2))
    if want_l0:
        hin("c_nodemask", (64, 46))
        hin("c_projmask", (64, 9, 64))
        hin("l0_wq", (8, 128, DH))
        hin("l0_wk", (8, 128, DH))
        hin("l0_wv", (8, 128, DH))
        hin("l0_wo", (DH, D))
        for nmx in ("UL", "VR", "UR", "VL"):
            for l in range(4):
                hin(f"l0_{nmx}T{l}", (RL[l], BS[l]))
        hin("l0_WT", (8, 128, D))
        hin("l0_b", (128, 8))
        hin("l0_ldT", (100, 4))
        hin("l0_luT", (4, 50))
        hin("l0_da", (100, D))
        hin("l0_uaT", (50, D))
        cc0_in = nc.dram_tensor("cc0_in", [DH, DH + 1], F32, kind="Internal").ap()
        cc0_out = nc.dram_tensor("cc0_out", [2 * DH, DH + 1], F32, kind="Internal").ap()
    if want_l1:
        for w in ("wq", "wk", "wv", "wo"):
            hin(f"l1_{w}T", (8, 128, D))
        hin("l1_bo", (128, 8))
        hin("l1_rw", (8, 128, NE))
        hin("l1_rb", (1, NE))
        hin("l1_w1", (16, 32, 128, EH))
        hin("l1_w2", (16, 8, 128, D))
        hin("l1_b1", (128, NE, 2))
        hin("l1_b2", (NE, D))
        hin("l1_sw1", (2, 8, 128, SH))
        hin("l1_sw2", (2, 16, 128, D))
        hin("l1_sb1", (128, 2, 16))
        hin("l1_sb2", (128, 2, 8))
        cc1_in = nc.dram_tensor("cc1_in", [NH * DH, DH + 1], F32, kind="Internal").ap()
        cc1_out = nc.dram_tensor("cc1_out", [2 * NH * DH, DH + 1], F32,
                                 kind="Internal").ap()
        cw_dram = nc.dram_tensor("cw_dram", [NE, NT], BF16, kind="Internal").ap()

    tap_out = {t: nc.dram_tensor("tap_" + t, TAPS[t][0], TAPS[t][1],
                                 kind="ExternalOutput").ap() for t in taps}

    with ExitStack() as ctx:
        P = Prog(nc, ctx)
        S = Sched(P)

        P.sb("xst", [128, 8, NT], F32)
        P.sb("xnb", [128, 8, NT], BF16)
        P.sb("AR", [128, 55488], BF16)
        P.sb("idn", [128, 128], BF16)
        P.sb("tri", [128, 128], BF16)
        P.sb("pmask", [128, 1], F32)
        P.sb("hmask", [128, 2], F32)
        P.sb("ones1f", [1, 128], F32)
        P.sb("ones1", [1, 128], BF16)
        P.sb("onesc", [128, 1], BF16)
        P.sb("onesk", [128, 1], BF16)
        P.carve("rows4", [1, 3 * NT], F32, 43232)
        P.accm = P.rows4[0:1, 0 * NT:1 * NT]
        P.accv = P.rows4[0:1, 1 * NT:2 * NT]
        P.rrow = P.rows4[0:1, 2 * NT:3 * NT]
        P.carve("xbch", [128, NT], BF16, 67808)
        P.carve("xsq", [128, NT], BF16, 71904)
        P.carve("xsc32", [128, 512], F32, 76000)
        P.carve("etmp", [64, 1024], F32, 78048)
        P.carve("etmp2", [64, 1024], F32, 82144)
        P.pfb = [P.ps(f"pf{i}", [128, 512], F32) for i in range(6)]
        P.ps("pbt", [128, 1024], BF16)
        P.ps("pbt2", [128, 1024], BF16)

        if want_l0:
            P.carve("s_projmask", [64, 9, 64], BF16, 104832)
            P.carve("s_nodemask", [64, 46], F32, 105984)
            P.sb("l0b", [128, 8], F32)
            for nmx in ("UL", "VR", "UR", "VL"):
                for l in range(4):
                    P.sb(f"l0{nmx}T{l}", [RL[l], BS[l]], F32)
            P.carve("pqb", [64, NT], BF16, 0)
            P.carve("kb", [64, NT], BF16, 4096)
            P.carve("vb", [64, NT], BF16, 8192)
            P.carve("attF", [64, NT], BF16, 12288)
            P.carve("kTa", [128, NCH, 64], BF16, 16384)
            P.carve("vexta", [128, NCH, 65], BF16, 18432)
            P.carve("tfF", [64, 8, 128], BF16, 20544)
            P.carve("tfT", [128, 8, 64], BF16, 22592)
            P.carve("msk", [64, 8, 128], BF16, 23616)
            P.carve("Am", [128, 48, 128], BF16, 86272)
            P.carve("att_tok", [128, 64], BF16, 27712)
            P.carve("crun", [64, 65], F32, 27840)
            P.carve("crb", [64, 65], BF16, 28104)
            P.carve("cts", [64, 64], BF16, 28240)
            P.carve("el", [64, 4, 64], BF16, 28368)
            P.carve("elt", [64, 4, 64], BF16, 28880)
            P.carve("eleaf", [64, 64], BF16, 29392)
            P.carve("s1b", [64, 8, 64], BF16, 98560)
            P.carve("gext", [64, 65], BF16, 29776)
            P.carve("ctot", [64, 65], F32, 29908)
            P.carve("gth", [64, 2, 65], F32, 30172)
            P.carve("bstg", [32, 32], BF16, 30696)
            P.carve("Bmat", [64, 16, 64], BF16, 30760)
            P.carve("wct", [128, 8, D], BF16, 43232)
            P.carve("wtb", [128, 1, D], F32, 59616)
            P.carve("amat", [4, D], BF16, 63712)
            P.carve("bmat2", [50, D], BF16, 65760)
            P.carve("l0wq", [128, 8, DH], BF16, 32800)
            P.carve("l0wk", [128, 8, DH], BF16, 33824)
            P.carve("l0wv", [128, 8, DH], BF16, 34848)
            P.carve("l0wo", [DH, D], BF16, 35872)
            P.carve("l0da", [100, D], BF16, 37920)
            P.carve("l0uaT", [50, D], BF16, 39968)
            P.carve("l0ldT", [100, 4], BF16, 42016)
            P.carve("l0luT", [4, 50], BF16, 42080)
            P.tmpa = P.etmp[0:64, 0:512]
            P.tmpr = P.etmp[0:64, 512:1024]
            P.dcol = P.xsc32[:, 504:512]
        if want_l1:
            P.carve("l1rw", [128, 8, NE], BF16, 108928)
            P.carve("l1rb", [1, NE], BF16, 109952)
            P.sb("b1s", [128, NE, 2], F32)
            P.carve("b2s", [64, D], BF16, 106880)
            P.sb("sb1s", [128, 2, 16], F32)
            P.sb("sb2s", [128, 2, 8], F32)
            P.sb("sb2sum", [128, 8], F32)
            P.sb("l1bo", [128, 8], F32)

        for c8 in range(8):
            S.dma("S", P.xst[:, c8, :], H["x_in"][c8])
        S.dma("G", P.idn[:], H["c_idn"][:])
        S.dma("G", P.tri[:], H["c_tri"][:])
        S.dma("S", P.pmask[:], H["c_pmask"][:])
        S.dma("S", P.hmask[:], H["c_hmask"][:])
        if want_l0:
            S.dma("S", P.s_nodemask[:], H["c_nodemask"][:])
            S.dma("G", P.s_projmask[:], H["c_projmask"][:])
            S.dma("G", P.l0wq[:], H["l0_wq"].rearrange("c p n -> p c n"))
            S.dma("G", P.l0wk[:], H["l0_wk"].rearrange("c p n -> p c n"))
            S.dma("G", P.l0wv[:], H["l0_wv"].rearrange("c p n -> p c n"))
            S.dma("G", P.l0wo[:], H["l0_wo"][:])
            for nmx in ("UL", "VR", "UR", "VL"):
                for l in range(4):
                    S.dma("S", getattr(P, f"l0{nmx}T{l}")[:], H[f"l0_{nmx}T{l}"][:])
            S.dma("S", P.l0b[:], H["l0_b"][:])
            S.dma("G", P.l0ldT[:], H["l0_ldT"][:])
            S.dma("G", P.l0luT[:], H["l0_luT"][:])
            S.dma("G", P.l0da[:], H["l0_da"][:])
            S.dma("G", P.l0uaT[:], H["l0_uaT"][:])
        if want_l1:
            S.dma("G", P.l1rw[:], H["l1_rw"].rearrange("c p n -> p c n"))
            S.dma("G", P.l1rb[:], H["l1_rb"][:])
            S.dma("S", P.b1s[:], H["l1_b1"][:])
            S.dma("S", P.sb1s[:], H["l1_sb1"][:])
            S.dma("S", P.sb2s[:], H["l1_sb2"][:])
            S.dma("S", P.l1bo[:], H["l1_bo"][:])
        x_ld = P.sX.at()
        w_ld = P.sW.at()

        def prep_ones(e):
            e.memset(P.ones1[:], 1.0)
            e.memset(P.ones1f[:], 1.0)
            e.memset(P.onesc[:], 1.0)
            return e.memset(P.onesk[:], 1.0)

        S.op("V", prep_ones, [], P.sV)

        xn_ok = phase_ln(P, S, x_ld)
        S.flush()

        if want_l0:
            last = build_l0(P, S, H, cc0_in, cc0_out, xn_ok, w_ld, x_ld)
        else:
            last = xn_ok
        if want_l1:
            S.op("V", lambda e: e.tensor_tensor(P.sb2sum[:], P.sb2s[:, 0, :],
                                                P.sb2s[:, 1, :], ALU.add),
                 [x_ld], P.sV)
            if want_l0:
                xn1_ok = phase_ln(P, S, last)
                S.flush()
            else:
                xn1_ok = last
            last = build_l1(P, S, H, cc1_in, cc1_out, cw_dram, xn1_ok, x_ld)

        for c8 in range(8):
            S.dma("S", x_out[c8], P.xst[:, c8, :], [last], P.sO)
        for t in taps:
            S.dma("S", tap_out[t], getattr(P, t)[:], [last], P.sO)
        fin = P.sO.at()
        S.op("S", lambda e: e.nop(), [fin])
        S.flush()

    return nc



def prep_core_inputs(inputs, cid, part):
    t = lambda a: np.ascontiguousarray(np.asarray(a, np.float32).T)
    m = dict(host_consts())
    m["c_pmask"] = np.full((128, 1), 1.0 if cid % 2 else 0.0, np.float32)
    x = np.asarray(inputs["L0_attn_x"], np.float32).reshape(16384, D)
    shard = x[cid * NT:(cid + 1) * NT]
    m["x_in"] = np.ascontiguousarray(shard.T).reshape(8, 128, NT)
    if part in ("l0", "fused"):
        m["l0_wq"] = np.asarray(inputs["L0_attn_Wq"], np.float32).reshape(8, 128, DH)
        m["l0_wk"] = np.asarray(inputs["L0_attn_Wk"], np.float32).reshape(8, 128, DH)
        m["l0_wv"] = np.asarray(inputs["L0_attn_Wv"], np.float32).reshape(8, 128, DH)
        m["l0_wo"] = np.ascontiguousarray(np.asarray(inputs["L0_attn_Wo"], np.float32))
        for nmx in ("UL", "VR", "UR", "VL"):
            for l in range(4):
                m[f"l0_{nmx}T{l}"] = t(inputs[f"L0_attn_{nmx}{l}"])
        m["l0_WT"] = t(inputs["L0_ffn_W"]).reshape(8, 128, D)
        m["l0_b"] = np.ascontiguousarray(
            np.asarray(inputs["L0_ffn_b"], np.float32).reshape(8, 128).T)
        m["l0_ldT"] = t(inputs["L0_ffn_lora_down"])
        m["l0_luT"] = t(inputs["L0_ffn_lora_up"])
        m["l0_da"] = np.ascontiguousarray(np.asarray(inputs["L0_ffn_down_aux"], np.float32))
        m["l0_uaT"] = t(inputs["L0_ffn_up_aux"])
    if part in ("l1", "fused"):
        for w in ("Wq", "Wk", "Wv", "Wo"):
            m[f"l1_{w.lower()}T"] = t(inputs[f"L1_attn_{w}"]).reshape(8, 128, D)
        m["l1_bo"] = np.ascontiguousarray(
            np.asarray(inputs["L1_attn_bo"], np.float32).reshape(8, 128).T)
        m["l1_rw"] = np.asarray(inputs["L1_ffn_router_w"], np.float32).reshape(8, 128, NE)
        m["l1_rb"] = np.asarray(inputs["L1_ffn_router_b"], np.float32).reshape(1, NE)
        m["l1_w1"] = np.asarray(inputs["L1_ffn_w1"], np.float32).reshape(16, 32, 128, EH)
        m["l1_w2"] = np.asarray(inputs["L1_ffn_w2"], np.float32).reshape(16, 8, 128, D)
        m["l1_b1"] = np.ascontiguousarray(
            np.asarray(inputs["L1_ffn_b1"], np.float32).reshape(NE, 2, 128)
            .transpose(2, 0, 1))
        m["l1_b2"] = np.asarray(inputs["L1_ffn_b2"], np.float32)
        m["l1_sw1"] = np.asarray(inputs["L1_ffn_sw1"], np.float32).reshape(2, 8, 128, SH)
        m["l1_sw2"] = np.asarray(inputs["L1_ffn_sw2"], np.float32).reshape(2, 16, 128, D)
        m["l1_sb1"] = np.ascontiguousarray(
            np.asarray(inputs["L1_ffn_sb1"], np.float32).reshape(2, 16, 128)
            .transpose(2, 0, 1))
        m["l1_sb2"] = np.ascontiguousarray(
            np.asarray(inputs["L1_ffn_sb2"], np.float32).reshape(2, 8, 128)
            .transpose(2, 0, 1))
    return m


def run_part(inputs, part, taps=(), x_override=None):
    nc = build_program(part, taps)
    in_maps = [prep_core_inputs(inputs, cid, part) for cid in range(8)]
    if x_override is not None:
        for cid in range(8):
            in_maps[cid]["x_in"] = np.ascontiguousarray(x_override[cid])
    return run_bass_kernel_spmd(nc, in_maps, core_ids=list(range(8)))


def kernel(**inputs):
    res0 = run_part(inputs, "l0")
    x1 = [r["x_out"] for r in res0.results]
    res1 = run_part(inputs, "l1", x_override=x1)
    shards = [r["x_out"].reshape(D, NT).T for r in res1.results]
    return np.concatenate(shards, axis=0).reshape(4, 4096, D)


# version 02 (final)
# speedup vs baseline: 1.0142x; 1.0142x over previous
import numpy as np
from contextlib import ExitStack

import concourse.bass as bass
import concourse.mybir as mybir
from concourse.bass import AP
from concourse.bass_utils import run_bass_kernel_spmd

F32 = mybir.dt.float32
BF16 = mybir.dt.bfloat16
AF = mybir.ActivationFunctionType
ALU = mybir.AluOpType
AX = mybir.AxisListType

NT = 2048
NCH = 16
D = 1024
DH = 64
NH = 16
NE = 64
EH = 256
SH = 2048
BS = [4, 8, 16, 32]
NN = [8, 4, 2, 1]
RL = [4, 8, 8, 8]
PAIRS = [[0, 1], [2, 3], [4, 5], [6, 7]]
L1_PARTS = "all"
L1_DEBUG_SHTAP = False

TAPS = {
    "xnb": ([128, 8, NT], BF16),
    "pqb": ([64, NT], BF16),
    "kb": ([64, NT], BF16),
    "vb": ([64, NT], BF16),
    "ctot": ([64, 65], F32),
    "crun": ([64, 65], F32),
    "attF": ([64, NT], BF16),
    "wct": ([128, 8, D], BF16),
    "gext": ([64, 65], BF16),
    "Bmat": ([64, 16, 64], BF16),
    "cwF": ([64, NT], BF16),
    "stot": ([128, 8, 65], F32),
    "pkF": ([128, 8, NT], BF16),
    "vta1": ([128, NCH, NH, 64], BF16),
    "S_run": ([128, 8, 65], F32),
    "attFt": ([128, 8, 256], BF16),
    "gsil": ([128, 2, 256], BF16),
    "gtil": ([128, 2, 256], BF16),
    "cwbS": ([128, 2, 256], BF16),
    "sb2sum": ([128, 8], F32),
    "sw1j": ([128, 8, SH], BF16),
    "sw2j": ([128, 16, D], BF16),
    "shdbg": ([128, 32, 128], F32),
}


class Sem:
    def __init__(self, ctx, nc, name):
        self.h = ctx.enter_context(nc.semaphore(name))
        self.v = 0
        self.name = name

    def at(self):
        return (self, self.v)


class Sched:
    ENGS = ("T", "V", "A", "G", "S")

    def __init__(self, P):
        self.P = P
        self.q = {e: [] for e in self.ENGS}

    def op(self, eng, emit, waits=(), inc=None, inc_n=1):
        self.q[eng].append((emit, [(s, v) for (s, v) in waits if v > 0], inc, inc_n))
        if inc is not None:
            inc.v += inc_n

    def dma(self, eng, out, in_, waits=(), sem=None):
        sem = sem if sem is not None else (self.P.sW if eng == "G" else self.P.sX)
        self.op(eng, lambda e, o=out, i=in_: e.dma_start(out=o, in_=i), waits, sem, 16)

    def flush(self):
        P = self.P
        secs = {"T": P.blk.tensor, "V": P.blk.vector, "A": P.blk.scalar,
                "G": P.blk.gpsimd, "S": P.blk.sync}
        for eng in self.ENGS:
            if not self.q[eng]:
                continue
            ops = self.q[eng]

            def body(e, ops=ops):
                waited = {}
                for emit, waits, inc, inc_n in ops:
                    for s, v in waits:
                        if waited.get(s.name, 0) < v:
                            e.wait_ge(s.h, v)
                            waited[s.name] = v
                    ins = emit(e)
                    if inc is not None:
                        ins.then_inc(inc.h, inc_n)

            secs[eng](body)
            self.q[eng] = []


class Prog:
    def __init__(self, nc, ctx):
        self.nc = nc
        self.ctx = ctx
        self.blk = ctx.enter_context(nc.Block())
        for n in ("sT", "sV", "sA", "sG", "sW", "sX", "sC", "sO"):
            setattr(self, n, Sem(ctx, nc, n))

    def sb(self, name, shape, dtype):
        t = self.ctx.enter_context(self.nc.sbuf_tensor(name, shape, dtype))
        setattr(self, name, t)
        return t

    def ps(self, name, shape, dtype):
        t = self.ctx.enter_context(self.nc.psum_tensor(name, shape, dtype))
        setattr(self, name, t)
        return t

    def carve(self, name, shape, dtype, off_bytes):
        import math
        elems = 1
        for d in shape[1:]:
            elems *= d
        nbytes = elems * (4 if dtype == F32 else 2)
        assert off_bytes % 4 == 0
        v = self.AR[0:shape[0], off_bytes // 2:(off_bytes + nbytes) // 2]
        if dtype == F32:
            v = v.bitcast(F32)
        if len(shape) == 3:
            v = v.rearrange("p (a b) -> p a b", a=shape[1])
        elif len(shape) == 4:
            v = v.rearrange("p (a b c) -> p a b c", a=shape[1], b=shape[2])
        setattr(self, name, v)
        return v


def bcast_free(t_ap, pdim, pcount, reps, inner):
    return AP(t_ap.tensor, t_ap.offset, [[pdim, pcount], [0, reps], [1, inner]])


def free_bcast(ap2d, reps):
    a = [list(x) for x in ap2d.ap]
    assert len(a) == 2
    return AP(ap2d.tensor, ap2d.offset, [a[0], a[1], [0, reps]])



def host_consts():
    c = {}
    c["c_idn"] = np.eye(128, dtype=np.float32)
    c["c_tri"] = np.triu(np.ones((128, 128), dtype=np.float32))
    hm = np.zeros((128, 2), np.float32)
    hm[0:64, 0] = 1.0
    hm[64:128, 1] = 1.0
    c["c_hmask"] = hm
    nm = np.zeros((64, 46), np.float32)
    col = 0
    for n in range(16):
        nm[4 * n:4 * n + 4, col] = 1.0
        col += 1
    for l in range(3):
        bs = BS[l]
        for n in range(NN[l]):
            nm[n * 2 * bs + bs:(n + 1) * 2 * bs, col] = 1.0
            col += 1
    for l in range(3):
        bs = BS[l]
        for n in range(NN[l]):
            nm[n * 2 * bs:n * 2 * bs + bs, col] = 1.0
            col += 1
    assert col == 44
    c["c_nodemask"] = nm
    pm = np.zeros((64, 9, 64), np.float32)
    for n in range(16):
        pm[4 * n:4 * n + 4, 0, 4 * n:4 * n + 4] = 1.0
    for l in range(4):
        bs = BS[l]
        for n in range(NN[l]):
            r0 = n * 2 * bs
            pm[r0:r0 + bs, 1 + l, r0 + bs:r0 + 2 * bs] = 1.0
            pm[r0 + bs:r0 + 2 * bs, 5 + l, r0:r0 + bs] = 1.0
    c["c_projmask"] = pm
    return c



def phase_ln(P, S, x_ready):
    pf = P.pfb
    for c8 in range(8):
        prev_mm = P.sT.at()
        S.op("V", lambda e, c=c8: e.tensor_copy(P.xbch[:], P.xst[:, c, :]),
             [x_ready, prev_mm], P.sV)
        xb_ok = P.sV.at()
        S.op("V", lambda e: e.tensor_tensor(P.xsq[:], P.xbch[:], P.xbch[:], ALU.mult),
             [], P.sV)
        sq_ok = P.sV.at()
        for q4 in range(4):
            b1, b2 = pf[q4 % 2], pf[2 + q4 % 2]
            prev_acc = P.sV.at()
            S.op("T", lambda e, q=q4, b=b1: e.matmul(
                b[0:1, :], P.onesk[:], P.xbch[:, 512 * q:512 * (q + 1)],
                start=True, stop=True), [xb_ok, prev_acc], P.sT)
            mm1 = P.sT.at()
            S.op("T", lambda e, q=q4, b=b2: e.matmul(
                b[0:1, :], P.onesk[:], P.xsq[:, 512 * q:512 * (q + 1)],
                start=True, stop=True), [sq_ok], P.sT)
            mm2 = P.sT.at()
            if c8 == 0:
                S.op("V", lambda e, q=q4, b=b1: e.tensor_copy(
                    P.accm[0:1, 512 * q:512 * (q + 1)], b[0:1, :]), [mm1], P.sV)
                S.op("V", lambda e, q=q4, b=b2: e.tensor_copy(
                    P.accv[0:1, 512 * q:512 * (q + 1)], b[0:1, :]), [mm2], P.sV)
            else:
                S.op("V", lambda e, q=q4, b=b1: e.tensor_tensor(
                    P.accm[0:1, 512 * q:512 * (q + 1)], P.accm[0:1, 512 * q:512 * (q + 1)],
                    b[0:1, :], ALU.add), [mm1], P.sV)
                S.op("V", lambda e, q=q4, b=b2: e.tensor_tensor(
                    P.accv[0:1, 512 * q:512 * (q + 1)], P.accv[0:1, 512 * q:512 * (q + 1)],
                    b[0:1, :], ALU.add), [mm2], P.sV)
    S.op("V", lambda e: e.tensor_scalar(P.accm, P.accm, 1.0 / D, None, ALU.mult),
         [], P.sV)
    S.op("V", lambda e: e.tensor_scalar(P.accv, P.accv, 1.0 / D, None, ALU.mult),
         [], P.sV)
    S.op("V", lambda e: e.tensor_tensor(P.rrow, P.accm, P.accm, ALU.mult), [], P.sV)
    S.op("V", lambda e: e.tensor_tensor(P.accv, P.accv, P.rrow, ALU.subtract),
         [], P.sV)
    S.op("V", lambda e: e.tensor_scalar(P.accv, P.accv, 1e-5, None, ALU.add),
         [], P.sV)
    var_ok = P.sV.at()
    S.op("A", lambda e: e.activation(P.accv, P.accv, AF.Ln), [var_ok], P.sA)
    S.op("A", lambda e: e.activation(P.rrow, P.accv, AF.Exp, scale=-0.5), [], P.sA)
    rstd_ok = P.sA.at()
    for q4 in range(4):
        prev_v = P.sV.at()
        S.op("T", lambda e, q=q4: e.matmul(
            pf[4][:], P.ones1f[:], P.accm[0:1, 512 * q:512 * (q + 1)],
            start=True, stop=True), [rstd_ok, prev_v], P.sT)
        S.op("T", lambda e, q=q4: e.matmul(
            pf[5][:], P.ones1f[:], P.rrow[0:1, 512 * q:512 * (q + 1)],
            start=True, stop=True), [], P.sT)
        bc_mm = P.sT.at()
        for c8 in range(8):
            sl = slice(512 * q4, 512 * (q4 + 1))
            S.op("V", lambda e, c=c8, s=sl: e.tensor_tensor(
                P.xsc32[:], P.xst[:, c, s], pf[4][:], ALU.subtract),
                [bc_mm], P.sV)
            S.op("V", lambda e, c=c8, s=sl: e.tensor_tensor(
                P.xnb[:, c, s], P.xsc32[:], pf[5][:], ALU.mult), [], P.sV)
    return P.sV.at()


ELU_MODE = "copyfirst"


def elu1_evac(P, S, src_psum, dst, tmp_a, tmp_r, waits):
    if ELU_MODE == "copyfirst":
        S.op("V", lambda e: e.tensor_copy(tmp_r, src_psum), waits, P.sV)
        c_ok = P.sV.at()
        S.op("A", lambda e: e.activation(tmp_a, tmp_r, AF.Exp), [c_ok], P.sA)
        a_ok = P.sA.at()
        S.op("V", lambda e: e.tensor_scalar(tmp_r, tmp_r, 0.0, None, ALU.max),
             [a_ok], P.sV)
        S.op("V", lambda e: e.scalar_tensor_tensor(dst, tmp_a, 1.0, tmp_r,
                                                   ALU.min, ALU.add), [], P.sV)
        return P.sV.at()
    if ELU_MODE == "split64":
        np_ = src_psum.shape[0]
        hp = np_ // 2
        for ph in range(2):
            ps = slice(hp * ph, hp * (ph + 1))
            S.op("A", lambda e, ps=ps: e.activation(tmp_a[ps], src_psum[ps],
                                                    AF.Exp), waits, P.sA)
            a_ok = P.sA.at()
            S.op("V", lambda e, ps=ps: e.tensor_scalar(
                tmp_r[ps], src_psum[ps], 0.0, None, ALU.max),
                list(waits) + [a_ok], P.sV)
        S.op("V", lambda e: e.scalar_tensor_tensor(dst, tmp_a, 1.0, tmp_r,
                                                   ALU.min, ALU.add), [], P.sV)
        return P.sV.at()
    S.op("A", lambda e: e.activation(tmp_a, src_psum, AF.Exp), waits, P.sA)
    a_ok = P.sA.at()
    S.op("V", lambda e: e.tensor_scalar(tmp_r, src_psum, 0.0, None, ALU.max),
         list(waits) + [a_ok], P.sV)
    S.op("V", lambda e: e.scalar_tensor_tensor(dst, tmp_a, 1.0, tmp_r, ALU.min, ALU.add),
         [], P.sV)
    return P.sV.at()



def build_l0(P, S, H, cc0_in, cc0_out, xn_ok, w_ld, x_ld):
    pf = P.pfb

    for tt in range(4):
        sl = slice(512 * tt, 512 * (tt + 1))
        prev_ev = P.sV.at()
        prev_ac = P.sA.at()
        for wi, (wname, prow, bank) in enumerate(
                (("l0wq", 0, pf[0]), ("l0wk", 64, pf[0]), ("l0wv", 0, pf[1]))):
            for ic in range(8):
                S.op("T", lambda e, w=wname, p=prow, b=bank, i=ic, s=sl: e.matmul(
                    b[p:p + 64, :], getattr(P, w)[:, i, :], P.xnb[:, i, s],
                    start=(i == 0), stop=(i == 7)),
                    [xn_ok, w_ld, prev_ev, prev_ac] if ic == 0 else [], P.sT)
        mm_ok = P.sT.at()
        elu1_evac(P, S, pf[0][0:64, :], P.pqb[0:64, sl], P.tmpa,
                  P.tmpr, [mm_ok])
        elu1_evac(P, S, pf[0][64:128, :], P.kb[0:64, sl], P.etmp2[0:64, 0:512],
                  P.etmp2[0:64, 512:1024], [mm_ok])
        S.op("V", lambda e, s=sl: e.tensor_copy(P.vb[0:64, s], pf[1][0:64, :]),
             [mm_ok], P.sV)
    qkv_ok = P.sV.at()

    S.op("V", lambda e: e.memset(P.vexta[:, :, 64], 1.0), [], P.sV)
    st_done = None
    for c in range(NCH):
        sl = slice(128 * c, 128 * (c + 1))
        pbk = P.pbt if c % 2 == 0 else P.pbt2
        prev_ev = P.sV.at()
        S.op("T", lambda e, s=sl, b=pbk: e.transpose(
            b[:, 0:64], P.kb[0:64, s], P.idn[0:64, 0:64]), [qkv_ok, prev_ev], P.sT)
        S.op("T", lambda e, s=sl, b=pbk: e.transpose(
            b[:, 64:128], P.vb[0:64, s], P.idn[0:64, 0:64]), [], P.sT)
        tr_ok = P.sT.at()
        S.op("V", lambda e, c=c, b=pbk: e.tensor_copy(P.kTa[:, c, :], b[:, 0:64]),
             [tr_ok], P.sV)
        S.op("V", lambda e, c=c, b=pbk: e.tensor_copy(P.vexta[:, c, 0:64], b[:, 64:128]),
             [], P.sV)
        ev_ok = P.sV.at()
        S.op("T", lambda e, c=c: e.matmul(
            pf[2][0:64, 0:65], P.kTa[:, c, :], P.vexta[:, c, :],
            start=(c == 0), stop=(c == NCH - 1)), [ev_ok], P.sT)
        st_done = P.sT.at()
    S.op("V", lambda e: e.tensor_copy(P.ctot[:], pf[2][0:64, 0:65]), [st_done], P.sV)
    ctot_ok = P.sV.at()
    S.dma("S", cc0_in[:], P.ctot[:], [ctot_ok])
    stg_ok = P.sX.at()
    S.op("G", lambda e: e.collective_compute(
        "AllGather", ALU.bypass, replica_groups=PAIRS,
        ins=[cc0_in[:]], outs=[cc0_out[:]]), [stg_ok], P.sC)
    cc_ok = P.sC.at()
    if L1_PARTS == "cc1":
        S.flush()
        return cc_ok

    S.op("V", lambda e: e.memset(P.Bmat[:], 0.0), [], P.sV)
    bm_z = P.sV.at()
    for l in range(4):
        for (wa, wb2, idx) in ((f"l0ULT{l}", f"l0ULT{l}", 0),
                               (f"l0VRT{l}", f"l0VRT{l}", 1),
                               (f"l0ULT{l}", f"l0VLT{l}", 2),
                               (f"l0VRT{l}", f"l0URT{l}", 3)):
            prev = P.sV.at()
            S.op("T", lambda e, a=wa, b=wb2, l=l: e.matmul(
                pf[3][0:BS[l], 0:BS[l]], getattr(P, a)[:], getattr(P, b)[:],
                start=True, stop=True), [x_ld, prev], P.sT)
            mm_ok = P.sT.at()
            S.op("V", lambda e, l=l: e.tensor_copy(
                P.bstg[0:BS[l], 0:BS[l]], pf[3][0:BS[l], 0:BS[l]]), [mm_ok], P.sV)
            ev_ok = P.sV.at()
            bs = BS[l]
            for n in range(NN[l]):
                r0 = n * 2 * bs + (bs if idx in (1, 3) else 0)
                S.dma("S", P.Bmat[r0:r0 + bs, 4 * l + idx, r0:r0 + bs],
                      P.bstg[0:bs, 0:bs], [ev_ok, bm_z])
    bm_ok = P.sX.at()

    prev = P.sV.at()
    S.op("T", lambda e: e.matmul(pf[3][0:4, 0:512], P.l0ldT[:], P.l0da[:, 0:512],
                                 start=True, stop=True), [x_ld, w_ld, prev], P.sT)
    S.op("T", lambda e: e.matmul(pf[4][0:4, 0:512], P.l0ldT[:], P.l0da[:, 512:1024],
                                 start=True, stop=True), [], P.sT)
    amm = P.sT.at()
    S.op("V", lambda e: e.tensor_copy(P.amat[0:4, 0:512], pf[3][0:4, 0:512]), [amm], P.sV)
    S.op("V", lambda e: e.tensor_copy(P.amat[0:4, 512:1024], pf[4][0:4, 0:512]),
         [], P.sV)
    a_ok = P.sV.at()
    S.op("T", lambda e: e.matmul(pf[3][0:50, 0:512], P.l0luT[:], P.amat[0:4, 0:512],
                                 start=True, stop=True), [a_ok], P.sT)
    S.op("T", lambda e: e.matmul(pf[4][0:50, 0:512], P.l0luT[:], P.amat[0:4, 512:1024],
                                 start=True, stop=True), [], P.sT)
    bmm = P.sT.at()
    S.op("V", lambda e: e.tensor_copy(P.bmat2[0:50, 0:512], pf[3][0:50, 0:512]),
         [bmm], P.sV)
    S.op("V", lambda e: e.tensor_copy(P.bmat2[0:50, 512:1024], pf[4][0:50, 0:512]),
         [], P.sV)
    b_ok = P.sV.at()
    for ic in range(8):
        S.dma("S", P.wtb[:, 0, :], H["l0_WT"][ic], [P.sV.at()])
        wt_ld = P.sX.at()
        prev = P.sV.at()
        for half in range(2):
            S.op("T", lambda e, i=ic, h=half: e.matmul(
                pf[3 + h][:, :], P.bmat2[0:50, 128 * i:128 * (i + 1)],
                P.l0uaT[:, 512 * h:512 * (h + 1)], start=True, stop=True),
                [b_ok, prev], P.sT)
        t3_ok = P.sT.at()
        for half in range(2):
            S.op("V", lambda e, i=ic, h=half: e.tensor_tensor(
                P.wct[:, i, 512 * h:512 * (h + 1)],
                P.wtb[:, 0, 512 * h:512 * (h + 1)],
                pf[3 + h][:, :], ALU.add), [t3_ok, wt_ld], P.sV)
    wct_ok = P.sV.at()
    for oc in range(8):
        for tt in range(4):
            sl = slice(512 * tt, 512 * (tt + 1))
            bank = pf[5] if (oc * 4 + tt) % 2 == 0 else pf[2]
            prev = P.sV.at()
            for ic in range(8):
                S.op("T", lambda e, i=ic, o=oc, s=sl, b=bank: e.matmul(
                    b[:], P.wct[:, i, 128 * o:128 * (o + 1)], P.xnb[:, i, s],
                    start=(i == 0), stop=(i == 7)),
                    [wct_ok, prev] if ic == 0 else [], P.sT)
            mm_ok = P.sT.at()
            S.op("V", lambda e, o=oc, s=sl, b=bank: e.scalar_tensor_tensor(
                P.xst[:, o, s], b[:], P.l0b[:, o:o + 1], P.xst[:, o, s],
                ALU.add, ALU.add), [mm_ok], P.sV)
    ffn_ok = P.sV.at()

    S.dma("S", P.gth[:], cc0_out.rearrange("(r p) n -> p r n", r=2), [cc_ok])
    gth_ok = P.sX.at()
    S.op("V", lambda e: e.tensor_scalar(P.crun[:], P.gth[:, 0, :], P.pmask[0:64, 0:1],
                                        None, ALU.mult), [gth_ok], P.sV)
    P.crun_w = P.sV.at()

    build_l0_attn2(P, S, bm_ok, qkv_ok)
    att_ok = P.sV.at()

    for oc in range(8):
        for tt in range(4):
            sl = slice(512 * tt, 512 * (tt + 1))
            bank = pf[0] if (oc * 4 + tt) % 2 == 0 else pf[1]
            prev = P.sV.at()
            S.op("T", lambda e, o=oc, s=sl, b=bank: e.matmul(
                b[:], P.l0wo[0:64, 128 * o:128 * (o + 1)], P.attF[0:64, s],
                start=True, stop=True), [att_ok, ffn_ok, prev], P.sT)
            mm_ok = P.sT.at()
            S.op("V", lambda e, o=oc, s=sl, b=bank: e.tensor_tensor(
                P.xst[:, o, s], P.xst[:, o, s], b[:], ALU.add), [mm_ok], P.sV)
    S.flush()
    return P.sV.at()


def build_l0_attn2(P, S, bm_ok, qkv_ok):
    pf = P.pfb
    comps = []
    for n in range(16):
        comps.append(("leaf", None, n, (4 * n, 4 * n + 4)))
    ci = 16
    for l in range(3):
        bs = BS[l]
        for n in range(NN[l]):
            comps.append(("tr", l, ci, (n * 2 * bs, n * 2 * bs + bs)))
            ci += 1
    comps.append(("tr", 3, None, (0, 32)))
    for l in range(3):
        bs = BS[l]
        for n in range(NN[l]):
            comps.append(("bl", l, ci, (n * 2 * bs + bs, n * 2 * bs + 2 * bs)))
            ci += 1
    comps.append(("bl", 3, None, (32, 64)))
    comps.append(("den", None, None, (64, 65)))
    assert len(comps) == 47 and ci == 44

    last_y = P.sV.at()
    last_grp_ev = {0: P.sV.at(), 1: P.sV.at()}
    P.mskslot = {s: (P.sT, 0) for s in range(8)}
    P.amslot = {0: (P.sT, 0), 1: (P.sT, 0)}

    for c in range(NCH):
        sl = slice(128 * c, 128 * (c + 1))
        prev_tf = P.sV.at()
        for l in range(4):
            S.op("T", lambda e, l=l, s=sl: e.matmul(
                pf[2][0:64, 128 * l:128 * (l + 1)], P.Bmat[:, 4 * l + 1, :],
                P.vb[0:64, s], start=True, stop=True),
                [qkv_ok, bm_ok, prev_tf], P.sT)
        tfv_mm = P.sT.at()
        S.op("V", lambda e: e.tensor_copy(
            P.tfF[:, 0:4, :], pf[2][0:64, 0:512].rearrange("p (l n) -> p l n", l=4)),
            [tfv_mm], P.sV)
        tfv_ok = P.sV.at()
        for l in range(4):
            S.op("T", lambda e, l=l, s=sl: e.matmul(
                pf[2][0:64, 128 * l:128 * (l + 1)], P.Bmat[:, 4 * l + 2, :],
                P.kb[0:64, s], start=True, stop=True), [tfv_ok], P.sT)
        tfk_mm = P.sT.at()
        S.op("V", lambda e: e.tensor_copy(
            P.tfF[:, 4:8, :], pf[2][0:64, 0:512].rearrange("p (l n) -> p l n", l=4)),
            [tfk_mm], P.sV)
        tfF_ok = P.sV.at()
        for l in range(4):
            S.op("T", lambda e, l=l, s=sl: e.matmul(
                pf[3][:, 64 * l:64 * (l + 1)], P.kb[0:64, s], P.Bmat[:, 4 * l + 0, :],
                start=True, stop=True), [tfF_ok], P.sT)
            S.op("T", lambda e, l=l, s=sl: e.matmul(
                pf[3][:, 256 + 64 * l:256 + 64 * (l + 1)], P.vb[0:64, s],
                P.Bmat[:, 4 * l + 3, :], start=True, stop=True), [], P.sT)
        tft_mm = P.sT.at()
        S.op("V", lambda e: e.tensor_copy(
            P.tfT[:], pf[3][:, 0:512].rearrange("p (l n) -> p l n", l=8)),
            [tft_mm], P.sV)
        tfT_ok = P.sV.at()
        S.op("V", lambda e: e.tensor_copy(P.crb[:], P.crun[:]), [P.crun_w], P.sV)
        crb_ok = P.sV.at()
        pbk = P.pbt if c % 2 == 0 else P.pbt2
        S.op("T", lambda e, b=pbk: e.transpose(b[0:64, 896:960], P.crb[0:64, 0:64],
                                               P.idn[0:64, 0:64]), [crb_ok], P.sT)
        ctr_mm = P.sT.at()
        S.op("V", lambda e, b=pbk: e.tensor_copy(P.cts[:], b[0:64, 896:960]),
             [ctr_mm], P.sV)
        cts_ok = P.sV.at()
        for l in range(4):
            S.op("V", lambda e, l=l: e.tensor_tensor(
                P.el[:, l, :], P.crb[:, 0:64], P.s_projmask[:, 1 + l, :], ALU.mult),
                [crb_ok], P.sV)
            S.op("V", lambda e, l=l: e.tensor_tensor(
                P.elt[:, l, :], P.cts[:], P.s_projmask[:, 5 + l, :], ALU.mult),
                [cts_ok], P.sV)
        S.op("V", lambda e: e.tensor_tensor(P.eleaf[:], P.cts[:], P.s_projmask[:, 0, :],
                                            ALU.mult), [cts_ok], P.sV)
        e_ok = P.sV.at()
        prev2 = P.sV.at()
        for l in range(4):
            S.op("T", lambda e, l=l: e.matmul(
                pf[2][0:64, 64 * l:64 * (l + 1)], P.el[:, l, :], P.Bmat[:, 4 * l + 0, :],
                start=True, stop=True), [e_ok, prev2], P.sT)
            S.op("T", lambda e, l=l: e.matmul(
                pf[3][0:64, 64 * l:64 * (l + 1)], P.elt[:, l, :], P.Bmat[:, 4 * l + 3, :],
                start=True, stop=True), [e_ok], P.sT)
        stg_mm = P.sT.at()
        S.op("V", lambda e: e.tensor_copy(P.s1b[:, 0:4, :], pf[2][0:64, 0:256]
                                          .rearrange("p (l n) -> p l n", l=4)),
             [stg_mm], P.sV)
        S.op("V", lambda e: e.tensor_copy(P.s1b[:, 4:8, :], pf[3][0:64, 0:256]
                                          .rearrange("p (l n) -> p l n", l=4)),
             [stg_mm], P.sV)
        s1_ok = P.sV.at()
        for l in range(4):
            S.op("T", lambda e, l=l: e.matmul(
                pf[4][0:64, 0:64], P.Bmat[:, 4 * l + 1, :], P.s1b[:, l, :],
                start=(l == 0), stop=False), [s1_ok], P.sT)
            S.op("T", lambda e, l=l: e.matmul(
                pf[4][0:64, 0:64], P.Bmat[:, 4 * l + 2, :], P.s1b[:, 4 + l, :],
                start=False, stop=(l == 3)), [], P.sT)
        gt_mm = P.sT.at()
        S.op("V", lambda e: e.tensor_tensor(P.gext[:, 0:64], pf[4][0:64, 0:64],
                                            P.eleaf[:], ALU.add), [gt_mm], P.sV)
        S.op("V", lambda e: e.tensor_copy(P.gext[:, 64:65], P.crun[:, 64:65]),
             [P.crun_w], P.sV)
        gext_ok = P.sV.at()

        S.op("T", lambda e, c=c: e.matmul(
            pf[4][0:64, 64:129], P.kTa[:, c, :], P.vexta[:, c, :],
            start=True, stop=True), [gext_ok], P.sT)
        st_mm = P.sT.at()
        S.op("V", lambda e: e.tensor_tensor(P.crun[:], P.crun[:], pf[4][0:64, 64:129],
                                            ALU.add), [st_mm, P.crun_w], P.sV)
        P.crun_w = P.sV.at()

        for g in range(12):
            bank = pf[g % 2]
            n_in_g = min(4, 47 - 4 * g)
            for j in range(n_in_g):
                i = 4 * g + j
                kind, l, mcol, cols = comps[i]
                if mcol is None:
                    continue
                if kind == "leaf":
                    srcf = lambda e, s=sl: P.vb[0:64, s]
                elif kind == "tr":
                    srcf = lambda e, l=l: P.tfF[:, l, :]
                else:
                    srcf = lambda e, l=l: P.tfF[:, 4 + l, :]
                S.op("V", lambda e, f=srcf, m=mcol, i=i: e.tensor_scalar(
                    P.msk[:, i % 8, :], f(e), P.s_nodemask[:, m:m + 1], None,
                    ALU.mult), [tfF_ok, P.mskslot[i % 8]], P.sV)
            mcp_ok = P.sV.at()
            for j in range(n_in_g):
                i = 4 * g + j
                kind, l, mcol, cols = comps[i]
                if mcol is not None:
                    lhsT_f = lambda e, i=i: P.msk[:, i % 8, :]
                elif kind == "tr":
                    lhsT_f = lambda e: P.tfF[:, 3, :]
                elif kind == "bl":
                    lhsT_f = lambda e: P.tfF[:, 7, :]
                else:
                    lhsT_f = lambda e, s=sl: P.kb[0:64, s]
                S.op("T", lambda e, f=lhsT_f, b=bank, j=j, s=sl: e.matmul(
                    b[:, 128 * j:128 * (j + 1)], f(e), P.pqb[0:64, s],
                    start=True, stop=True),
                    [mcp_ok, last_grp_ev[g % 2]] if j == 0 else [], P.sT)
            g_mm = P.sT.at()
            for j in range(n_in_g):
                P.mskslot[(4 * g + j) % 8] = g_mm
            S.op("V", lambda e, b=bank, n=n_in_g, g=g: e.tensor_tensor(
                P.Am[:, 4 * g:4 * g + n, :], b[:, 0:128 * n],
                bcast_free(P.tri[:], 128, 128, n, 128), ALU.mult),
                [g_mm], P.sV)
            last_grp_ev[g % 2] = P.sV.at()
        sc_ok = P.sV.at()

        S.op("T", lambda e, s=sl: e.matmul(pf[5][:, 0:65], P.pqb[0:64, s], P.gext[:],
                                           start=True, stop=False),
             [gext_ok, sc_ok, tfT_ok, last_y], P.sT)
        for i, (kind, l, mcol, cols) in enumerate(comps):
            if kind == "leaf":
                rhs_f = lambda e, c=c, co=cols: P.kTa[:, c, co[0]:co[1]]
            elif kind == "tr":
                rhs_f = lambda e, l=l, co=cols: P.tfT[:, l, co[0]:co[1]]
            elif kind == "bl":
                rhs_f = lambda e, l=l, co=cols: P.tfT[:, 4 + l, co[0]:co[1]]
            else:
                rhs_f = lambda e: P.onesc[:]
            S.op("T", lambda e, r=rhs_f, co=cols, i=i: e.matmul(
                pf[5][:, co[0]:co[1]], P.Am[:, i, :], r(e), start=False,
                stop=(i == 46)), [], P.sT)
        y_mm = P.sT.at()

        S.op("V", lambda e: e.tensor_scalar(P.dcol[:, 0:1], pf[5][:, 64:65], 1e-6,
                                            None, ALU.max), [y_mm], P.sV)
        d_ok = P.sV.at()
        S.op("V", lambda e: e.reciprocal(P.dcol[:, 1:2], P.dcol[:, 0:1]), [d_ok], P.sV)
        r_ok = P.sV.at()
        S.op("V", lambda e: e.tensor_scalar(P.att_tok[:], pf[5][:, 0:64],
                                            P.dcol[:, 1:2], None, ALU.mult),
             [r_ok], P.sV)
        atok_ok = P.sV.at()
        last_y = atok_ok
        S.op("T", lambda e, b=pbk: e.transpose(b[0:64, 768:896], P.att_tok[:],
                                               P.idn[:]), [atok_ok], P.sT)
        atr_mm = P.sT.at()
        S.op("V", lambda e, s=sl, b=pbk: e.tensor_copy(P.attF[0:64, s],
                                                       b[0:64, 768:896]),
             [atr_mm], P.sV)





def l1_carves(P):
    P.carve("pkF", [128, 8, NT], BF16, 0)
    P.carve("vta1", [128, NCH, NH, 64], BF16, 32768)
    P.carve("wbig", [128, 8, D], BF16, 65536)
    P.carve("S_run", [128, 8, 65], F32, 81920)
    P.carve("S_runb", [128, 8, 65], BF16, 84032)
    P.carve("stot", [128, 8, 65], F32, 85088)
    P.carve("wk2", [128, NCH, NE], F32, 85088)
    P.carve("lg", [128, NCH, NE], F32, 89184)
    P.carve("sel", [128, NCH, NE], BF16, 93280)
    P.carve("mtmp", [128, NCH, NE], BF16, 95328)
    P.carve("gth1", [128, 8, 65], F32, 85088)
    P.carve("pqt", [128, 8, 128], BF16, 89344)
    P.carve("attFt", [128, 8, 256], BF16, 91392)
    P.carve("att_tok1", [128, NH, 64], BF16, 95488)
    P.carve("pkTs", [128, 8, 128], BF16, 97536)
    P.carve("Am1", [128, 4, 128], BF16, 99584)
    P.carve("cwF", [64, NT], BF16, 100608)
    P.carve("dden", [128, 32], F32, 104704)
    P.carve("eluA", [128, 256], F32, 85088)
    P.carve("eluB", [128, 256], F32, 86112)
    P.carve("eluC", [128, 256], F32, 87136)
    P.carve("eluD", [128, 256], F32, 88160)
    P.u_w1, P.u_w2 = [], []
    for u in range(3):
        P.u_w1.append(P.carve(f"u{u}w1", [128, 4, 8, EH], BF16, 32768 * u))
        P.u_w2.append(P.carve(f"u{u}w2", [128, 4, 2, D], BF16, 32768 * u + 16384))
    P.carve("gsil", [128, 2, 256], BF16, 104832)
    P.carve("gtil", [128, 2, 256], BF16, 105856)
    P.carve("cwbS", [128, 2, 256], BF16, 98304)
    P.carve("pqtZ", [128, 16, 128], BF16, 106880)
    P.carve("sw1j", [128, 8, SH], BF16, 0)
    P.carve("sw2j", [128, 16, D], BF16, 32768)
    if L1_DEBUG_SHTAP:
        P.carve("shdbg", [128, 32, 128], F32, 65536)
    P.tmpa1 = P.eluA[:]
    P.tmpr1 = P.eluB[:]


def build_l1(P, S, H, cc1_in, cc1_out, cw_dram, xn_ok, x_ld):
    pf = P.pfb
    l1_carves(P)
    if L1_PARTS == "ln":
        S.flush()
        return xn_ok

    S.dma("G", P.wbig[:], H["l1_wkT"].rearrange("c p n -> p c n"), [xn_ok])
    wk_ld = P.sW.at()
    if L1_PARTS == "p1w":
        S.flush()
        return wk_ld
    for oc in range(8):
        for tt in range(4):
            sl = slice(512 * tt, 512 * (tt + 1))
            bank = pf[0] if (oc * 4 + tt) % 2 == 0 else pf[1]
            prev = P.sV.at()
            prev_a = P.sA.at()
            for ic in range(8):
                S.op("T", lambda e, i=ic, o=oc, s=sl, b=bank: e.matmul(
                    b[:], P.wbig[:, i, 128 * o:128 * (o + 1)], P.xnb[:, i, s],
                    start=(i == 0), stop=(i == 7)),
                    [xn_ok, wk_ld, prev, prev_a] if ic == 0 else [], P.sT)
            mm_ok = P.sT.at()
            if L1_PARTS == "p1k":
                continue
            for half in range(2):
                hs = slice(512 * tt + 256 * half, 512 * tt + 256 * (half + 1))
                if L1_PARTS in ("p1x", "p1y", "p1z"):
                    ta = P.eluA[:] if half == 0 else P.eluC[:]
                    tr = P.eluB[:] if half == 0 else P.eluD[:]
                    S.op("A", lambda e, b=bank, h=half, ta=ta: e.activation(
                        ta, b[:, 256 * h:256 * (h + 1)], AF.Exp), [mm_ok], P.sA)
                    a_ok = P.sA.at()
                    if L1_PARTS == "p1y":
                        S.op("V", lambda e, b=bank, h=half, tr=tr: e.tensor_scalar(
                            tr, b[:, 256 * h:256 * (h + 1)], 0.0, None, ALU.max),
                            [mm_ok, a_ok], P.sV)
                    elif L1_PARTS == "p1z":
                        S.op("V", lambda e, o=oc, hs=hs, ta=ta: e.scalar_tensor_tensor(
                            P.pkF[:, o, hs], ta, 1.0, ta, ALU.min, ALU.add),
                            [a_ok], P.sV)
                    continue
                elu1_evac(P, S, bank[:, 256 * half:256 * (half + 1)],
                          P.pkF[:, oc, hs],
                          P.eluA[:] if half == 0 else P.eluC[:],
                          P.eluB[:] if half == 0 else P.eluD[:], [mm_ok])
    if L1_PARTS == "p1k":
        S.flush()
        return P.sT.at()
    if L1_PARTS == "p1x":
        S.flush()
        return P.sA.at()
    if L1_PARTS in ("p1y", "p1z"):
        S.flush()
        return P.sV.at()
    pk_ok = P.sV.at()
    if L1_PARTS == "p1a":
        S.flush()
        return pk_ok
    S.dma("G", P.wbig[:], H["l1_wvT"].rearrange("c p n -> p c n"), [pk_ok])
    wv_ld = P.sW.at()
    for c in range(NCH):
        slc = slice(128 * c, 128 * (c + 1))
        for half in range(2):
            bank = pf[0] if (c * 2 + half) % 2 == 0 else pf[1]
            prev = P.sV.at()
            for ic in range(8):
                S.op("T", lambda e, i=ic, h=half, s=slc, b=bank: e.matmul(
                    b[:], P.xnb[:, i, s], P.wbig[:, i, 512 * h:512 * (h + 1)],
                    start=(i == 0), stop=(i == 7)),
                    [wv_ld, prev] if ic == 0 else [], P.sT)
            mm_ok = P.sT.at()
            S.op("V", lambda e, c=c, h=half, b=bank: e.tensor_copy(
                P.vta1[:, c, 8 * h:8 * (h + 1), :].rearrange("p h n -> p (h n)"),
                b[:]), [mm_ok], P.sV)
    v_ok = P.sV.at()
    if L1_PARTS == "p1b":
        S.flush()
        return v_ok
    S.op("V", lambda e: e.memset(P.stot.rearrange("p a b -> p (a b)"), 0.0),
         [], P.sV)
    for c in range(NCH):
        slc = slice(128 * c, 128 * (c + 1))
        pbk = P.pbt if c % 2 == 0 else P.pbt2
        prev_ev = P.sV.at()
        for hp in range(8):
            S.op("T", lambda e, hp=hp, s=slc, b=pbk: e.transpose(
                b[:, 128 * hp:128 * (hp + 1)], P.pkF[:, hp, s], P.idn[:]),
                [pk_ok, v_ok, prev_ev] if hp == 0 else [], P.sT)
        tr_ok = P.sT.at()
        S.op("V", lambda e, b=pbk: e.tensor_copy(
            P.pkTs[:], b[:, 0:1024].rearrange("p (a b) -> p a b", a=8)),
            [tr_ok], P.sV)
        ev_ok = P.sV.at()
        for h in range(NH):
            hp, half = h // 2, h % 2
            bank = pf[4] if hp < 4 else pf[5]
            col = (hp % 4) * 65
            S.op("T", lambda e, hp=hp, hf=half, c=c, b=bank, col=col: e.matmul(
                b[64 * hf:64 * (hf + 1), col:col + 64],
                P.pkTs[:, hp, 64 * hf:64 * (hf + 1)], P.vta1[:, c, 2 * hp + hf, :],
                start=True, stop=False), [ev_ok], P.sT)
            S.op("T", lambda e, hp=hp, hf=half, b=bank, col=col: e.matmul(
                b[64 * hf:64 * (hf + 1), col + 64:col + 65],
                P.pkTs[:, hp, 64 * hf:64 * (hf + 1)], P.onesc[:],
                start=False, stop=True), [], P.sT)
        st_mm = P.sT.at()
        S.op("V", lambda e: e.tensor_tensor(
            P.stot[:, 0:4, :].rearrange("p a b -> p (a b)"),
            P.stot[:, 0:4, :].rearrange("p a b -> p (a b)"), pf[4][:, 0:260],
            ALU.add), [st_mm], P.sV)
        S.op("V", lambda e: e.tensor_tensor(
            P.stot[:, 4:8, :].rearrange("p a b -> p (a b)"),
            P.stot[:, 4:8, :].rearrange("p a b -> p (a b)"), pf[5][:, 0:260],
            ALU.add), [], P.sV)
    stot_ok = P.sV.at()
    if L1_PARTS == "p1c":
        S.flush()
        return stot_ok
    S.dma("S", cc1_in.rearrange("(hp p) n -> p hp n", hp=8), P.stot[:], [stot_ok])
    stg_ok = P.sX.at()
    if L1_PARTS == "p1":
        S.flush()
        return stg_ok
    S.op("G", lambda e: e.collective_compute(
        "AllGather", ALU.bypass, replica_groups=PAIRS,
        ins=[cc1_in[:]], outs=[cc1_out[:]]), [stg_ok], P.sC)
    cc_ok = P.sC.at()
    if L1_PARTS == "cc1":
        S.flush()
        return cc_ok

    for t16 in range(NCH):
        slc = slice(128 * t16, 128 * (t16 + 1))
        bank = pf[2] if t16 % 2 == 0 else pf[3]
        prev = P.sV.at()
        for ic in range(8):
            S.op("T", lambda e, i=ic, s=slc, b=bank: e.matmul(
                b[:, 0:NE], P.xnb[:, i, s], P.l1rw[:, i, :],
                start=(i == 0), stop=False),
                [prev, stg_ok] if ic == 0 else [], P.sT)
        S.op("T", lambda e, b=bank: e.matmul(
            b[:, 0:NE], P.ones1[:], P.l1rb[:], start=False, stop=True), [], P.sT)
        mm_ok = P.sT.at()
        S.op("V", lambda e, t=t16, b=bank: e.tensor_copy(P.lg[:, t, :], b[:, 0:NE]),
             [mm_ok, stot_ok], P.sV)
    lg_ok = P.sV.at()
    mx = P.dden[:, 0:16]
    mxb = free_bcast(P.dden[:, 0:16], NE)
    r8b = free_bcast(P.dden[:, 16:32], NE)
    S.op("V", lambda e: e.tensor_reduce(mx, P.lg[:], AX.X, ALU.max), [lg_ok], P.sV)
    S.op("V", lambda e: e.drain(), [], None)
    S.op("V", lambda e: e.tensor_tensor(P.wk2[:], P.lg[:], mxb, ALU.subtract),
         [], P.sV)
    sub_ok = P.sV.at()
    S.op("A", lambda e: e.activation(P.lg.rearrange("p a b -> p (a b)"),
                                     P.wk2.rearrange("p a b -> p (a b)"), AF.Exp),
         [sub_ok], P.sA)
    exp_ok = P.sA.at()
    S.op("V", lambda e: e.tensor_copy(P.wk2.rearrange("p a b -> p (a b)"),
                                      P.lg.rearrange("p a b -> p (a b)")),
         [exp_ok], P.sV)
    S.op("V", lambda e: e.memset(P.sel.rearrange("p a b -> p (a b)"), 0.0), [], P.sV)
    for it in range(8):
        S.op("V", lambda e: e.tensor_reduce(mx, P.wk2[:], AX.X, ALU.max), [], P.sV)
        S.op("V", lambda e: e.drain(), [], None)
        S.op("V", lambda e: e.tensor_tensor(P.mtmp[:], P.wk2[:], mxb, ALU.is_ge),
             [], P.sV)
        S.op("V", lambda e: e.tensor_tensor(P.sel[:], P.sel[:], P.mtmp[:], ALU.max),
             [], P.sV)
        S.op("V", lambda e: e.scalar_tensor_tensor(
            P.wk2.rearrange("p a b -> p (a b)"), P.mtmp.rearrange("p a b -> p (a b)"),
            -1e30, P.wk2.rearrange("p a b -> p (a b)"), ALU.mult, ALU.add), [], P.sV)
    S.op("V", lambda e: e.tensor_tensor(P.wk2[:], P.lg[:], P.sel[:], ALU.mult),
         [], P.sV)
    S.op("V", lambda e: e.tensor_reduce(mx, P.wk2[:], AX.X, ALU.add), [], P.sV)
    S.op("V", lambda e: e.drain(), [], None)
    S.op("V", lambda e: e.reciprocal(P.dden[:, 16:32], mx), [], P.sV)
    S.op("V", lambda e: e.drain(), [], None)
    S.op("V", lambda e: e.tensor_tensor(P.mtmp[:], P.wk2[:], r8b, ALU.mult), [], P.sV)
    cw_ok = P.sV.at()
    for t16 in range(NCH):
        pbk = P.pbt if t16 % 2 == 0 else P.pbt2
        prev_ev = P.sV.at()
        S.op("T", lambda e, t=t16, b=pbk: e.transpose(
            b[0:64, 896:1024], P.mtmp[:, t, :], P.idn[:]), [cw_ok, prev_ev], P.sT)
        tr_ok = P.sT.at()
        S.op("V", lambda e, t=t16, b=pbk: e.tensor_copy(
            P.cwF[0:64, 128 * t:128 * (t + 1)], b[0:64, 896:1024]), [tr_ok], P.sV)
    cwF_ok = P.sV.at()
    if L1_PARTS == "rt":
        S.flush()
        return cwF_ok

    S.dma("S", P.gth1[:], cc1_out[0:NH * DH].rearrange("(hp p) n -> p hp n", hp=8),
          [cc_ok, cwF_ok])
    gth_ok = P.sX.at()
    S.op("V", lambda e: e.tensor_scalar(
        P.S_run[:], P.gth1[:],
        P.pmask[:, 0:1], None, ALU.mult), [gth_ok], P.sV)
    P.srun_w = P.sV.at()
    S.op("V", lambda e: e.tensor_copy(P.S_runb.rearrange("p a b -> p (a b)"),
                                      P.S_run.rearrange("p a b -> p (a b)")),
         [P.srun_w], P.sV)
    srunb_ok = P.sV.at()
    if L1_PARTS == "cc2":
        S.flush()
        return srunb_ok

    A2LVL = {"a2q": 1, "a2sm": 2, "a2sc": 2, "a2s": 2, "a2y": 3, "a2d": 4,
             "a2u": 5, "a2t": 6, "a2sm2": 2, "a2sm3": 2, "a2sm4": 2}.get(
                 L1_PARTS, 7)
    l1grp = {0: P.sV.at(), 1: P.sV.at()}
    l1am = P.sT.at()
    last_wo_mm = P.sT.at()
    for sub in range(8):
        ssl = slice(256 * sub, 256 * (sub + 1))
        S.dma("G", P.wbig[:], H["l1_wqT"].rearrange("c p n -> p c n"),
              [last_wo_mm, v_ok])
        wq_ld = P.sW.at()
        attf_ok = []
        for c2 in range(2):
            c = 2 * sub + c2
            slc = slice(128 * c, 128 * (c + 1))
            for oc in range(8):
                bank = pf[2] if oc % 2 == 0 else pf[3]
                prev = P.sV.at()
                prev_a = P.sA.at()
                for ic in range(8):
                    S.op("T", lambda e, i=ic, o=oc, s=slc, b=bank: e.matmul(
                        b[:, 0:128], P.wbig[:, i, 128 * o:128 * (o + 1)],
                        P.xnb[:, i, s], start=(i == 0), stop=(i == 7)),
                        [wq_ld, prev, prev_a, srunb_ok] if ic == 0 else [], P.sT)
                mm_ok = P.sT.at()
                elu1_evac(P, S, bank[:, 0:128], P.pqt[:, oc, :],
                          P.eluA[:, 0:128], P.eluB[:, 0:128], [mm_ok])
                for k in range(2):
                    S.op("V", lambda e, hp=oc, k=k: e.tensor_scalar(
                        P.pqtZ[:, 2 * hp + k, :], P.pqt[:, hp, :],
                        P.hmask[:, k:k + 1], None, ALU.mult),
                        [(P.sT, l1am[1])] if oc == 0 and k == 0 else [], P.sV)
            pq_ok = P.sV.at()
            if A2LVL < 2:
                continue
            for sg in range(4):
                bank = pf[0] if sg % 2 == 0 else pf[1]
                for j in range(4):
                    h = 4 * sg + j
                    hp = h // 2
                    S.op("T", lambda e, hp=hp, h=h, j=j, s=slc, b=bank:
                         e.matmul(b[:, 128 * j:128 * (j + 1)],
                                  P.pkF[:, hp, s],
                                  P.pqtZ[:, h, :],
                                  start=True, stop=True),
                         [pq_ok, l1grp[sg % 2]] if j == 0 else [], P.sT)
                g_mm = P.sT.at()
                if L1_PARTS in ("a2sm", "a2sm2", "a2sm3", "a2sm4"):
                    l1grp[sg % 2] = (P.sV, l1grp[sg % 2][1])
                    continue
                if L1_PARTS == "a2sc":
                    S.op("V", lambda e, b=bank: e.tensor_copy(
                        P.Am1[:], b[:]), [g_mm, l1am], P.sV)
                else:
                    S.op("V", lambda e, b=bank: e.tensor_tensor(
                        P.Am1[:], b[:], bcast_free(P.tri[:], 128, 128, 4, 128),
                        ALU.mult), [g_mm, l1am], P.sV)
                am_ok = P.sV.at()
                l1grp[sg % 2] = am_ok
                if A2LVL < 3:
                    continue
                for j in range(4):
                    h = 4 * sg + j
                    hp = h // 2
                    yb = pf[2 + h // 4]
                    col = (h % 4) * 65
                    S.op("T", lambda e, hp=hp, h=h, b=yb, col=col:
                         e.matmul(b[:, col:col + 65],
                                  P.pqtZ[:, h, :],
                                  P.S_runb[:, hp, :],
                                  start=True, stop=False),
                         [am_ok] if j == 0 else [], P.sT)
                    S.op("T", lambda e, j=j, h=h, c=c, b=yb, col=col: e.matmul(
                        b[:, col:col + 64], P.Am1[:, j, :], P.vta1[:, c, h, :],
                        start=False, stop=False), [], P.sT)
                    S.op("T", lambda e, j=j, b=yb, col=col: e.matmul(
                        b[:, col + 64:col + 65], P.Am1[:, j, :], P.onesc[:],
                        start=False, stop=True), [], P.sT)
                l1am = P.sT.at()
            y_mm = P.sT.at()
            if A2LVL < 4:
                continue
            for ybk in range(4):
                yb = pf[2 + ybk]
                ybv = yb[:, 0:260].rearrange("p (h n) -> p h n", h=4)
                S.op("V", lambda e, v=ybv, k=ybk: e.tensor_scalar(
                    P.dden[:, 4 * k:4 * k + 4], v[:, :, 64], 1e-5, None, ALU.add),
                    [y_mm], P.sV)
            dn_ok = P.sV.at()
            S.op("V", lambda e: e.reciprocal(P.dden[:, 16:32], P.dden[:, 0:16]),
                 [dn_ok], P.sV)
            S.op("V", lambda e: e.drain(), [], None)
            for ybk in range(4):
                yb = pf[2 + ybk]
                ybv = yb[:, 0:260].rearrange("p (h n) -> p h n", h=4)
                rb = free_bcast(P.dden[:, 16 + 4 * ybk:16 + 4 * ybk + 4], 64)
                S.op("V", lambda e, v=ybv, k=ybk, rb=rb: e.tensor_tensor(
                    P.att_tok1[:, 4 * k:4 * k + 4, :], v[:, :, 0:64], rb, ALU.mult),
                    [], P.sV)
            att_ok = P.sV.at()
            if A2LVL < 5:
                continue
            pbk = P.pbt if c % 2 == 0 else P.pbt2
            for hp in range(8):
                S.op("T", lambda e, hp=hp, s=slc, b=pbk: e.transpose(
                    b[:, 128 * hp:128 * (hp + 1)], P.pkF[:, hp, s], P.idn[:]),
                    [att_ok] if hp == 0 else [], P.sT)
            tr_ok = P.sT.at()
            S.op("V", lambda e, b=pbk: e.tensor_copy(
                P.pkTs[:], b[:, 0:1024].rearrange("p (a b) -> p a b", a=8)),
                [tr_ok], P.sV)
            ev_ok = P.sV.at()
            for h in range(NH):
                hp, hf = h // 2, h % 2
                bank = pf[0] if hp < 4 else pf[1]
                col = (hp % 4) * 65
                S.op("T", lambda e, hp=hp, hf=hf, c=c, b=bank, col=col: e.matmul(
                    b[64 * hf:64 * (hf + 1), col:col + 64],
                    P.pkTs[:, hp, 64 * hf:64 * (hf + 1)],
                    P.vta1[:, c, 2 * hp + hf, :],
                    start=True, stop=False), [ev_ok], P.sT)
                S.op("T", lambda e, hp=hp, hf=hf, b=bank, col=col,
                     last=True: e.matmul(
                    b[64 * hf:64 * (hf + 1), col + 64:col + 65],
                    P.pkTs[:, hp, 64 * hf:64 * (hf + 1)], P.onesc[:],
                    start=False, stop=last), [], P.sT)
            m_mm = P.sT.at()
            S.op("V", lambda e: e.tensor_tensor(
                P.S_run[:, 0:4, :].rearrange("p a b -> p (a b)"),
                P.S_run[:, 0:4, :].rearrange("p a b -> p (a b)"),
                pf[0][:, 0:260], ALU.add), [m_mm, P.srun_w], P.sV)
            S.op("V", lambda e: e.tensor_tensor(
                P.S_run[:, 4:8, :].rearrange("p a b -> p (a b)"),
                P.S_run[:, 4:8, :].rearrange("p a b -> p (a b)"),
                pf[1][:, 0:260], ALU.add), [], P.sV)
            P.srun_w = P.sV.at()
            S.op("V", lambda e: e.tensor_copy(
                P.S_runb.rearrange("p a b -> p (a b)"),
                P.S_run.rearrange("p a b -> p (a b)")), [P.srun_w], P.sV)
            srunb_ok = P.sV.at()
            if A2LVL < 6:
                continue
            pbk2 = P.pbt2 if c % 2 == 0 else P.pbt
            prev_ev2 = P.sV.at()
            for hp in range(8):
                S.op("T", lambda e, hp=hp, b=pbk2: e.transpose(
                    b[:, 128 * hp:128 * (hp + 1)],
                    P.att_tok1[:, 2 * hp:2 * hp + 2, :]
                    .rearrange("p a b -> p (a b)"), P.idn[:]),
                    [att_ok, prev_ev2] if hp == 0 else [], P.sT)
            atr_ok = P.sT.at()
            S.op("V", lambda e, b=pbk2, c2=c2: e.tensor_copy(
                P.attFt[:, :, 128 * c2:128 * (c2 + 1)],
                b[:, 0:1024].rearrange("p (a b) -> p a b", a=8)),
                [atr_ok], P.sV)
            attf_ok.append(P.sV.at())
        if A2LVL < 7:
            continue
        S.dma("G", P.wbig[:], H["l1_woT"].rearrange("c p n -> p c n"),
              [(P.sT, y_mm[1])])
        wo_ld = P.sW.at()
        for oc in range(8):
            bank = pf[4] if oc % 2 == 0 else pf[5]
            prev = P.sV.at()
            for ec in range(8):
                S.op("T", lambda e, ec=ec, o=oc, b=bank: e.matmul(
                    b[:, 0:256], P.wbig[:, ec, 128 * o:128 * (o + 1)],
                    P.attFt[:, ec, :], start=(ec == 0), stop=(ec == 7)),
                    [wo_ld, attf_ok[-1], prev] if ec == 0 else [], P.sT)
            mm_ok = P.sT.at()
            S.op("V", lambda e, o=oc, s=ssl, b=bank: e.scalar_tensor_tensor(
                P.xst[:, o, s], b[:, 0:256], P.l1bo[:, o:o + 1], P.xst[:, o, s],
                ALU.add, ALU.add), [mm_ok], P.sV)
        last_wo_mm = P.sT.at()
    att1_ok = P.sV.at()
    if L1_PARTS == "att":
        S.flush()
        return att1_ok

    S.dma("G", P.b2s[:], H["l1_b2"][:], [att1_ok])
    b2_ld = P.sW.at()
    slot_free = {0: P.sT.at(), 1: P.sT.at(), 2: P.sT.at()}
    unit_ld = {}

    def load_unit(u):
        slot = u % 3
        S.dma("G", P.u_w1[slot].rearrange("p e c n -> p (e c) n"),
              H["l1_w1"][u].rearrange("m p n -> p m n"),
              [slot_free[slot], att1_ok])
        S.dma("G", P.u_w2[slot].rearrange("p e c n -> p (e c) n"),
              H["l1_w2"][u].rearrange("m p n -> p m n"),
              [slot_free[slot]])
        unit_ld[u] = P.sW.at()

    if L1_PARTS != "nomoe":
        load_unit(0)
        load_unit(1)
        load_unit(2)
    cwb_free = {0: P.sV.at(), 1: P.sV.at()}
    P.partial_free = P.sV.at()
    for grp in ([] if L1_PARTS == "nomoe" else range(8)):
        if 2 * grp + 3 < 16 and grp > 0:
            load_unit(2 * grp + 2)
        for tt in range(8):
            tsl = slice(256 * tt, 256 * (tt + 1))
            if tt == 0 and grp == 0:
                P.mm1_bank_free = {0: P.sA.at(), 1: P.sA.at()}
                P.gsil_free = P.sV.at()
                P.gtil_free = P.sT.at()
            for ei in range(8):
                e_glob = 8 * grp + ei
                u = 2 * grp + ei // 4
                eu = ei % 4
                bank = pf[0] if ei % 2 == 0 else pf[1]
                prev_g = P.mm1_bank_free[ei % 2]
                for hc in range(2):
                    for ic in range(8):
                        S.op("T", lambda e, u=u % 3, eu=eu, i=ic, hc=hc, t=tsl,
                             b=bank: e.matmul(
                                 b[:, 256 * hc:256 * (hc + 1)],
                                 P.u_w1[u][:, eu, i, 128 * hc:128 * (hc + 1)],
                                 P.xnb[:, i, t], start=(i == 0), stop=(i == 7)),
                             [unit_ld[u], prev_g] if (hc == 0 and ic == 0) else [],
                             P.sT)
                mm1_ok = P.sT.at()
                for hc in range(2):
                    S.op("A", lambda e, hc=hc, eg=e_glob, b=bank: e.activation(
                        P.gsil[:, hc, :], b[:, 256 * hc:256 * (hc + 1)], AF.Silu,
                        bias=P.b1s[:, eg, hc:hc + 1]), [mm1_ok, P.gsil_free], P.sA)
                sil_ok = P.sA.at()
                P.mm1_bank_free[ei % 2] = sil_ok
                cwb = (P.pbt if ei % 2 == 0 else P.pbt2)[:, 0:512].bitcast(F32)
                if L1_PARTS != "nocwns":
                    S.op("T", lambda e, eg=e_glob, t=tsl, o=cwb: e.matmul(
                        o[:, 0:256],
                        free_bcast(P.idn[0:64, eg % 64:eg % 64 + 1], 128),
                        P.cwF[0:64, t], start=True, stop=True),
                        [cwF_ok, cwb_free[ei % 2]], P.sT)
                    cwb_mm = P.sT.at()
                for hc in range(2):
                    if L1_PARTS == "nocwns":
                        S.op("V", lambda e, hc=hc: e.tensor_copy(
                            P.gtil[:, hc, :], P.gsil[:, hc, :]),
                            [sil_ok, P.gtil_free], P.sV)
                    else:
                        S.op("V", lambda e, hc=hc, o=cwb: e.tensor_tensor(
                            P.gtil[:, hc, :], P.gsil[:, hc, :],
                            o[:, 0:256],
                            ALU.mult), [sil_ok, cwb_mm, P.gtil_free], P.sV)
                gt_ok = P.sV.at()
                P.gsil_free = gt_ok
                cwb_free[ei % 2] = gt_ok
                for oc in range(8):
                    pb_i, colh = divmod(oc, 2)
                    bank2 = pf[2 + pb_i]
                    for hc in range(2):
                        st = (ei == 0 and oc % 2 == 0 and hc == 0)
                        sp = (grp > 0 and ei == 7 and oc % 2 == 1 and hc == 1)
                        S.op("T", lambda e, u=u % 3, eu=eu, hc=hc, o=oc, b=bank2,
                             ch=colh, st=st, sp=sp: e.matmul(
                                 b[:, 256 * ch:256 * (ch + 1)],
                                 P.u_w2[u][:, eu, hc, 128 * o:128 * (o + 1)],
                                 P.gtil[:, hc, :], start=st, stop=sp),
                             [gt_ok, P.partial_free] if (oc == 0 and hc == 0)
                             else [], P.sT)
                P.gtil_free = P.sT.at()
            for oc in range(8):
                pb_i, colh = divmod(oc, 2)
                bank2 = pf[2 + pb_i]
                if grp == 0:
                    S.op("T", lambda e, o=oc, b=bank2, ch=colh, t=tsl,
                         last=(oc % 2 == 1): e.matmul(
                        b[:, 256 * ch:256 * (ch + 1)],
                        P.b2s[0:64, 128 * o:128 * (o + 1)], P.cwF[0:64, t],
                        start=False, stop=last), [cwF_ok, b2_ld], P.sT)
                else:
                    pass
            close_mm = P.sT.at()
            for oc in range(8):
                pb_i, colh = divmod(oc, 2)
                bank2 = pf[2 + pb_i]
                S.op("V", lambda e, o=oc, b=bank2, ch=colh, t=tsl: e.tensor_tensor(
                    P.xst[:, o, t], P.xst[:, o, t], b[:, 256 * ch:256 * (ch + 1)],
                    ALU.add), [close_mm], P.sV)
            P.partial_free = P.sV.at()
        slot_free[(2 * grp) % 3] = P.sT.at()
        slot_free[(2 * grp + 1) % 3] = P.sT.at()
        if 2 * grp + 3 < 16:
            load_unit(2 * grp + 3)
        elif grp == 0:
            load_unit(3)
    moe_ok = P.sV.at()
    if L1_PARTS in ("nosh", "nocwns"):
        S.flush()
        return moe_ok

    if L1_PARTS == "nomoe":
        moe_ok = att1_ok
    for j in range(2):
        S.dma("G", P.sw1j,
              H["l1_sw1"][j].rearrange("c p n -> p c n"),
              [moe_ok if j == 0 else P.sV.at()])
        S.dma("G", P.sw2j,
              H["l1_sw2"][j].rearrange("c p n -> p c n"), [])
        sw_ld = P.sW.at()
        for tt in range(8):
            tsl = slice(256 * tt, 256 * (tt + 1))
            if tt == 0:
                sh_bank_free = {0: P.sA.at(), 1: P.sA.at()}
                sh_gsil_free = {0: P.sT.at(), 1: P.sT.at()}
            for hc in range(16):
                bank = pf[0] if hc % 2 == 0 else pf[1]
                for ic in range(8):
                    S.op("T", lambda e, i=ic, hc=hc, t=tsl, b=bank: e.matmul(
                        b[:, 0:256], P.sw1j[:, i, 128 * hc:128 * (hc + 1)],
                        P.xnb[:, i, t], start=(i == 0), stop=(i == 7)),
                        [sw_ld, sh_bank_free[hc % 2]] if ic == 0 else [], P.sT)
                mm1_ok = P.sT.at()
                S.op("A", lambda e, hc=hc, j=j, b=bank: e.activation(
                    P.gsil[:, hc % 2, :], b[:, 0:256], AF.Silu,
                    bias=P.sb1s[:, j, hc:hc + 1]), [mm1_ok, sh_gsil_free[hc % 2]],
                     P.sA)
                sil_ok = P.sA.at()
                sh_bank_free[hc % 2] = sil_ok
                for oc in range(8):
                    pb_i, colh = divmod(oc, 2)
                    bank2 = pf[2 + pb_i]
                    S.op("T", lambda e, hc=hc, o=oc, b=bank2, ch=colh,
                         st=(hc == 0 and oc % 2 == 0),
                         sp=(hc == 15 and oc % 2 == 1): e.matmul(
                             b[:, 256 * ch:256 * (ch + 1)],
                             P.sw2j[:, hc, 128 * o:128 * (o + 1)],
                             P.gsil[:, hc % 2, :], start=st, stop=sp),
                         [(P.sA, sil_ok[1]), P.partial_free]
                         if (oc == 0 and hc == 0) else
                         ([(P.sA, sil_ok[1])] if oc == 0 else []), P.sT)
                sh_gsil_free[hc % 2] = P.sT.at()
            close_mm = P.sT.at()
            if L1_DEBUG_SHTAP and ((j == 0 and tt == 0) or (j == 1 and tt == 7)):
                dbase = 0 if j == 0 else 16
                for oc in range(8):
                    pb_i, colh = divmod(oc, 2)
                    bank2 = pf[2 + pb_i]
                    for half in range(2):
                        S.op("V", lambda e, o=oc, b=bank2, ch=colh, h2=half,
                             ba=dbase: e.tensor_copy(
                                 P.shdbg[:, ba + 2 * o + h2, :],
                                 b[:, 256 * ch + 128 * h2:256 * ch + 128 * (h2 + 1)]),
                             [close_mm], P.sV)
            for oc in range(8):
                pb_i, colh = divmod(oc, 2)
                bank2 = pf[2 + pb_i]
                if j == 1:
                    S.op("V", lambda e, o=oc, b=bank2, ch=colh, t=tsl:
                         e.scalar_tensor_tensor(
                             P.xst[:, o, t], b[:, 256 * ch:256 * (ch + 1)],
                             P.sb2sum[:, o:o + 1], P.xst[:, o, t], ALU.add,
                             ALU.add), [close_mm], P.sV)
                else:
                    S.op("V", lambda e, o=oc, b=bank2, ch=colh, t=tsl:
                         e.tensor_tensor(
                             P.xst[:, o, t], P.xst[:, o, t],
                             b[:, 256 * ch:256 * (ch + 1)], ALU.add),
                         [close_mm], P.sV)
            P.partial_free = P.sV.at()
    sh_ok = P.sV.at()
    S.flush()
    return sh_ok




def build_l1_moe_only(P, S, H, cw_dram, xn_ok, x_ld):
    pf = P.pfb
    for t16 in range(NCH):
        slc = slice(128 * t16, 128 * (t16 + 1))
        bank = pf[2] if t16 % 2 == 0 else pf[3]
        prev = P.sV.at()
        for ic in range(8):
            S.op("T", lambda e, i=ic, s=slc, b=bank: e.matmul(
                b[:, 0:NE], P.xnb[:, i, s], P.l1rw[:, i, :],
                start=(i == 0), stop=False),
                [prev, xn_ok, x_ld] if ic == 0 else [], P.sT)
        S.op("T", lambda e, b=bank: e.matmul(
            b[:, 0:NE], P.ones1[:], P.l1rb[:], start=False, stop=True), [], P.sT)
        mm_ok = P.sT.at()
        S.op("V", lambda e, t=t16, b=bank: e.tensor_copy(P.lg[:, t, :], b[:, 0:NE]),
             [mm_ok], P.sV)
    lg_ok = P.sV.at()
    mx = P.dden[:, 0:16]
    mxb = free_bcast(P.dden[:, 0:16], NE)
    r8b = free_bcast(P.dden[:, 16:32], NE)
    S.op("V", lambda e: e.tensor_reduce(mx, P.lg[:], AX.X, ALU.max), [lg_ok], P.sV)
    S.op("V", lambda e: e.drain(), [], None)
    S.op("V", lambda e: e.tensor_tensor(P.wk2[:], P.lg[:], mxb, ALU.subtract),
         [], P.sV)
    sub_ok = P.sV.at()
    S.op("A", lambda e: e.activation(P.lg.rearrange("p a b -> p (a b)"),
                                     P.wk2.rearrange("p a b -> p (a b)"), AF.Exp),
         [sub_ok], P.sA)
    exp_ok = P.sA.at()
    S.op("V", lambda e: e.tensor_copy(P.wk2.rearrange("p a b -> p (a b)"),
                                      P.lg.rearrange("p a b -> p (a b)")),
         [exp_ok], P.sV)
    S.op("V", lambda e: e.memset(P.sel.rearrange("p a b -> p (a b)"), 0.0), [], P.sV)
    for it in range(8):
        S.op("V", lambda e: e.tensor_reduce(mx, P.wk2[:], AX.X, ALU.max), [], P.sV)
        S.op("V", lambda e: e.drain(), [], None)
        S.op("V", lambda e: e.tensor_tensor(P.mtmp[:], P.wk2[:], mxb, ALU.is_ge),
             [], P.sV)
        S.op("V", lambda e: e.tensor_tensor(P.sel[:], P.sel[:], P.mtmp[:], ALU.max),
             [], P.sV)
        S.op("V", lambda e: e.scalar_tensor_tensor(
            P.wk2.rearrange("p a b -> p (a b)"), P.mtmp.rearrange("p a b -> p (a b)"),
            -1e30, P.wk2.rearrange("p a b -> p (a b)"), ALU.mult, ALU.add), [], P.sV)
    S.op("V", lambda e: e.tensor_tensor(P.wk2[:], P.lg[:], P.sel[:], ALU.mult),
         [], P.sV)
    S.op("V", lambda e: e.tensor_reduce(mx, P.wk2[:], AX.X, ALU.add), [], P.sV)
    S.op("V", lambda e: e.drain(), [], None)
    S.op("V", lambda e: e.reciprocal(P.dden[:, 16:32], mx), [], P.sV)
    S.op("V", lambda e: e.drain(), [], None)
    S.op("V", lambda e: e.tensor_tensor(P.mtmp[:], P.wk2[:], r8b, ALU.mult), [], P.sV)
    cw_ok = P.sV.at()
    for t16 in range(NCH):
        pbk = P.pbt if t16 % 2 == 0 else P.pbt2
        prev_ev = P.sV.at()
        S.op("T", lambda e, t=t16, b=pbk: e.transpose(
            b[0:64, 896:1024], P.mtmp[:, t, :], P.idn[:]), [cw_ok, prev_ev], P.sT)
        tr_ok = P.sT.at()
        S.op("V", lambda e, t=t16, b=pbk: e.tensor_copy(
            P.cwF[0:64, 128 * t:128 * (t + 1)], b[0:64, 896:1024]), [tr_ok], P.sV)
    cwF_ok = P.sV.at()
    S.dma("S", cw_dram[:], P.cwF[:], [cwF_ok])
    cwd_ok = P.sX.at()
    att1_ok = P.sV.at()
    P.moe_inline = (cwF_ok, cwd_ok, att1_ok)
    return None



def build_program(part="l0", taps=()):
    nc = bass.Bass("TRN2", target_bir_lowering=False)
    want_l0 = part in ("l0", "fused")
    want_l1 = part in ("l1", "fused")

    H = {}

    def hin(name, shape, dtype=F32):
        H[name] = nc.dram_tensor(name, list(shape), dtype, kind="ExternalInput").ap()
        return H[name]

    hin("x_in", (8, 128, NT))
    x_out = nc.dram_tensor("x_out", [8, 128, NT], F32, kind="ExternalOutput").ap()
    hin("c_idn", (128, 128))
    hin("c_tri", (128, 128))
    hin("c_pmask", (128, 1))
    hin("c_hmask", (128, 2))
    if want_l0:
        hin("c_nodemask", (64, 46))
        hin("c_projmask", (64, 9, 64))
        hin("l0_wq", (8, 128, DH))
        hin("l0_wk", (8, 128, DH))
        hin("l0_wv", (8, 128, DH))
        hin("l0_wo", (DH, D))
        for nmx in ("UL", "VR", "UR", "VL"):
            for l in range(4):
                hin(f"l0_{nmx}T{l}", (RL[l], BS[l]))
        hin("l0_WT", (8, 128, D))
        hin("l0_b", (128, 8))
        hin("l0_ldT", (100, 4))
        hin("l0_luT", (4, 50))
        hin("l0_da", (100, D))
        hin("l0_uaT", (50, D))
        cc0_in = nc.dram_tensor("cc0_in", [DH, DH + 1], F32, kind="Internal").ap()
        cc0_out = nc.dram_tensor("cc0_out", [2 * DH, DH + 1], F32, kind="Internal").ap()
    if want_l1:
        for w in ("wq", "wk", "wv", "wo"):
            hin(f"l1_{w}T", (8, 128, D))
        hin("l1_bo", (128, 8))
        hin("l1_rw", (8, 128, NE))
        hin("l1_rb", (1, NE))
        hin("l1_w1", (16, 32, 128, EH))
        hin("l1_w2", (16, 8, 128, D))
        hin("l1_b1", (128, NE, 2))
        hin("l1_b2", (NE, D))
        hin("l1_sw1", (2, 8, 128, SH))
        hin("l1_sw2", (2, 16, 128, D))
        hin("l1_sb1", (128, 2, 16))
        hin("l1_sb2", (128, 2, 8))
        cc1_in = nc.dram_tensor("cc1_in", [NH * DH, DH + 1], F32, kind="Internal").ap()
        cc1_out = nc.dram_tensor("cc1_out", [2 * NH * DH, DH + 1], F32,
                                 kind="Internal").ap()
        cw_dram = nc.dram_tensor("cw_dram", [NE, NT], BF16, kind="Internal").ap()

    tap_out = {t: nc.dram_tensor("tap_" + t, TAPS[t][0], TAPS[t][1],
                                 kind="ExternalOutput").ap() for t in taps}

    with ExitStack() as ctx:
        P = Prog(nc, ctx)
        S = Sched(P)

        P.sb("xst", [128, 8, NT], F32)
        P.sb("xnb", [128, 8, NT], BF16)
        P.sb("AR", [128, 55488], BF16)
        P.sb("idn", [128, 128], BF16)
        P.sb("tri", [128, 128], BF16)
        P.sb("pmask", [128, 1], F32)
        P.sb("hmask", [128, 2], F32)
        P.sb("ones1f", [1, 128], F32)
        P.sb("ones1", [1, 128], BF16)
        P.sb("onesc", [128, 1], BF16)
        P.sb("onesk", [128, 1], BF16)
        P.carve("rows4", [1, 3 * NT], F32, 43232)
        P.accm = P.rows4[0:1, 0 * NT:1 * NT]
        P.accv = P.rows4[0:1, 1 * NT:2 * NT]
        P.rrow = P.rows4[0:1, 2 * NT:3 * NT]
        P.carve("xbch", [128, NT], BF16, 67808)
        P.carve("xsq", [128, NT], BF16, 71904)
        P.carve("xsc32", [128, 512], F32, 76000)
        P.carve("etmp", [64, 1024], F32, 78048)
        P.carve("etmp2", [64, 1024], F32, 82144)
        P.pfb = [P.ps(f"pf{i}", [128, 512], F32) for i in range(6)]
        P.ps("pbt", [128, 1024], BF16)
        P.ps("pbt2", [128, 1024], BF16)

        if want_l0:
            P.carve("s_projmask", [64, 9, 64], BF16, 104832)
            P.carve("s_nodemask", [64, 46], F32, 105984)
            P.sb("l0b", [128, 8], F32)
            for nmx in ("UL", "VR", "UR", "VL"):
                for l in range(4):
                    P.sb(f"l0{nmx}T{l}", [RL[l], BS[l]], F32)
            P.carve("pqb", [64, NT], BF16, 0)
            P.carve("kb", [64, NT], BF16, 4096)
            P.carve("vb", [64, NT], BF16, 8192)
            P.carve("attF", [64, NT], BF16, 12288)
            P.carve("kTa", [128, NCH, 64], BF16, 16384)
            P.carve("vexta", [128, NCH, 65], BF16, 18432)
            P.carve("tfF", [64, 8, 128], BF16, 20544)
            P.carve("tfT", [128, 8, 64], BF16, 22592)
            P.carve("msk", [64, 8, 128], BF16, 23616)
            P.carve("Am", [128, 48, 128], BF16, 86272)
            P.carve("att_tok", [128, 64], BF16, 27712)
            P.carve("crun", [64, 65], F32, 27840)
            P.carve("crb", [64, 65], BF16, 28104)
            P.carve("cts", [64, 64], BF16, 28240)
            P.carve("el", [64, 4, 64], BF16, 28368)
            P.carve("elt", [64, 4, 64], BF16, 28880)
            P.carve("eleaf", [64, 64], BF16, 29392)
            P.carve("s1b", [64, 8, 64], BF16, 98560)
            P.carve("gext", [64, 65], BF16, 29776)
            P.carve("ctot", [64, 65], F32, 29908)
            P.carve("gth", [64, 2, 65], F32, 30172)
            P.carve("bstg", [32, 32], BF16, 30696)
            P.carve("Bmat", [64, 16, 64], BF16, 30760)
            P.carve("wct", [128, 8, D], BF16, 43232)
            P.carve("wtb", [128, 1, D], F32, 59616)
            P.carve("amat", [4, D], BF16, 63712)
            P.carve("bmat2", [50, D], BF16, 65760)
            P.carve("l0wq", [128, 8, DH], BF16, 32800)
            P.carve("l0wk", [128, 8, DH], BF16, 33824)
            P.carve("l0wv", [128, 8, DH], BF16, 34848)
            P.carve("l0wo", [DH, D], BF16, 35872)
            P.carve("l0da", [100, D], BF16, 37920)
            P.carve("l0uaT", [50, D], BF16, 39968)
            P.carve("l0ldT", [100, 4], BF16, 42016)
            P.carve("l0luT", [4, 50], BF16, 42080)
            P.tmpa = P.etmp[0:64, 0:512]
            P.tmpr = P.etmp[0:64, 512:1024]
            P.dcol = P.xsc32[:, 504:512]
        if want_l1:
            P.carve("l1rw", [128, 8, NE], BF16, 108928)
            P.carve("l1rb", [1, NE], BF16, 109952)
            P.sb("b1s", [128, NE, 2], F32)
            P.carve("b2s", [64, D], BF16, 106880)
            P.sb("sb1s", [128, 2, 16], F32)
            P.sb("sb2s", [128, 2, 8], F32)
            P.sb("sb2sum", [128, 8], F32)
            P.sb("l1bo", [128, 8], F32)

        for c8 in range(8):
            S.dma("S", P.xst[:, c8, :], H["x_in"][c8])
        S.dma("G", P.idn[:], H["c_idn"][:])
        S.dma("G", P.tri[:], H["c_tri"][:])
        S.dma("S", P.pmask[:], H["c_pmask"][:])
        S.dma("S", P.hmask[:], H["c_hmask"][:])
        if want_l0:
            S.dma("S", P.s_nodemask[:], H["c_nodemask"][:])
            S.dma("G", P.s_projmask[:], H["c_projmask"][:])
            S.dma("G", P.l0wq[:], H["l0_wq"].rearrange("c p n -> p c n"))
            S.dma("G", P.l0wk[:], H["l0_wk"].rearrange("c p n -> p c n"))
            S.dma("G", P.l0wv[:], H["l0_wv"].rearrange("c p n -> p c n"))
            S.dma("G", P.l0wo[:], H["l0_wo"][:])
            for nmx in ("UL", "VR", "UR", "VL"):
                for l in range(4):
                    S.dma("S", getattr(P, f"l0{nmx}T{l}")[:], H[f"l0_{nmx}T{l}"][:])
            S.dma("S", P.l0b[:], H["l0_b"][:])
            S.dma("G", P.l0ldT[:], H["l0_ldT"][:])
            S.dma("G", P.l0luT[:], H["l0_luT"][:])
            S.dma("G", P.l0da[:], H["l0_da"][:])
            S.dma("G", P.l0uaT[:], H["l0_uaT"][:])
        if want_l1:
            S.dma("G", P.l1rw[:], H["l1_rw"].rearrange("c p n -> p c n"))
            S.dma("G", P.l1rb[:], H["l1_rb"][:])
            S.dma("S", P.b1s[:], H["l1_b1"][:])
            S.dma("S", P.sb1s[:], H["l1_sb1"][:])
            S.dma("S", P.sb2s[:], H["l1_sb2"][:])
            S.dma("S", P.l1bo[:], H["l1_bo"][:])
        x_ld = P.sX.at()
        w_ld = P.sW.at()

        def prep_ones(e):
            e.memset(P.ones1[:], 1.0)
            e.memset(P.ones1f[:], 1.0)
            e.memset(P.onesc[:], 1.0)
            return e.memset(P.onesk[:], 1.0)

        S.op("V", prep_ones, [], P.sV)

        xn_ok = phase_ln(P, S, x_ld)
        S.flush()

        if want_l0:
            last = build_l0(P, S, H, cc0_in, cc0_out, xn_ok, w_ld, x_ld)
        else:
            last = xn_ok
        if want_l1:
            S.op("V", lambda e: e.tensor_tensor(P.sb2sum[:], P.sb2s[:, 0, :],
                                                P.sb2s[:, 1, :], ALU.add),
                 [x_ld], P.sV)
            if want_l0:
                xn1_ok = phase_ln(P, S, last)
                S.flush()
            else:
                xn1_ok = last
            last = build_l1(P, S, H, cc1_in, cc1_out, cw_dram, xn1_ok, x_ld)

        for c8 in range(8):
            S.dma("S", x_out[c8], P.xst[:, c8, :], [last], P.sO)
        for t in taps:
            S.dma("S", tap_out[t], getattr(P, t)[:], [last], P.sO)
        fin = P.sO.at()
        S.op("S", lambda e: e.nop(), [fin])
        S.flush()

    return nc



def prep_core_inputs(inputs, cid, part):
    t = lambda a: np.ascontiguousarray(np.asarray(a, np.float32).T)
    m = dict(host_consts())
    m["c_pmask"] = np.full((128, 1), 1.0 if cid % 2 else 0.0, np.float32)
    x = np.asarray(inputs["L0_attn_x"], np.float32).reshape(16384, D)
    shard = x[cid * NT:(cid + 1) * NT]
    m["x_in"] = np.ascontiguousarray(shard.T).reshape(8, 128, NT)
    if part in ("l0", "fused"):
        m["l0_wq"] = np.asarray(inputs["L0_attn_Wq"], np.float32).reshape(8, 128, DH)
        m["l0_wk"] = np.asarray(inputs["L0_attn_Wk"], np.float32).reshape(8, 128, DH)
        m["l0_wv"] = np.asarray(inputs["L0_attn_Wv"], np.float32).reshape(8, 128, DH)
        m["l0_wo"] = np.ascontiguousarray(np.asarray(inputs["L0_attn_Wo"], np.float32))
        for nmx in ("UL", "VR", "UR", "VL"):
            for l in range(4):
                m[f"l0_{nmx}T{l}"] = t(inputs[f"L0_attn_{nmx}{l}"])
        m["l0_WT"] = t(inputs["L0_ffn_W"]).reshape(8, 128, D)
        m["l0_b"] = np.ascontiguousarray(
            np.asarray(inputs["L0_ffn_b"], np.float32).reshape(8, 128).T)
        m["l0_ldT"] = t(inputs["L0_ffn_lora_down"])
        m["l0_luT"] = t(inputs["L0_ffn_lora_up"])
        m["l0_da"] = np.ascontiguousarray(np.asarray(inputs["L0_ffn_down_aux"], np.float32))
        m["l0_uaT"] = t(inputs["L0_ffn_up_aux"])
    if part in ("l1", "fused"):
        for w in ("Wq", "Wk", "Wv", "Wo"):
            m[f"l1_{w.lower()}T"] = t(inputs[f"L1_attn_{w}"]).reshape(8, 128, D)
        m["l1_bo"] = np.ascontiguousarray(
            np.asarray(inputs["L1_attn_bo"], np.float32).reshape(8, 128).T)
        m["l1_rw"] = np.asarray(inputs["L1_ffn_router_w"], np.float32).reshape(8, 128, NE)
        m["l1_rb"] = np.asarray(inputs["L1_ffn_router_b"], np.float32).reshape(1, NE)
        m["l1_w1"] = np.asarray(inputs["L1_ffn_w1"], np.float32).reshape(16, 32, 128, EH)
        m["l1_w2"] = np.asarray(inputs["L1_ffn_w2"], np.float32).reshape(16, 8, 128, D)
        m["l1_b1"] = np.ascontiguousarray(
            np.asarray(inputs["L1_ffn_b1"], np.float32).reshape(NE, 2, 128)
            .transpose(2, 0, 1))
        m["l1_b2"] = np.asarray(inputs["L1_ffn_b2"], np.float32)
        m["l1_sw1"] = np.asarray(inputs["L1_ffn_sw1"], np.float32).reshape(2, 8, 128, SH)
        m["l1_sw2"] = np.asarray(inputs["L1_ffn_sw2"], np.float32).reshape(2, 16, 128, D)
        m["l1_sb1"] = np.ascontiguousarray(
            np.asarray(inputs["L1_ffn_sb1"], np.float32).reshape(2, 16, 128)
            .transpose(2, 0, 1))
        m["l1_sb2"] = np.ascontiguousarray(
            np.asarray(inputs["L1_ffn_sb2"], np.float32).reshape(2, 8, 128)
            .transpose(2, 0, 1))
    return m


def run_part(inputs, part, taps=(), x_override=None):
    nc = build_program(part, taps)
    in_maps = [prep_core_inputs(inputs, cid, part) for cid in range(8)]
    if x_override is not None:
        for cid in range(8):
            in_maps[cid]["x_in"] = np.ascontiguousarray(x_override[cid])
    return run_bass_kernel_spmd(nc, in_maps, core_ids=list(range(8)))


def kernel(**inputs):
    res = run_part(inputs, "fused")
    shards = [r["x_out"].reshape(D, NT).T for r in res.results]
    return np.concatenate(shards, axis=0).reshape(4, 4096, D)


# version 03
# speedup vs baseline: 1.0227x; 1.0084x over previous
import numpy as np
from contextlib import ExitStack

import concourse.bass as bass
import concourse.mybir as mybir
from concourse.bass import AP
from concourse.bass_utils import run_bass_kernel_spmd

F32 = mybir.dt.float32
BF16 = mybir.dt.bfloat16
AF = mybir.ActivationFunctionType
ALU = mybir.AluOpType
AX = mybir.AxisListType

NT = 2048
NCH = 16
D = 1024
DH = 64
NH = 16
NE = 64
EH = 256
SH = 2048
BS = [4, 8, 16, 32]
NN = [8, 4, 2, 1]
RL = [4, 8, 8, 8]
PAIRS = [[0, 1], [2, 3], [4, 5], [6, 7]]
L1_PARTS = "all"
L1_DEBUG_SHTAP = False

TAPS = {
    "xnb": ([128, 8, NT], BF16),
    "pqb": ([64, NT], BF16),
    "kb": ([64, NT], BF16),
    "vb": ([64, NT], BF16),
    "ctot": ([64, 65], F32),
    "crun": ([64, 65], F32),
    "attF": ([64, NT], BF16),
    "wct": ([128, 8, D], BF16),
    "gext": ([64, 65], BF16),
    "Bmat": ([64, 16, 64], BF16),
    "cwF": ([64, NT], BF16),
    "stot": ([128, 8, 65], F32),
    "pkF": ([128, 8, NT], BF16),
    "vta1": ([128, NCH, NH, 64], BF16),
    "S_run": ([128, 8, 65], F32),
    "attFt": ([128, 8, 256], BF16),
    "gsil": ([128, 2, 256], BF16),
    "gtil": ([128, 2, 256], BF16),
    "cwbS": ([128, 2, 256], BF16),
    "sb2sum": ([128, 8], F32),
    "sw1j": ([128, 8, SH], BF16),
    "sw2j": ([128, 16, D], BF16),
    "shdbg": ([128, 32, 128], F32),
}


class Sem:
    def __init__(self, ctx, nc, name):
        self.h = ctx.enter_context(nc.semaphore(name))
        self.v = 0
        self.name = name

    def at(self):
        return (self, self.v)


class Sched:
    ENGS = ("T", "V", "A", "G", "S")

    def __init__(self, P):
        self.P = P
        self.q = {e: [] for e in self.ENGS}

    def op(self, eng, emit, waits=(), inc=None, inc_n=1):
        self.q[eng].append((emit, [(s, v) for (s, v) in waits if v > 0], inc, inc_n))
        if inc is not None:
            inc.v += inc_n

    def dma(self, eng, out, in_, waits=(), sem=None):
        sem = sem if sem is not None else (self.P.sW if eng == "G" else self.P.sX)
        self.op(eng, lambda e, o=out, i=in_: e.dma_start(out=o, in_=i), waits, sem, 16)

    def flush(self):
        P = self.P
        secs = {"T": P.blk.tensor, "V": P.blk.vector, "A": P.blk.scalar,
                "G": P.blk.gpsimd, "S": P.blk.sync}
        for eng in self.ENGS:
            if not self.q[eng]:
                continue
            ops = self.q[eng]

            def body(e, ops=ops):
                waited = {}
                for emit, waits, inc, inc_n in ops:
                    for s, v in waits:
                        if waited.get(s.name, 0) < v:
                            e.wait_ge(s.h, v)
                            waited[s.name] = v
                    ins = emit(e)
                    if inc is not None:
                        ins.then_inc(inc.h, inc_n)

            secs[eng](body)
            self.q[eng] = []


class Prog:
    def __init__(self, nc, ctx):
        self.nc = nc
        self.ctx = ctx
        self.blk = ctx.enter_context(nc.Block())
        for n in ("sT", "sV", "sA", "sG", "sW", "sX", "sC", "sO"):
            setattr(self, n, Sem(ctx, nc, n))

    def sb(self, name, shape, dtype):
        t = self.ctx.enter_context(self.nc.sbuf_tensor(name, shape, dtype))
        setattr(self, name, t)
        return t

    def ps(self, name, shape, dtype):
        t = self.ctx.enter_context(self.nc.psum_tensor(name, shape, dtype))
        setattr(self, name, t)
        return t

    def carve(self, name, shape, dtype, off_bytes):
        import math
        elems = 1
        for d in shape[1:]:
            elems *= d
        nbytes = elems * (4 if dtype == F32 else 2)
        assert off_bytes % 4 == 0
        v = self.AR[0:shape[0], off_bytes // 2:(off_bytes + nbytes) // 2]
        if dtype == F32:
            v = v.bitcast(F32)
        if len(shape) == 3:
            v = v.rearrange("p (a b) -> p a b", a=shape[1])
        elif len(shape) == 4:
            v = v.rearrange("p (a b c) -> p a b c", a=shape[1], b=shape[2])
        setattr(self, name, v)
        return v


def bcast_free(t_ap, pdim, pcount, reps, inner):
    return AP(t_ap.tensor, t_ap.offset, [[pdim, pcount], [0, reps], [1, inner]])


def free_bcast(ap2d, reps):
    a = [list(x) for x in ap2d.ap]
    assert len(a) == 2
    return AP(ap2d.tensor, ap2d.offset, [a[0], a[1], [0, reps]])



def host_consts():
    c = {}
    c["c_idn"] = np.eye(128, dtype=np.float32)
    c["c_tri"] = np.triu(np.ones((128, 128), dtype=np.float32))
    hm = np.zeros((128, 2), np.float32)
    hm[0:64, 0] = 1.0
    hm[64:128, 1] = 1.0
    c["c_hmask"] = hm
    nm = np.zeros((64, 46), np.float32)
    col = 0
    for n in range(16):
        nm[4 * n:4 * n + 4, col] = 1.0
        col += 1
    for l in range(3):
        bs = BS[l]
        for n in range(NN[l]):
            nm[n * 2 * bs + bs:(n + 1) * 2 * bs, col] = 1.0
            col += 1
    for l in range(3):
        bs = BS[l]
        for n in range(NN[l]):
            nm[n * 2 * bs:n * 2 * bs + bs, col] = 1.0
            col += 1
    assert col == 44
    c["c_nodemask"] = nm
    pm = np.zeros((64, 9, 64), np.float32)
    for n in range(16):
        pm[4 * n:4 * n + 4, 0, 4 * n:4 * n + 4] = 1.0
    for l in range(4):
        bs = BS[l]
        for n in range(NN[l]):
            r0 = n * 2 * bs
            pm[r0:r0 + bs, 1 + l, r0 + bs:r0 + 2 * bs] = 1.0
            pm[r0 + bs:r0 + 2 * bs, 5 + l, r0:r0 + bs] = 1.0
    c["c_projmask"] = pm
    return c



def phase_ln(P, S, x_ready):
    pf = P.pfb
    for c8 in range(8):
        prev_mm = P.sT.at()
        S.op("V", lambda e, c=c8: e.tensor_copy(P.xbch[:], P.xst[:, c, :]),
             [x_ready, prev_mm], P.sV)
        xb_ok = P.sV.at()
        S.op("V", lambda e: e.tensor_tensor(P.xsq[:], P.xbch[:], P.xbch[:], ALU.mult),
             [], P.sV)
        sq_ok = P.sV.at()
        for q4 in range(4):
            b1, b2 = pf[q4 % 2], pf[2 + q4 % 2]
            prev_acc = P.sV.at()
            S.op("T", lambda e, q=q4, b=b1: e.matmul(
                b[0:1, :], P.onesk[:], P.xbch[:, 512 * q:512 * (q + 1)],
                start=True, stop=True), [xb_ok, prev_acc], P.sT)
            mm1 = P.sT.at()
            S.op("T", lambda e, q=q4, b=b2: e.matmul(
                b[0:1, :], P.onesk[:], P.xsq[:, 512 * q:512 * (q + 1)],
                start=True, stop=True), [sq_ok], P.sT)
            mm2 = P.sT.at()
            if c8 == 0:
                S.op("V", lambda e, q=q4, b=b1: e.tensor_copy(
                    P.accm[0:1, 512 * q:512 * (q + 1)], b[0:1, :]), [mm1], P.sV)
                S.op("V", lambda e, q=q4, b=b2: e.tensor_copy(
                    P.accv[0:1, 512 * q:512 * (q + 1)], b[0:1, :]), [mm2], P.sV)
            else:
                S.op("V", lambda e, q=q4, b=b1: e.tensor_tensor(
                    P.accm[0:1, 512 * q:512 * (q + 1)], P.accm[0:1, 512 * q:512 * (q + 1)],
                    b[0:1, :], ALU.add), [mm1], P.sV)
                S.op("V", lambda e, q=q4, b=b2: e.tensor_tensor(
                    P.accv[0:1, 512 * q:512 * (q + 1)], P.accv[0:1, 512 * q:512 * (q + 1)],
                    b[0:1, :], ALU.add), [mm2], P.sV)
    S.op("V", lambda e: e.tensor_scalar(P.accm, P.accm, 1.0 / D, None, ALU.mult),
         [], P.sV)
    S.op("V", lambda e: e.tensor_scalar(P.accv, P.accv, 1.0 / D, None, ALU.mult),
         [], P.sV)
    S.op("V", lambda e: e.tensor_tensor(P.rrow, P.accm, P.accm, ALU.mult), [], P.sV)
    S.op("V", lambda e: e.tensor_tensor(P.accv, P.accv, P.rrow, ALU.subtract),
         [], P.sV)
    S.op("V", lambda e: e.tensor_scalar(P.accv, P.accv, 1e-5, None, ALU.add),
         [], P.sV)
    var_ok = P.sV.at()
    S.op("A", lambda e: e.activation(P.accv, P.accv, AF.Ln), [var_ok], P.sA)
    S.op("A", lambda e: e.activation(P.rrow, P.accv, AF.Exp, scale=-0.5), [], P.sA)
    rstd_ok = P.sA.at()
    for q4 in range(4):
        prev_v = P.sV.at()
        S.op("T", lambda e, q=q4: e.matmul(
            pf[4][:], P.ones1f[:], P.accm[0:1, 512 * q:512 * (q + 1)],
            start=True, stop=True), [rstd_ok, prev_v], P.sT)
        S.op("T", lambda e, q=q4: e.matmul(
            pf[5][:], P.ones1f[:], P.rrow[0:1, 512 * q:512 * (q + 1)],
            start=True, stop=True), [], P.sT)
        bc_mm = P.sT.at()
        for c8 in range(8):
            sl = slice(512 * q4, 512 * (q4 + 1))
            S.op("V", lambda e, c=c8, s=sl: e.tensor_tensor(
                P.xsc32[:], P.xst[:, c, s], pf[4][:], ALU.subtract),
                [bc_mm], P.sV)
            S.op("V", lambda e, c=c8, s=sl: e.tensor_tensor(
                P.xnb[:, c, s], P.xsc32[:], pf[5][:], ALU.mult), [], P.sV)
    return P.sV.at()


ELU_MODE = "copyfirst"


def elu1_evac(P, S, src_psum, dst, tmp_a, tmp_r, waits):
    if ELU_MODE == "copyfirst":
        S.op("V", lambda e: e.tensor_copy(tmp_r, src_psum), waits, P.sV)
        c_ok = P.sV.at()
        S.op("A", lambda e: e.activation(tmp_a, tmp_r, AF.Exp), [c_ok], P.sA)
        a_ok = P.sA.at()
        S.op("V", lambda e: e.tensor_scalar(tmp_r, tmp_r, 0.0, None, ALU.max),
             [a_ok], P.sV)
        S.op("V", lambda e: e.scalar_tensor_tensor(dst, tmp_a, 1.0, tmp_r,
                                                   ALU.min, ALU.add), [], P.sV)
        return P.sV.at()
    if ELU_MODE == "split64":
        np_ = src_psum.shape[0]
        hp = np_ // 2
        for ph in range(2):
            ps = slice(hp * ph, hp * (ph + 1))
            S.op("A", lambda e, ps=ps: e.activation(tmp_a[ps], src_psum[ps],
                                                    AF.Exp), waits, P.sA)
            a_ok = P.sA.at()
            S.op("V", lambda e, ps=ps: e.tensor_scalar(
                tmp_r[ps], src_psum[ps], 0.0, None, ALU.max),
                list(waits) + [a_ok], P.sV)
        S.op("V", lambda e: e.scalar_tensor_tensor(dst, tmp_a, 1.0, tmp_r,
                                                   ALU.min, ALU.add), [], P.sV)
        return P.sV.at()
    S.op("A", lambda e: e.activation(tmp_a, src_psum, AF.Exp), waits, P.sA)
    a_ok = P.sA.at()
    S.op("V", lambda e: e.tensor_scalar(tmp_r, src_psum, 0.0, None, ALU.max),
         list(waits) + [a_ok], P.sV)
    S.op("V", lambda e: e.scalar_tensor_tensor(dst, tmp_a, 1.0, tmp_r, ALU.min, ALU.add),
         [], P.sV)
    return P.sV.at()



def build_l0(P, S, H, cc0_in, cc0_out, xn_ok, w_ld, x_ld):
    pf = P.pfb

    for tt in range(4):
        sl = slice(512 * tt, 512 * (tt + 1))
        prev_ev = P.sV.at()
        prev_ac = P.sA.at()
        for wi, (wname, prow, bank) in enumerate(
                (("l0wq", 0, pf[0]), ("l0wk", 64, pf[0]), ("l0wv", 0, pf[1]))):
            for ic in range(8):
                S.op("T", lambda e, w=wname, p=prow, b=bank, i=ic, s=sl: e.matmul(
                    b[p:p + 64, :], getattr(P, w)[:, i, :], P.xnb[:, i, s],
                    start=(i == 0), stop=(i == 7)),
                    [xn_ok, w_ld, prev_ev, prev_ac] if ic == 0 else [], P.sT)
        mm_ok = P.sT.at()
        elu1_evac(P, S, pf[0][0:64, :], P.pqb[0:64, sl], P.tmpa,
                  P.tmpr, [mm_ok])
        elu1_evac(P, S, pf[0][64:128, :], P.kb[0:64, sl], P.etmp2[0:64, 0:512],
                  P.etmp2[0:64, 512:1024], [mm_ok])
        S.op("V", lambda e, s=sl: e.tensor_copy(P.vb[0:64, s], pf[1][0:64, :]),
             [mm_ok], P.sV)
    qkv_ok = P.sV.at()

    S.op("V", lambda e: e.memset(P.vexta[:, :, 64], 1.0), [], P.sV)
    st_done = None
    for c in range(NCH):
        sl = slice(128 * c, 128 * (c + 1))
        pbk = P.pbt if c % 2 == 0 else P.pbt2
        prev_ev = P.sV.at()
        S.op("T", lambda e, s=sl, b=pbk: e.transpose(
            b[:, 0:64], P.kb[0:64, s], P.idn[0:64, 0:64]), [qkv_ok, prev_ev], P.sT)
        S.op("T", lambda e, s=sl, b=pbk: e.transpose(
            b[:, 64:128], P.vb[0:64, s], P.idn[0:64, 0:64]), [], P.sT)
        tr_ok = P.sT.at()
        S.op("V", lambda e, c=c, b=pbk: e.tensor_copy(P.kTa[:, c, :], b[:, 0:64]),
             [tr_ok], P.sV)
        S.op("V", lambda e, c=c, b=pbk: e.tensor_copy(P.vexta[:, c, 0:64], b[:, 64:128]),
             [], P.sV)
        ev_ok = P.sV.at()
        S.op("T", lambda e, c=c: e.matmul(
            pf[2][0:64, 0:65], P.kTa[:, c, :], P.vexta[:, c, :],
            start=(c == 0), stop=(c == NCH - 1)), [ev_ok], P.sT)
        st_done = P.sT.at()
    S.op("V", lambda e: e.tensor_copy(P.ctot[:], pf[2][0:64, 0:65]), [st_done], P.sV)
    ctot_ok = P.sV.at()
    S.dma("S", cc0_in[:], P.ctot[:], [ctot_ok])
    stg_ok = P.sX.at()
    S.op("G", lambda e: e.collective_compute(
        "AllGather", ALU.bypass, replica_groups=PAIRS,
        ins=[cc0_in[:]], outs=[cc0_out[:]]), [stg_ok], P.sC)
    cc_ok = P.sC.at()
    if L1_PARTS == "cc1":
        S.flush()
        return cc_ok

    S.op("V", lambda e: e.memset(P.Bmat[:], 0.0), [], P.sV)
    bm_z = P.sV.at()
    for l in range(4):
        for (wa, wb2, idx) in ((f"l0ULT{l}", f"l0ULT{l}", 0),
                               (f"l0VRT{l}", f"l0VRT{l}", 1),
                               (f"l0ULT{l}", f"l0VLT{l}", 2),
                               (f"l0VRT{l}", f"l0URT{l}", 3)):
            prev = P.sV.at()
            S.op("T", lambda e, a=wa, b=wb2, l=l: e.matmul(
                pf[3][0:BS[l], 0:BS[l]], getattr(P, a)[:], getattr(P, b)[:],
                start=True, stop=True), [x_ld, prev], P.sT)
            mm_ok = P.sT.at()
            S.op("V", lambda e, l=l: e.tensor_copy(
                P.bstg[0:BS[l], 0:BS[l]], pf[3][0:BS[l], 0:BS[l]]), [mm_ok], P.sV)
            ev_ok = P.sV.at()
            bs = BS[l]
            for n in range(NN[l]):
                r0 = n * 2 * bs + (bs if idx in (1, 3) else 0)
                S.dma("S", P.Bmat[r0:r0 + bs, 4 * l + idx, r0:r0 + bs],
                      P.bstg[0:bs, 0:bs], [ev_ok, bm_z])
    bm_ok = P.sX.at()

    prev = P.sV.at()
    S.op("T", lambda e: e.matmul(pf[3][0:4, 0:512], P.l0ldT[:], P.l0da[:, 0:512],
                                 start=True, stop=True), [x_ld, w_ld, prev], P.sT)
    S.op("T", lambda e: e.matmul(pf[4][0:4, 0:512], P.l0ldT[:], P.l0da[:, 512:1024],
                                 start=True, stop=True), [], P.sT)
    amm = P.sT.at()
    S.op("V", lambda e: e.tensor_copy(P.amat[0:4, 0:512], pf[3][0:4, 0:512]), [amm], P.sV)
    S.op("V", lambda e: e.tensor_copy(P.amat[0:4, 512:1024], pf[4][0:4, 0:512]),
         [], P.sV)
    a_ok = P.sV.at()
    S.op("T", lambda e: e.matmul(pf[3][0:50, 0:512], P.l0luT[:], P.amat[0:4, 0:512],
                                 start=True, stop=True), [a_ok], P.sT)
    S.op("T", lambda e: e.matmul(pf[4][0:50, 0:512], P.l0luT[:], P.amat[0:4, 512:1024],
                                 start=True, stop=True), [], P.sT)
    bmm = P.sT.at()
    S.op("V", lambda e: e.tensor_copy(P.bmat2[0:50, 0:512], pf[3][0:50, 0:512]),
         [bmm], P.sV)
    S.op("V", lambda e: e.tensor_copy(P.bmat2[0:50, 512:1024], pf[4][0:50, 0:512]),
         [], P.sV)
    b_ok = P.sV.at()
    for ic in range(8):
        S.dma("S", P.wtb[:, 0, :], H["l0_WT"][ic], [P.sV.at()])
        wt_ld = P.sX.at()
        prev = P.sV.at()
        for half in range(2):
            S.op("T", lambda e, i=ic, h=half: e.matmul(
                pf[3 + h][:, :], P.bmat2[0:50, 128 * i:128 * (i + 1)],
                P.l0uaT[:, 512 * h:512 * (h + 1)], start=True, stop=True),
                [b_ok, prev], P.sT)
        t3_ok = P.sT.at()
        for half in range(2):
            S.op("V", lambda e, i=ic, h=half: e.tensor_tensor(
                P.wct[:, i, 512 * h:512 * (h + 1)],
                P.wtb[:, 0, 512 * h:512 * (h + 1)],
                pf[3 + h][:, :], ALU.add), [t3_ok, wt_ld], P.sV)
    wct_ok = P.sV.at()
    for oc in range(8):
        for tt in range(4):
            sl = slice(512 * tt, 512 * (tt + 1))
            bank = pf[5] if (oc * 4 + tt) % 2 == 0 else pf[2]
            prev = P.sV.at()
            for ic in range(8):
                S.op("T", lambda e, i=ic, o=oc, s=sl, b=bank: e.matmul(
                    b[:], P.wct[:, i, 128 * o:128 * (o + 1)], P.xnb[:, i, s],
                    start=(i == 0), stop=(i == 7)),
                    [wct_ok, prev] if ic == 0 else [], P.sT)
            mm_ok = P.sT.at()
            S.op("V", lambda e, o=oc, s=sl, b=bank: e.scalar_tensor_tensor(
                P.xst[:, o, s], b[:], P.l0b[:, o:o + 1], P.xst[:, o, s],
                ALU.add, ALU.add), [mm_ok], P.sV)
    ffn_ok = P.sV.at()

    S.dma("S", P.gth[:], cc0_out.rearrange("(r p) n -> p r n", r=2), [cc_ok])
    gth_ok = P.sX.at()
    S.op("V", lambda e: e.tensor_scalar(P.crun[:], P.gth[:, 0, :], P.pmask[0:64, 0:1],
                                        None, ALU.mult), [gth_ok], P.sV)
    P.crun_w = P.sV.at()

    build_l0_attn2(P, S, bm_ok, qkv_ok)
    att_ok = P.sV.at()

    for oc in range(8):
        for tt in range(4):
            sl = slice(512 * tt, 512 * (tt + 1))
            bank = pf[0] if (oc * 4 + tt) % 2 == 0 else pf[1]
            prev = P.sV.at()
            S.op("T", lambda e, o=oc, s=sl, b=bank: e.matmul(
                b[:], P.l0wo[0:64, 128 * o:128 * (o + 1)], P.attF[0:64, s],
                start=True, stop=True), [att_ok, ffn_ok, prev], P.sT)
            mm_ok = P.sT.at()
            S.op("V", lambda e, o=oc, s=sl, b=bank: e.tensor_tensor(
                P.xst[:, o, s], P.xst[:, o, s], b[:], ALU.add), [mm_ok], P.sV)
    S.flush()
    return P.sV.at()


def build_l0_attn2(P, S, bm_ok, qkv_ok):
    pf = P.pfb
    comps = []
    for n in range(16):
        comps.append(("leaf", None, n, (4 * n, 4 * n + 4)))
    ci = 16
    for l in range(3):
        bs = BS[l]
        for n in range(NN[l]):
            comps.append(("tr", l, ci, (n * 2 * bs, n * 2 * bs + bs)))
            ci += 1
    comps.append(("tr", 3, None, (0, 32)))
    for l in range(3):
        bs = BS[l]
        for n in range(NN[l]):
            comps.append(("bl", l, ci, (n * 2 * bs + bs, n * 2 * bs + 2 * bs)))
            ci += 1
    comps.append(("bl", 3, None, (32, 64)))
    comps.append(("den", None, None, (64, 65)))
    assert len(comps) == 47 and ci == 44

    last_y = P.sV.at()
    last_grp_ev = {0: P.sV.at(), 1: P.sV.at()}
    P.mskslot = {s: (P.sT, 0) for s in range(8)}
    P.amslot = {0: (P.sT, 0), 1: (P.sT, 0)}

    for c in range(NCH):
        sl = slice(128 * c, 128 * (c + 1))
        prev_tf = P.sV.at()
        for l in range(4):
            S.op("T", lambda e, l=l, s=sl: e.matmul(
                pf[2][0:64, 128 * l:128 * (l + 1)], P.Bmat[:, 4 * l + 1, :],
                P.vb[0:64, s], start=True, stop=True),
                [qkv_ok, bm_ok, prev_tf], P.sT)
        tfv_mm = P.sT.at()
        S.op("V", lambda e: e.tensor_copy(
            P.tfF[:, 0:4, :], pf[2][0:64, 0:512].rearrange("p (l n) -> p l n", l=4)),
            [tfv_mm], P.sV)
        tfv_ok = P.sV.at()
        for l in range(4):
            S.op("T", lambda e, l=l, s=sl: e.matmul(
                pf[2][0:64, 128 * l:128 * (l + 1)], P.Bmat[:, 4 * l + 2, :],
                P.kb[0:64, s], start=True, stop=True), [tfv_ok], P.sT)
        tfk_mm = P.sT.at()
        S.op("V", lambda e: e.tensor_copy(
            P.tfF[:, 4:8, :], pf[2][0:64, 0:512].rearrange("p (l n) -> p l n", l=4)),
            [tfk_mm], P.sV)
        tfF_ok = P.sV.at()
        for l in range(4):
            S.op("T", lambda e, l=l, s=sl: e.matmul(
                pf[3][:, 64 * l:64 * (l + 1)], P.kb[0:64, s], P.Bmat[:, 4 * l + 0, :],
                start=True, stop=True), [tfF_ok], P.sT)
            S.op("T", lambda e, l=l, s=sl: e.matmul(
                pf[3][:, 256 + 64 * l:256 + 64 * (l + 1)], P.vb[0:64, s],
                P.Bmat[:, 4 * l + 3, :], start=True, stop=True), [], P.sT)
        tft_mm = P.sT.at()
        S.op("V", lambda e: e.tensor_copy(
            P.tfT[:], pf[3][:, 0:512].rearrange("p (l n) -> p l n", l=8)),
            [tft_mm], P.sV)
        tfT_ok = P.sV.at()
        S.op("V", lambda e: e.tensor_copy(P.crb[:], P.crun[:]), [P.crun_w], P.sV)
        crb_ok = P.sV.at()
        pbk = P.pbt if c % 2 == 0 else P.pbt2
        S.op("T", lambda e, b=pbk: e.transpose(b[0:64, 896:960], P.crb[0:64, 0:64],
                                               P.idn[0:64, 0:64]), [crb_ok], P.sT)
        ctr_mm = P.sT.at()
        S.op("V", lambda e, b=pbk: e.tensor_copy(P.cts[:], b[0:64, 896:960]),
             [ctr_mm], P.sV)
        cts_ok = P.sV.at()
        for l in range(4):
            S.op("V", lambda e, l=l: e.tensor_tensor(
                P.el[:, l, :], P.crb[:, 0:64], P.s_projmask[:, 1 + l, :], ALU.mult),
                [crb_ok], P.sV)
            S.op("V", lambda e, l=l: e.tensor_tensor(
                P.elt[:, l, :], P.cts[:], P.s_projmask[:, 5 + l, :], ALU.mult),
                [cts_ok], P.sV)
        S.op("V", lambda e: e.tensor_tensor(P.eleaf[:], P.cts[:], P.s_projmask[:, 0, :],
                                            ALU.mult), [cts_ok], P.sV)
        e_ok = P.sV.at()
        prev2 = P.sV.at()
        for l in range(4):
            S.op("T", lambda e, l=l: e.matmul(
                pf[2][0:64, 64 * l:64 * (l + 1)], P.el[:, l, :], P.Bmat[:, 4 * l + 0, :],
                start=True, stop=True), [e_ok, prev2], P.sT)
            S.op("T", lambda e, l=l: e.matmul(
                pf[3][0:64, 64 * l:64 * (l + 1)], P.elt[:, l, :], P.Bmat[:, 4 * l + 3, :],
                start=True, stop=True), [e_ok], P.sT)
        stg_mm = P.sT.at()
        S.op("V", lambda e: e.tensor_copy(P.s1b[:, 0:4, :], pf[2][0:64, 0:256]
                                          .rearrange("p (l n) -> p l n", l=4)),
             [stg_mm], P.sV)
        S.op("V", lambda e: e.tensor_copy(P.s1b[:, 4:8, :], pf[3][0:64, 0:256]
                                          .rearrange("p (l n) -> p l n", l=4)),
             [stg_mm], P.sV)
        s1_ok = P.sV.at()
        for l in range(4):
            S.op("T", lambda e, l=l: e.matmul(
                pf[4][0:64, 0:64], P.Bmat[:, 4 * l + 1, :], P.s1b[:, l, :],
                start=(l == 0), stop=False), [s1_ok], P.sT)
            S.op("T", lambda e, l=l: e.matmul(
                pf[4][0:64, 0:64], P.Bmat[:, 4 * l + 2, :], P.s1b[:, 4 + l, :],
                start=False, stop=(l == 3)), [], P.sT)
        gt_mm = P.sT.at()
        S.op("V", lambda e: e.tensor_tensor(P.gext[:, 0:64], pf[4][0:64, 0:64],
                                            P.eleaf[:], ALU.add), [gt_mm], P.sV)
        S.op("V", lambda e: e.tensor_copy(P.gext[:, 64:65], P.crun[:, 64:65]),
             [P.crun_w], P.sV)
        gext_ok = P.sV.at()

        S.op("T", lambda e, c=c: e.matmul(
            pf[4][0:64, 64:129], P.kTa[:, c, :], P.vexta[:, c, :],
            start=True, stop=True), [gext_ok], P.sT)
        st_mm = P.sT.at()
        S.op("V", lambda e: e.tensor_tensor(P.crun[:], P.crun[:], pf[4][0:64, 64:129],
                                            ALU.add), [st_mm, P.crun_w], P.sV)
        P.crun_w = P.sV.at()

        for g in range(12):
            bank = pf[g % 2]
            n_in_g = min(4, 47 - 4 * g)
            for j in range(n_in_g):
                i = 4 * g + j
                kind, l, mcol, cols = comps[i]
                if mcol is None:
                    continue
                if kind == "leaf":
                    srcf = lambda e, s=sl: P.vb[0:64, s]
                elif kind == "tr":
                    srcf = lambda e, l=l: P.tfF[:, l, :]
                else:
                    srcf = lambda e, l=l: P.tfF[:, 4 + l, :]
                S.op("V", lambda e, f=srcf, m=mcol, i=i: e.tensor_scalar(
                    P.msk[:, i % 8, :], f(e), P.s_nodemask[:, m:m + 1], None,
                    ALU.mult), [tfF_ok, P.mskslot[i % 8]], P.sV)
            mcp_ok = P.sV.at()
            for j in range(n_in_g):
                i = 4 * g + j
                kind, l, mcol, cols = comps[i]
                if mcol is not None:
                    lhsT_f = lambda e, i=i: P.msk[:, i % 8, :]
                elif kind == "tr":
                    lhsT_f = lambda e: P.tfF[:, 3, :]
                elif kind == "bl":
                    lhsT_f = lambda e: P.tfF[:, 7, :]
                else:
                    lhsT_f = lambda e, s=sl: P.kb[0:64, s]
                S.op("T", lambda e, f=lhsT_f, b=bank, j=j, s=sl: e.matmul(
                    b[:, 128 * j:128 * (j + 1)], f(e), P.pqb[0:64, s],
                    start=True, stop=True),
                    [mcp_ok, last_grp_ev[g % 2]] if j == 0 else [], P.sT)
            g_mm = P.sT.at()
            for j in range(n_in_g):
                P.mskslot[(4 * g + j) % 8] = g_mm
            S.op("V", lambda e, b=bank, n=n_in_g, g=g: e.tensor_tensor(
                P.Am[:, 4 * g:4 * g + n, :], b[:, 0:128 * n],
                bcast_free(P.tri[:], 128, 128, n, 128), ALU.mult),
                [g_mm], P.sV)
            last_grp_ev[g % 2] = P.sV.at()
        sc_ok = P.sV.at()

        S.op("T", lambda e, s=sl: e.matmul(pf[5][:, 0:65], P.pqb[0:64, s], P.gext[:],
                                           start=True, stop=False),
             [gext_ok, sc_ok, tfT_ok, last_y], P.sT)
        for i, (kind, l, mcol, cols) in enumerate(comps):
            if kind == "leaf":
                rhs_f = lambda e, c=c, co=cols: P.kTa[:, c, co[0]:co[1]]
            elif kind == "tr":
                rhs_f = lambda e, l=l, co=cols: P.tfT[:, l, co[0]:co[1]]
            elif kind == "bl":
                rhs_f = lambda e, l=l, co=cols: P.tfT[:, 4 + l, co[0]:co[1]]
            else:
                rhs_f = lambda e: P.onesc[:]
            S.op("T", lambda e, r=rhs_f, co=cols, i=i: e.matmul(
                pf[5][:, co[0]:co[1]], P.Am[:, i, :], r(e), start=False,
                stop=(i == 46)), [], P.sT)
        y_mm = P.sT.at()

        S.op("V", lambda e: e.tensor_scalar(P.dcol[:, 0:1], pf[5][:, 64:65], 1e-6,
                                            None, ALU.max), [y_mm], P.sV)
        d_ok = P.sV.at()
        S.op("V", lambda e: e.reciprocal(P.dcol[:, 1:2], P.dcol[:, 0:1]), [d_ok], P.sV)
        r_ok = P.sV.at()
        S.op("V", lambda e: e.tensor_scalar(P.att_tok[:], pf[5][:, 0:64],
                                            P.dcol[:, 1:2], None, ALU.mult),
             [r_ok], P.sV)
        atok_ok = P.sV.at()
        last_y = atok_ok
        S.op("T", lambda e, b=pbk: e.transpose(b[0:64, 768:896], P.att_tok[:],
                                               P.idn[:]), [atok_ok], P.sT)
        atr_mm = P.sT.at()
        S.op("V", lambda e, s=sl, b=pbk: e.tensor_copy(P.attF[0:64, s],
                                                       b[0:64, 768:896]),
             [atr_mm], P.sV)





def l1_carves(P):
    P.carve("pkF", [128, 8, NT], BF16, 0)
    P.carve("vta1", [128, NCH, NH, 64], BF16, 32768)
    P.carve("wbig", [128, 8, D], BF16, 65536)
    P.carve("S_run", [128, 8, 65], F32, 81920)
    P.carve("S_runb", [128, 8, 65], BF16, 84032)
    P.carve("stot", [128, 8, 65], F32, 85088)
    P.carve("wk2", [128, NCH, NE], F32, 85088)
    P.carve("lg", [128, NCH, NE], F32, 89184)
    P.carve("sel", [128, NCH, NE], BF16, 93280)
    P.carve("mtmp", [128, NCH, NE], BF16, 95328)
    P.carve("gth1", [128, 8, 65], F32, 85088)
    P.carve("pqt", [128, 8, 128], BF16, 89344)
    P.carve("attFt", [128, 8, 256], BF16, 91392)
    P.carve("att_tok1", [128, NH, 64], BF16, 95488)
    P.carve("pkTs", [128, 8, 128], BF16, 97536)
    P.carve("Am1", [128, 4, 128], BF16, 99584)
    P.carve("cwF", [64, NT], BF16, 100608)
    P.carve("dden", [128, 32], F32, 104704)
    P.carve("eluA", [128, 256], F32, 85088)
    P.carve("eluB", [128, 256], F32, 86112)
    P.carve("eluC", [128, 256], F32, 87136)
    P.carve("eluD", [128, 256], F32, 88160)
    P.u_w1, P.u_w2 = [], []
    for u in range(3):
        P.u_w1.append(P.carve(f"u{u}w1", [128, 4, 8, EH], BF16, 32768 * u))
        P.u_w2.append(P.carve(f"u{u}w2", [128, 4, 2, D], BF16, 32768 * u + 16384))
    P.carve("gsil", [128, 2, 256], BF16, 104832)
    P.carve("gtil", [128, 2, 256], BF16, 105856)
    P.carve("cwbS", [128, 2, 256], BF16, 98304)
    P.carve("pqtZ", [128, 16, 128], BF16, 106880)
    P.carve("sw1j", [128, 8, SH], BF16, 0)
    P.carve("sw2j", [128, 16, D], BF16, 32768)
    if L1_DEBUG_SHTAP:
        P.carve("shdbg", [128, 32, 128], F32, 65536)
    P.tmpa1 = P.eluA[:]
    P.tmpr1 = P.eluB[:]


def build_l1(P, S, H, cc1_in, cc1_out, cw_dram, xn_ok, x_ld,
             wq_bf=None, wo_bf=None):
    pf = P.pfb
    l1_carves(P)
    if L1_PARTS == "ln":
        S.flush()
        return xn_ok

    S.dma("G", P.wbig[:], H["l1_wkT"].rearrange("c p n -> p c n"), [xn_ok])
    wk_ld = P.sW.at()
    if L1_PARTS == "p1w":
        S.flush()
        return wk_ld
    for oc in range(8):
        for tt in range(4):
            sl = slice(512 * tt, 512 * (tt + 1))
            bank = pf[0] if (oc * 4 + tt) % 2 == 0 else pf[1]
            prev = P.sV.at()
            prev_a = P.sA.at()
            for ic in range(8):
                S.op("T", lambda e, i=ic, o=oc, s=sl, b=bank: e.matmul(
                    b[:], P.wbig[:, i, 128 * o:128 * (o + 1)], P.xnb[:, i, s],
                    start=(i == 0), stop=(i == 7)),
                    [xn_ok, wk_ld, prev, prev_a] if ic == 0 else [], P.sT)
            mm_ok = P.sT.at()
            if L1_PARTS == "p1k":
                continue
            for half in range(2):
                hs = slice(512 * tt + 256 * half, 512 * tt + 256 * (half + 1))
                if L1_PARTS in ("p1x", "p1y", "p1z"):
                    ta = P.eluA[:] if half == 0 else P.eluC[:]
                    tr = P.eluB[:] if half == 0 else P.eluD[:]
                    S.op("A", lambda e, b=bank, h=half, ta=ta: e.activation(
                        ta, b[:, 256 * h:256 * (h + 1)], AF.Exp), [mm_ok], P.sA)
                    a_ok = P.sA.at()
                    if L1_PARTS == "p1y":
                        S.op("V", lambda e, b=bank, h=half, tr=tr: e.tensor_scalar(
                            tr, b[:, 256 * h:256 * (h + 1)], 0.0, None, ALU.max),
                            [mm_ok, a_ok], P.sV)
                    elif L1_PARTS == "p1z":
                        S.op("V", lambda e, o=oc, hs=hs, ta=ta: e.scalar_tensor_tensor(
                            P.pkF[:, o, hs], ta, 1.0, ta, ALU.min, ALU.add),
                            [a_ok], P.sV)
                    continue
                elu1_evac(P, S, bank[:, 256 * half:256 * (half + 1)],
                          P.pkF[:, oc, hs],
                          P.eluA[:] if half == 0 else P.eluC[:],
                          P.eluB[:] if half == 0 else P.eluD[:], [mm_ok])
    if L1_PARTS == "p1k":
        S.flush()
        return P.sT.at()
    if L1_PARTS == "p1x":
        S.flush()
        return P.sA.at()
    if L1_PARTS in ("p1y", "p1z"):
        S.flush()
        return P.sV.at()
    pk_ok = P.sV.at()
    if L1_PARTS == "p1a":
        S.flush()
        return pk_ok
    S.dma("G", P.wbig[:], H["l1_wvT"].rearrange("c p n -> p c n"), [pk_ok])
    wv_ld = P.sW.at()
    S.dma("G", wq_bf, H["l1_wqT"][:], [])
    S.dma("G", wo_bf, H["l1_woT"][:], [])
    pc_ok = P.sW.at()
    for c in range(NCH):
        slc = slice(128 * c, 128 * (c + 1))
        for half in range(2):
            bank = pf[0] if (c * 2 + half) % 2 == 0 else pf[1]
            prev = P.sV.at()
            for ic in range(8):
                S.op("T", lambda e, i=ic, h=half, s=slc, b=bank: e.matmul(
                    b[:], P.xnb[:, i, s], P.wbig[:, i, 512 * h:512 * (h + 1)],
                    start=(i == 0), stop=(i == 7)),
                    [wv_ld, prev] if ic == 0 else [], P.sT)
            mm_ok = P.sT.at()
            S.op("V", lambda e, c=c, h=half, b=bank: e.tensor_copy(
                P.vta1[:, c, 8 * h:8 * (h + 1), :].rearrange("p h n -> p (h n)"),
                b[:]), [mm_ok], P.sV)
    v_ok = P.sV.at()
    if L1_PARTS == "p1b":
        S.flush()
        return v_ok
    S.op("V", lambda e: e.memset(P.stot.rearrange("p a b -> p (a b)"), 0.0),
         [], P.sV)
    for c in range(NCH):
        slc = slice(128 * c, 128 * (c + 1))
        pbk = P.pbt if c % 2 == 0 else P.pbt2
        prev_ev = P.sV.at()
        for hp in range(8):
            S.op("T", lambda e, hp=hp, s=slc, b=pbk: e.transpose(
                b[:, 128 * hp:128 * (hp + 1)], P.pkF[:, hp, s], P.idn[:]),
                [pk_ok, v_ok, prev_ev] if hp == 0 else [], P.sT)
        tr_ok = P.sT.at()
        S.op("V", lambda e, b=pbk: e.tensor_copy(
            P.pkTs[:], b[:, 0:1024].rearrange("p (a b) -> p a b", a=8)),
            [tr_ok], P.sV)
        ev_ok = P.sV.at()
        for h in range(NH):
            hp, half = h // 2, h % 2
            bank = pf[4] if hp < 4 else pf[5]
            col = (hp % 4) * 65
            S.op("T", lambda e, hp=hp, hf=half, c=c, b=bank, col=col: e.matmul(
                b[64 * hf:64 * (hf + 1), col:col + 64],
                P.pkTs[:, hp, 64 * hf:64 * (hf + 1)], P.vta1[:, c, 2 * hp + hf, :],
                start=True, stop=False), [ev_ok], P.sT)
            S.op("T", lambda e, hp=hp, hf=half, b=bank, col=col: e.matmul(
                b[64 * hf:64 * (hf + 1), col + 64:col + 65],
                P.pkTs[:, hp, 64 * hf:64 * (hf + 1)], P.onesc[:],
                start=False, stop=True), [], P.sT)
        st_mm = P.sT.at()
        S.op("V", lambda e: e.tensor_tensor(
            P.stot[:, 0:4, :].rearrange("p a b -> p (a b)"),
            P.stot[:, 0:4, :].rearrange("p a b -> p (a b)"), pf[4][:, 0:260],
            ALU.add), [st_mm], P.sV)
        S.op("V", lambda e: e.tensor_tensor(
            P.stot[:, 4:8, :].rearrange("p a b -> p (a b)"),
            P.stot[:, 4:8, :].rearrange("p a b -> p (a b)"), pf[5][:, 0:260],
            ALU.add), [], P.sV)
    stot_ok = P.sV.at()
    if L1_PARTS == "p1c":
        S.flush()
        return stot_ok
    S.dma("S", cc1_in.rearrange("(hp p) n -> p hp n", hp=8), P.stot[:], [stot_ok])
    stg_ok = P.sX.at()
    if L1_PARTS == "p1":
        S.flush()
        return stg_ok
    S.op("G", lambda e: e.collective_compute(
        "AllGather", ALU.bypass, replica_groups=PAIRS,
        ins=[cc1_in[:]], outs=[cc1_out[:]]), [stg_ok], P.sC)
    cc_ok = P.sC.at()
    if L1_PARTS == "cc1":
        S.flush()
        return cc_ok

    for t16 in range(NCH):
        slc = slice(128 * t16, 128 * (t16 + 1))
        bank = pf[2] if t16 % 2 == 0 else pf[3]
        prev = P.sV.at()
        for ic in range(8):
            S.op("T", lambda e, i=ic, s=slc, b=bank: e.matmul(
                b[:, 0:NE], P.xnb[:, i, s], P.l1rw[:, i, :],
                start=(i == 0), stop=False),
                [prev, stg_ok] if ic == 0 else [], P.sT)
        S.op("T", lambda e, b=bank: e.matmul(
            b[:, 0:NE], P.ones1[:], P.l1rb[:], start=False, stop=True), [], P.sT)
        mm_ok = P.sT.at()
        S.op("V", lambda e, t=t16, b=bank: e.tensor_copy(P.lg[:, t, :], b[:, 0:NE]),
             [mm_ok, stot_ok], P.sV)
    lg_ok = P.sV.at()
    mx = P.dden[:, 0:16]
    mxb = free_bcast(P.dden[:, 0:16], NE)
    r8b = free_bcast(P.dden[:, 16:32], NE)
    S.op("V", lambda e: e.tensor_reduce(mx, P.lg[:], AX.X, ALU.max), [lg_ok], P.sV)
    S.op("V", lambda e: e.drain(), [], None)
    S.op("V", lambda e: e.tensor_tensor(P.wk2[:], P.lg[:], mxb, ALU.subtract),
         [], P.sV)
    sub_ok = P.sV.at()
    S.op("A", lambda e: e.activation(P.lg.rearrange("p a b -> p (a b)"),
                                     P.wk2.rearrange("p a b -> p (a b)"), AF.Exp),
         [sub_ok], P.sA)
    exp_ok = P.sA.at()
    S.op("V", lambda e: e.tensor_copy(P.wk2.rearrange("p a b -> p (a b)"),
                                      P.lg.rearrange("p a b -> p (a b)")),
         [exp_ok], P.sV)
    S.op("V", lambda e: e.memset(P.sel.rearrange("p a b -> p (a b)"), 0.0), [], P.sV)
    for it in range(8):
        S.op("V", lambda e: e.tensor_reduce(mx, P.wk2[:], AX.X, ALU.max), [], P.sV)
        S.op("V", lambda e: e.drain(), [], None)
        S.op("V", lambda e: e.tensor_tensor(P.mtmp[:], P.wk2[:], mxb, ALU.is_ge),
             [], P.sV)
        S.op("V", lambda e: e.tensor_tensor(P.sel[:], P.sel[:], P.mtmp[:], ALU.max),
             [], P.sV)
        S.op("V", lambda e: e.scalar_tensor_tensor(
            P.wk2.rearrange("p a b -> p (a b)"), P.mtmp.rearrange("p a b -> p (a b)"),
            -1e30, P.wk2.rearrange("p a b -> p (a b)"), ALU.mult, ALU.add), [], P.sV)
    S.op("V", lambda e: e.tensor_tensor(P.wk2[:], P.lg[:], P.sel[:], ALU.mult),
         [], P.sV)
    S.op("V", lambda e: e.tensor_reduce(mx, P.wk2[:], AX.X, ALU.add), [], P.sV)
    S.op("V", lambda e: e.drain(), [], None)
    S.op("V", lambda e: e.reciprocal(P.dden[:, 16:32], mx), [], P.sV)
    S.op("V", lambda e: e.drain(), [], None)
    S.op("V", lambda e: e.tensor_tensor(P.mtmp[:], P.wk2[:], r8b, ALU.mult), [], P.sV)
    cw_ok = P.sV.at()
    for t16 in range(NCH):
        pbk = P.pbt if t16 % 2 == 0 else P.pbt2
        prev_ev = P.sV.at()
        S.op("T", lambda e, t=t16, b=pbk: e.transpose(
            b[0:64, 896:1024], P.mtmp[:, t, :], P.idn[:]), [cw_ok, prev_ev], P.sT)
        tr_ok = P.sT.at()
        S.op("V", lambda e, t=t16, b=pbk: e.tensor_copy(
            P.cwF[0:64, 128 * t:128 * (t + 1)], b[0:64, 896:1024]), [tr_ok], P.sV)
    cwF_ok = P.sV.at()
    if L1_PARTS == "rt":
        S.flush()
        return cwF_ok

    S.dma("S", P.gth1[:], cc1_out[0:NH * DH].rearrange("(hp p) n -> p hp n", hp=8),
          [cc_ok, cwF_ok])
    gth_ok = P.sX.at()
    S.op("V", lambda e: e.tensor_scalar(
        P.S_run[:], P.gth1[:],
        P.pmask[:, 0:1], None, ALU.mult), [gth_ok], P.sV)
    P.srun_w = P.sV.at()
    S.op("V", lambda e: e.tensor_copy(P.S_runb.rearrange("p a b -> p (a b)"),
                                      P.S_run.rearrange("p a b -> p (a b)")),
         [P.srun_w], P.sV)
    srunb_ok = P.sV.at()
    if L1_PARTS == "cc2":
        S.flush()
        return srunb_ok

    A2LVL = {"a2q": 1, "a2sm": 2, "a2sc": 2, "a2s": 2, "a2y": 3, "a2d": 4,
             "a2u": 5, "a2t": 6, "a2sm2": 2, "a2sm3": 2, "a2sm4": 2}.get(
                 L1_PARTS, 7)
    l1grp = {0: P.sV.at(), 1: P.sV.at()}
    l1am = P.sT.at()
    last_wo_mm = P.sT.at()
    for sub in range(8):
        ssl = slice(256 * sub, 256 * (sub + 1))
        S.dma("S", P.wbig[:], wq_bf.rearrange("c p n -> p c n"),
              [last_wo_mm, v_ok] + ([pc_ok] if sub == 0 else []))
        wq_ld = P.sX.at()
        attf_ok = []
        for c2 in range(2):
            c = 2 * sub + c2
            slc = slice(128 * c, 128 * (c + 1))
            for oc in range(8):
                bank = pf[2] if oc % 2 == 0 else pf[3]
                prev = P.sV.at()
                prev_a = P.sA.at()
                for ic in range(8):
                    S.op("T", lambda e, i=ic, o=oc, s=slc, b=bank: e.matmul(
                        b[:, 0:128], P.wbig[:, i, 128 * o:128 * (o + 1)],
                        P.xnb[:, i, s], start=(i == 0), stop=(i == 7)),
                        [wq_ld, prev, prev_a, srunb_ok] if ic == 0 else [], P.sT)
                mm_ok = P.sT.at()
                elu1_evac(P, S, bank[:, 0:128], P.pqt[:, oc, :],
                          P.eluA[:, 0:128], P.eluB[:, 0:128], [mm_ok])
                for k in range(2):
                    S.op("V", lambda e, hp=oc, k=k: e.tensor_scalar(
                        P.pqtZ[:, 2 * hp + k, :], P.pqt[:, hp, :],
                        P.hmask[:, k:k + 1], None, ALU.mult),
                        [(P.sT, l1am[1])] if oc == 0 and k == 0 else [], P.sV)
            pq_ok = P.sV.at()
            if A2LVL < 2:
                continue
            for sg in range(4):
                bank = pf[0] if sg % 2 == 0 else pf[1]
                for j in range(4):
                    h = 4 * sg + j
                    hp = h // 2
                    S.op("T", lambda e, hp=hp, h=h, j=j, s=slc, b=bank:
                         e.matmul(b[:, 128 * j:128 * (j + 1)],
                                  P.pkF[:, hp, s],
                                  P.pqtZ[:, h, :],
                                  start=True, stop=True),
                         [pq_ok, l1grp[sg % 2]] if j == 0 else [], P.sT)
                g_mm = P.sT.at()
                if L1_PARTS in ("a2sm", "a2sm2", "a2sm3", "a2sm4"):
                    l1grp[sg % 2] = (P.sV, l1grp[sg % 2][1])
                    continue
                if L1_PARTS == "a2sc":
                    S.op("V", lambda e, b=bank: e.tensor_copy(
                        P.Am1[:], b[:]), [g_mm, l1am], P.sV)
                else:
                    S.op("V", lambda e, b=bank: e.tensor_tensor(
                        P.Am1[:], b[:], bcast_free(P.tri[:], 128, 128, 4, 128),
                        ALU.mult), [g_mm, l1am], P.sV)
                am_ok = P.sV.at()
                l1grp[sg % 2] = am_ok
                if A2LVL < 3:
                    continue
                for j in range(4):
                    h = 4 * sg + j
                    hp = h // 2
                    yb = pf[2 + h // 4]
                    col = (h % 4) * 65
                    S.op("T", lambda e, hp=hp, h=h, b=yb, col=col:
                         e.matmul(b[:, col:col + 65],
                                  P.pqtZ[:, h, :],
                                  P.S_runb[:, hp, :],
                                  start=True, stop=False),
                         [am_ok] if j == 0 else [], P.sT)
                    S.op("T", lambda e, j=j, h=h, c=c, b=yb, col=col: e.matmul(
                        b[:, col:col + 64], P.Am1[:, j, :], P.vta1[:, c, h, :],
                        start=False, stop=False), [], P.sT)
                    S.op("T", lambda e, j=j, b=yb, col=col: e.matmul(
                        b[:, col + 64:col + 65], P.Am1[:, j, :], P.onesc[:],
                        start=False, stop=True), [], P.sT)
                l1am = P.sT.at()
            y_mm = P.sT.at()
            if A2LVL < 4:
                continue
            for ybk in range(4):
                yb = pf[2 + ybk]
                ybv = yb[:, 0:260].rearrange("p (h n) -> p h n", h=4)
                S.op("V", lambda e, v=ybv, k=ybk: e.tensor_scalar(
                    P.dden[:, 4 * k:4 * k + 4], v[:, :, 64], 1e-5, None, ALU.add),
                    [y_mm], P.sV)
            dn_ok = P.sV.at()
            S.op("V", lambda e: e.reciprocal(P.dden[:, 16:32], P.dden[:, 0:16]),
                 [dn_ok], P.sV)
            S.op("V", lambda e: e.drain(), [], None)
            for ybk in range(4):
                yb = pf[2 + ybk]
                ybv = yb[:, 0:260].rearrange("p (h n) -> p h n", h=4)
                rb = free_bcast(P.dden[:, 16 + 4 * ybk:16 + 4 * ybk + 4], 64)
                S.op("V", lambda e, v=ybv, k=ybk, rb=rb: e.tensor_tensor(
                    P.att_tok1[:, 4 * k:4 * k + 4, :], v[:, :, 0:64], rb, ALU.mult),
                    [], P.sV)
            att_ok = P.sV.at()
            if A2LVL < 5:
                continue
            pbk = P.pbt if c % 2 == 0 else P.pbt2
            for hp in range(8):
                S.op("T", lambda e, hp=hp, s=slc, b=pbk: e.transpose(
                    b[:, 128 * hp:128 * (hp + 1)], P.pkF[:, hp, s], P.idn[:]),
                    [att_ok] if hp == 0 else [], P.sT)
            tr_ok = P.sT.at()
            S.op("V", lambda e, b=pbk: e.tensor_copy(
                P.pkTs[:], b[:, 0:1024].rearrange("p (a b) -> p a b", a=8)),
                [tr_ok], P.sV)
            ev_ok = P.sV.at()
            for h in range(NH):
                hp, hf = h // 2, h % 2
                bank = pf[0] if hp < 4 else pf[1]
                col = (hp % 4) * 65
                S.op("T", lambda e, hp=hp, hf=hf, c=c, b=bank, col=col: e.matmul(
                    b[64 * hf:64 * (hf + 1), col:col + 64],
                    P.pkTs[:, hp, 64 * hf:64 * (hf + 1)],
                    P.vta1[:, c, 2 * hp + hf, :],
                    start=True, stop=False), [ev_ok], P.sT)
                S.op("T", lambda e, hp=hp, hf=hf, b=bank, col=col,
                     last=True: e.matmul(
                    b[64 * hf:64 * (hf + 1), col + 64:col + 65],
                    P.pkTs[:, hp, 64 * hf:64 * (hf + 1)], P.onesc[:],
                    start=False, stop=last), [], P.sT)
            m_mm = P.sT.at()
            S.op("V", lambda e: e.tensor_tensor(
                P.S_run[:, 0:4, :].rearrange("p a b -> p (a b)"),
                P.S_run[:, 0:4, :].rearrange("p a b -> p (a b)"),
                pf[0][:, 0:260], ALU.add), [m_mm, P.srun_w], P.sV)
            S.op("V", lambda e: e.tensor_tensor(
                P.S_run[:, 4:8, :].rearrange("p a b -> p (a b)"),
                P.S_run[:, 4:8, :].rearrange("p a b -> p (a b)"),
                pf[1][:, 0:260], ALU.add), [], P.sV)
            P.srun_w = P.sV.at()
            S.op("V", lambda e: e.tensor_copy(
                P.S_runb.rearrange("p a b -> p (a b)"),
                P.S_run.rearrange("p a b -> p (a b)")), [P.srun_w], P.sV)
            srunb_ok = P.sV.at()
            if A2LVL < 6:
                continue
            pbk2 = P.pbt2 if c % 2 == 0 else P.pbt
            prev_ev2 = P.sV.at()
            for hp in range(8):
                S.op("T", lambda e, hp=hp, b=pbk2: e.transpose(
                    b[:, 128 * hp:128 * (hp + 1)],
                    P.att_tok1[:, 2 * hp:2 * hp + 2, :]
                    .rearrange("p a b -> p (a b)"), P.idn[:]),
                    [att_ok, prev_ev2] if hp == 0 else [], P.sT)
            atr_ok = P.sT.at()
            S.op("V", lambda e, b=pbk2, c2=c2: e.tensor_copy(
                P.attFt[:, :, 128 * c2:128 * (c2 + 1)],
                b[:, 0:1024].rearrange("p (a b) -> p a b", a=8)),
                [atr_ok], P.sV)
            attf_ok.append(P.sV.at())
        if A2LVL < 7:
            continue
        S.dma("S", P.wbig[:], wo_bf.rearrange("c p n -> p c n"),
              [(P.sT, y_mm[1])])
        wo_ld = P.sX.at()
        for oc in range(8):
            bank = pf[4] if oc % 2 == 0 else pf[5]
            prev = P.sV.at()
            for ec in range(8):
                S.op("T", lambda e, ec=ec, o=oc, b=bank: e.matmul(
                    b[:, 0:256], P.wbig[:, ec, 128 * o:128 * (o + 1)],
                    P.attFt[:, ec, :], start=(ec == 0), stop=(ec == 7)),
                    [wo_ld, attf_ok[-1], prev] if ec == 0 else [], P.sT)
            mm_ok = P.sT.at()
            S.op("V", lambda e, o=oc, s=ssl, b=bank: e.scalar_tensor_tensor(
                P.xst[:, o, s], b[:, 0:256], P.l1bo[:, o:o + 1], P.xst[:, o, s],
                ALU.add, ALU.add), [mm_ok], P.sV)
        last_wo_mm = P.sT.at()
    att1_ok = P.sV.at()
    if L1_PARTS == "att":
        S.flush()
        return att1_ok

    S.dma("G", P.b2s[:], H["l1_b2"][:], [att1_ok])
    b2_ld = P.sW.at()
    slot_free = {0: P.sT.at(), 1: P.sT.at(), 2: P.sT.at()}
    unit_ld = {}

    def load_unit(u):
        slot = u % 3
        S.dma("G", P.u_w1[slot].rearrange("p e c n -> p (e c) n"),
              H["l1_w1"][u].rearrange("m p n -> p m n"),
              [slot_free[slot], att1_ok])
        S.dma("G", P.u_w2[slot].rearrange("p e c n -> p (e c) n"),
              H["l1_w2"][u].rearrange("m p n -> p m n"),
              [slot_free[slot]])
        unit_ld[u] = P.sW.at()

    if L1_PARTS != "nomoe":
        load_unit(0)
        load_unit(1)
        load_unit(2)
    cwb_free = {0: P.sV.at(), 1: P.sV.at()}
    P.partial_free = P.sV.at()
    for grp in ([] if L1_PARTS == "nomoe" else range(8)):
        if 2 * grp + 3 < 16 and grp > 0:
            load_unit(2 * grp + 2)
        for tt in range(8):
            tsl = slice(256 * tt, 256 * (tt + 1))
            if tt == 0 and grp == 0:
                P.mm1_bank_free = {0: P.sA.at(), 1: P.sA.at()}
                P.gsil_free = P.sV.at()
                P.gtil_free = P.sT.at()
            for ei in range(8):
                e_glob = 8 * grp + ei
                u = 2 * grp + ei // 4
                eu = ei % 4
                bank = pf[0] if ei % 2 == 0 else pf[1]
                prev_g = P.mm1_bank_free[ei % 2]
                for hc in range(2):
                    for ic in range(8):
                        S.op("T", lambda e, u=u % 3, eu=eu, i=ic, hc=hc, t=tsl,
                             b=bank: e.matmul(
                                 b[:, 256 * hc:256 * (hc + 1)],
                                 P.u_w1[u][:, eu, i, 128 * hc:128 * (hc + 1)],
                                 P.xnb[:, i, t], start=(i == 0), stop=(i == 7)),
                             [unit_ld[u], prev_g] if (hc == 0 and ic == 0) else [],
                             P.sT)
                mm1_ok = P.sT.at()
                for hc in range(2):
                    S.op("A", lambda e, hc=hc, eg=e_glob, b=bank: e.activation(
                        P.gsil[:, hc, :], b[:, 256 * hc:256 * (hc + 1)], AF.Silu,
                        bias=P.b1s[:, eg, hc:hc + 1]), [mm1_ok, P.gsil_free], P.sA)
                sil_ok = P.sA.at()
                P.mm1_bank_free[ei % 2] = sil_ok
                cwb = (P.pbt if ei % 2 == 0 else P.pbt2)[:, 0:512].bitcast(F32)
                if L1_PARTS != "nocwns":
                    S.op("T", lambda e, eg=e_glob, t=tsl, o=cwb: e.matmul(
                        o[:, 0:256],
                        free_bcast(P.idn[0:64, eg % 64:eg % 64 + 1], 128),
                        P.cwF[0:64, t], start=True, stop=True),
                        [cwF_ok, cwb_free[ei % 2]], P.sT)
                    cwb_mm = P.sT.at()
                for hc in range(2):
                    if L1_PARTS == "nocwns":
                        S.op("V", lambda e, hc=hc: e.tensor_copy(
                            P.gtil[:, hc, :], P.gsil[:, hc, :]),
                            [sil_ok, P.gtil_free], P.sV)
                    else:
                        S.op("V", lambda e, hc=hc, o=cwb: e.tensor_tensor(
                            P.gtil[:, hc, :], P.gsil[:, hc, :],
                            o[:, 0:256],
                            ALU.mult), [sil_ok, cwb_mm, P.gtil_free], P.sV)
                gt_ok = P.sV.at()
                P.gsil_free = gt_ok
                cwb_free[ei % 2] = gt_ok
                for oc in range(8):
                    pb_i, colh = divmod(oc, 2)
                    bank2 = pf[2 + pb_i]
                    for hc in range(2):
                        st = (ei == 0 and oc % 2 == 0 and hc == 0)
                        sp = (grp > 0 and ei == 7 and oc % 2 == 1 and hc == 1)
                        S.op("T", lambda e, u=u % 3, eu=eu, hc=hc, o=oc, b=bank2,
                             ch=colh, st=st, sp=sp: e.matmul(
                                 b[:, 256 * ch:256 * (ch + 1)],
                                 P.u_w2[u][:, eu, hc, 128 * o:128 * (o + 1)],
                                 P.gtil[:, hc, :], start=st, stop=sp),
                             [gt_ok, P.partial_free] if (oc == 0 and hc == 0)
                             else [], P.sT)
                P.gtil_free = P.sT.at()
            for oc in range(8):
                pb_i, colh = divmod(oc, 2)
                bank2 = pf[2 + pb_i]
                if grp == 0:
                    S.op("T", lambda e, o=oc, b=bank2, ch=colh, t=tsl,
                         last=(oc % 2 == 1): e.matmul(
                        b[:, 256 * ch:256 * (ch + 1)],
                        P.b2s[0:64, 128 * o:128 * (o + 1)], P.cwF[0:64, t],
                        start=False, stop=last), [cwF_ok, b2_ld], P.sT)
                else:
                    pass
            close_mm = P.sT.at()
            for oc in range(8):
                pb_i, colh = divmod(oc, 2)
                bank2 = pf[2 + pb_i]
                S.op("V", lambda e, o=oc, b=bank2, ch=colh, t=tsl: e.tensor_tensor(
                    P.xst[:, o, t], P.xst[:, o, t], b[:, 256 * ch:256 * (ch + 1)],
                    ALU.add), [close_mm], P.sV)
            P.partial_free = P.sV.at()
        slot_free[(2 * grp) % 3] = P.sT.at()
        slot_free[(2 * grp + 1) % 3] = P.sT.at()
        if 2 * grp + 3 < 16:
            load_unit(2 * grp + 3)
        elif grp == 0:
            load_unit(3)
    moe_ok = P.sV.at()
    if L1_PARTS in ("nosh", "nocwns"):
        S.flush()
        return moe_ok

    if L1_PARTS == "nomoe":
        moe_ok = att1_ok
    for j in range(2):
        S.dma("G", P.sw1j,
              H["l1_sw1"][j].rearrange("c p n -> p c n"),
              [moe_ok if j == 0 else P.sV.at()])
        S.dma("G", P.sw2j,
              H["l1_sw2"][j].rearrange("c p n -> p c n"), [])
        sw_ld = P.sW.at()
        for tt in range(8):
            tsl = slice(256 * tt, 256 * (tt + 1))
            if tt == 0:
                sh_bank_free = {0: P.sA.at(), 1: P.sA.at()}
                sh_gsil_free = {0: P.sT.at(), 1: P.sT.at()}
            for hc in range(16):
                bank = pf[0] if hc % 2 == 0 else pf[1]
                for ic in range(8):
                    S.op("T", lambda e, i=ic, hc=hc, t=tsl, b=bank: e.matmul(
                        b[:, 0:256], P.sw1j[:, i, 128 * hc:128 * (hc + 1)],
                        P.xnb[:, i, t], start=(i == 0), stop=(i == 7)),
                        [sw_ld, sh_bank_free[hc % 2]] if ic == 0 else [], P.sT)
                mm1_ok = P.sT.at()
                S.op("A", lambda e, hc=hc, j=j, b=bank: e.activation(
                    P.gsil[:, hc % 2, :], b[:, 0:256], AF.Silu,
                    bias=P.sb1s[:, j, hc:hc + 1]), [mm1_ok, sh_gsil_free[hc % 2]],
                     P.sA)
                sil_ok = P.sA.at()
                sh_bank_free[hc % 2] = sil_ok
                for oc in range(8):
                    pb_i, colh = divmod(oc, 2)
                    bank2 = pf[2 + pb_i]
                    S.op("T", lambda e, hc=hc, o=oc, b=bank2, ch=colh,
                         st=(hc == 0 and oc % 2 == 0),
                         sp=(hc == 15 and oc % 2 == 1): e.matmul(
                             b[:, 256 * ch:256 * (ch + 1)],
                             P.sw2j[:, hc, 128 * o:128 * (o + 1)],
                             P.gsil[:, hc % 2, :], start=st, stop=sp),
                         [(P.sA, sil_ok[1]), P.partial_free]
                         if (oc == 0 and hc == 0) else
                         ([(P.sA, sil_ok[1])] if oc == 0 else []), P.sT)
                sh_gsil_free[hc % 2] = P.sT.at()
            close_mm = P.sT.at()
            if L1_DEBUG_SHTAP and ((j == 0 and tt == 0) or (j == 1 and tt == 7)):
                dbase = 0 if j == 0 else 16
                for oc in range(8):
                    pb_i, colh = divmod(oc, 2)
                    bank2 = pf[2 + pb_i]
                    for half in range(2):
                        S.op("V", lambda e, o=oc, b=bank2, ch=colh, h2=half,
                             ba=dbase: e.tensor_copy(
                                 P.shdbg[:, ba + 2 * o + h2, :],
                                 b[:, 256 * ch + 128 * h2:256 * ch + 128 * (h2 + 1)]),
                             [close_mm], P.sV)
            for oc in range(8):
                pb_i, colh = divmod(oc, 2)
                bank2 = pf[2 + pb_i]
                if j == 1:
                    S.op("V", lambda e, o=oc, b=bank2, ch=colh, t=tsl:
                         e.scalar_tensor_tensor(
                             P.xst[:, o, t], b[:, 256 * ch:256 * (ch + 1)],
                             P.sb2sum[:, o:o + 1], P.xst[:, o, t], ALU.add,
                             ALU.add), [close_mm], P.sV)
                else:
                    S.op("V", lambda e, o=oc, b=bank2, ch=colh, t=tsl:
                         e.tensor_tensor(
                             P.xst[:, o, t], P.xst[:, o, t],
                             b[:, 256 * ch:256 * (ch + 1)], ALU.add),
                         [close_mm], P.sV)
            P.partial_free = P.sV.at()
    sh_ok = P.sV.at()
    S.flush()
    return sh_ok




def build_l1_moe_only(P, S, H, cw_dram, xn_ok, x_ld):
    pf = P.pfb
    for t16 in range(NCH):
        slc = slice(128 * t16, 128 * (t16 + 1))
        bank = pf[2] if t16 % 2 == 0 else pf[3]
        prev = P.sV.at()
        for ic in range(8):
            S.op("T", lambda e, i=ic, s=slc, b=bank: e.matmul(
                b[:, 0:NE], P.xnb[:, i, s], P.l1rw[:, i, :],
                start=(i == 0), stop=False),
                [prev, xn_ok, x_ld] if ic == 0 else [], P.sT)
        S.op("T", lambda e, b=bank: e.matmul(
            b[:, 0:NE], P.ones1[:], P.l1rb[:], start=False, stop=True), [], P.sT)
        mm_ok = P.sT.at()
        S.op("V", lambda e, t=t16, b=bank: e.tensor_copy(P.lg[:, t, :], b[:, 0:NE]),
             [mm_ok], P.sV)
    lg_ok = P.sV.at()
    mx = P.dden[:, 0:16]
    mxb = free_bcast(P.dden[:, 0:16], NE)
    r8b = free_bcast(P.dden[:, 16:32], NE)
    S.op("V", lambda e: e.tensor_reduce(mx, P.lg[:], AX.X, ALU.max), [lg_ok], P.sV)
    S.op("V", lambda e: e.drain(), [], None)
    S.op("V", lambda e: e.tensor_tensor(P.wk2[:], P.lg[:], mxb, ALU.subtract),
         [], P.sV)
    sub_ok = P.sV.at()
    S.op("A", lambda e: e.activation(P.lg.rearrange("p a b -> p (a b)"),
                                     P.wk2.rearrange("p a b -> p (a b)"), AF.Exp),
         [sub_ok], P.sA)
    exp_ok = P.sA.at()
    S.op("V", lambda e: e.tensor_copy(P.wk2.rearrange("p a b -> p (a b)"),
                                      P.lg.rearrange("p a b -> p (a b)")),
         [exp_ok], P.sV)
    S.op("V", lambda e: e.memset(P.sel.rearrange("p a b -> p (a b)"), 0.0), [], P.sV)
    for it in range(8):
        S.op("V", lambda e: e.tensor_reduce(mx, P.wk2[:], AX.X, ALU.max), [], P.sV)
        S.op("V", lambda e: e.drain(), [], None)
        S.op("V", lambda e: e.tensor_tensor(P.mtmp[:], P.wk2[:], mxb, ALU.is_ge),
             [], P.sV)
        S.op("V", lambda e: e.tensor_tensor(P.sel[:], P.sel[:], P.mtmp[:], ALU.max),
             [], P.sV)
        S.op("V", lambda e: e.scalar_tensor_tensor(
            P.wk2.rearrange("p a b -> p (a b)"), P.mtmp.rearrange("p a b -> p (a b)"),
            -1e30, P.wk2.rearrange("p a b -> p (a b)"), ALU.mult, ALU.add), [], P.sV)
    S.op("V", lambda e: e.tensor_tensor(P.wk2[:], P.lg[:], P.sel[:], ALU.mult),
         [], P.sV)
    S.op("V", lambda e: e.tensor_reduce(mx, P.wk2[:], AX.X, ALU.add), [], P.sV)
    S.op("V", lambda e: e.drain(), [], None)
    S.op("V", lambda e: e.reciprocal(P.dden[:, 16:32], mx), [], P.sV)
    S.op("V", lambda e: e.drain(), [], None)
    S.op("V", lambda e: e.tensor_tensor(P.mtmp[:], P.wk2[:], r8b, ALU.mult), [], P.sV)
    cw_ok = P.sV.at()
    for t16 in range(NCH):
        pbk = P.pbt if t16 % 2 == 0 else P.pbt2
        prev_ev = P.sV.at()
        S.op("T", lambda e, t=t16, b=pbk: e.transpose(
            b[0:64, 896:1024], P.mtmp[:, t, :], P.idn[:]), [cw_ok, prev_ev], P.sT)
        tr_ok = P.sT.at()
        S.op("V", lambda e, t=t16, b=pbk: e.tensor_copy(
            P.cwF[0:64, 128 * t:128 * (t + 1)], b[0:64, 896:1024]), [tr_ok], P.sV)
    cwF_ok = P.sV.at()
    S.dma("S", cw_dram[:], P.cwF[:], [cwF_ok])
    cwd_ok = P.sX.at()
    att1_ok = P.sV.at()
    P.moe_inline = (cwF_ok, cwd_ok, att1_ok)
    return None



def build_program(part="l0", taps=()):
    nc = bass.Bass("TRN2", target_bir_lowering=False)
    want_l0 = part in ("l0", "fused")
    want_l1 = part in ("l1", "fused")

    H = {}

    def hin(name, shape, dtype=F32):
        H[name] = nc.dram_tensor(name, list(shape), dtype, kind="ExternalInput").ap()
        return H[name]

    hin("x_in", (8, 128, NT))
    x_out = nc.dram_tensor("x_out", [8, 128, NT], F32, kind="ExternalOutput").ap()
    hin("c_idn", (128, 128))
    hin("c_tri", (128, 128))
    hin("c_pmask", (128, 1))
    hin("c_hmask", (128, 2))
    if want_l0:
        hin("c_nodemask", (64, 46))
        hin("c_projmask", (64, 9, 64))
        hin("l0_wq", (8, 128, DH))
        hin("l0_wk", (8, 128, DH))
        hin("l0_wv", (8, 128, DH))
        hin("l0_wo", (DH, D))
        for nmx in ("UL", "VR", "UR", "VL"):
            for l in range(4):
                hin(f"l0_{nmx}T{l}", (RL[l], BS[l]))
        hin("l0_WT", (8, 128, D))
        hin("l0_b", (128, 8))
        hin("l0_ldT", (100, 4))
        hin("l0_luT", (4, 50))
        hin("l0_da", (100, D))
        hin("l0_uaT", (50, D))
        cc0_in = nc.dram_tensor("cc0_in", [DH, DH + 1], F32, kind="Internal").ap()
        cc0_out = nc.dram_tensor("cc0_out", [2 * DH, DH + 1], F32, kind="Internal").ap()
    if want_l1:
        for w in ("wq", "wk", "wv", "wo"):
            hin(f"l1_{w}T", (8, 128, D))
        hin("l1_bo", (128, 8))
        hin("l1_rw", (8, 128, NE))
        hin("l1_rb", (1, NE))
        hin("l1_w1", (16, 32, 128, EH))
        hin("l1_w2", (16, 8, 128, D))
        hin("l1_b1", (128, NE, 2))
        hin("l1_b2", (NE, D))
        hin("l1_sw1", (2, 8, 128, SH))
        hin("l1_sw2", (2, 16, 128, D))
        hin("l1_sb1", (128, 2, 16))
        hin("l1_sb2", (128, 2, 8))
        cc1_in = nc.dram_tensor("cc1_in", [NH * DH, DH + 1], F32, kind="Internal").ap()
        cc1_out = nc.dram_tensor("cc1_out", [2 * NH * DH, DH + 1], F32,
                                 kind="Internal").ap()
        cw_dram = nc.dram_tensor("cw_dram", [NE, NT], BF16, kind="Internal").ap()
        wq_bf = nc.dram_tensor("wq_bf", [8, 128, D], BF16, kind="Internal").ap()
        wo_bf = nc.dram_tensor("wo_bf", [8, 128, D], BF16, kind="Internal").ap()

    tap_out = {t: nc.dram_tensor("tap_" + t, TAPS[t][0], TAPS[t][1],
                                 kind="ExternalOutput").ap() for t in taps}

    with ExitStack() as ctx:
        P = Prog(nc, ctx)
        S = Sched(P)

        P.sb("xst", [128, 8, NT], F32)
        P.sb("xnb", [128, 8, NT], BF16)
        P.sb("AR", [128, 55488], BF16)
        P.sb("idn", [128, 128], BF16)
        P.sb("tri", [128, 128], BF16)
        P.sb("pmask", [128, 1], F32)
        P.sb("hmask", [128, 2], F32)
        P.sb("ones1f", [1, 128], F32)
        P.sb("ones1", [1, 128], BF16)
        P.sb("onesc", [128, 1], BF16)
        P.sb("onesk", [128, 1], BF16)
        P.carve("rows4", [1, 3 * NT], F32, 43232)
        P.accm = P.rows4[0:1, 0 * NT:1 * NT]
        P.accv = P.rows4[0:1, 1 * NT:2 * NT]
        P.rrow = P.rows4[0:1, 2 * NT:3 * NT]
        P.carve("xbch", [128, NT], BF16, 67808)
        P.carve("xsq", [128, NT], BF16, 71904)
        P.carve("xsc32", [128, 512], F32, 76000)
        P.carve("etmp", [64, 1024], F32, 78048)
        P.carve("etmp2", [64, 1024], F32, 82144)
        P.pfb = [P.ps(f"pf{i}", [128, 512], F32) for i in range(6)]
        P.ps("pbt", [128, 1024], BF16)
        P.ps("pbt2", [128, 1024], BF16)

        if want_l0:
            P.carve("s_projmask", [64, 9, 64], BF16, 104832)
            P.carve("s_nodemask", [64, 46], F32, 105984)
            P.sb("l0b", [128, 8], F32)
            for nmx in ("UL", "VR", "UR", "VL"):
                for l in range(4):
                    P.sb(f"l0{nmx}T{l}", [RL[l], BS[l]], F32)
            P.carve("pqb", [64, NT], BF16, 0)
            P.carve("kb", [64, NT], BF16, 4096)
            P.carve("vb", [64, NT], BF16, 8192)
            P.carve("attF", [64, NT], BF16, 12288)
            P.carve("kTa", [128, NCH, 64], BF16, 16384)
            P.carve("vexta", [128, NCH, 65], BF16, 18432)
            P.carve("tfF", [64, 8, 128], BF16, 20544)
            P.carve("tfT", [128, 8, 64], BF16, 22592)
            P.carve("msk", [64, 8, 128], BF16, 23616)
            P.carve("Am", [128, 48, 128], BF16, 86272)
            P.carve("att_tok", [128, 64], BF16, 27712)
            P.carve("crun", [64, 65], F32, 27840)
            P.carve("crb", [64, 65], BF16, 28104)
            P.carve("cts", [64, 64], BF16, 28240)
            P.carve("el", [64, 4, 64], BF16, 28368)
            P.carve("elt", [64, 4, 64], BF16, 28880)
            P.carve("eleaf", [64, 64], BF16, 29392)
            P.carve("s1b", [64, 8, 64], BF16, 98560)
            P.carve("gext", [64, 65], BF16, 29776)
            P.carve("ctot", [64, 65], F32, 29908)
            P.carve("gth", [64, 2, 65], F32, 30172)
            P.carve("bstg", [32, 32], BF16, 30696)
            P.carve("Bmat", [64, 16, 64], BF16, 30760)
            P.carve("wct", [128, 8, D], BF16, 43232)
            P.carve("wtb", [128, 1, D], F32, 59616)
            P.carve("amat", [4, D], BF16, 63712)
            P.carve("bmat2", [50, D], BF16, 65760)
            P.carve("l0wq", [128, 8, DH], BF16, 32800)
            P.carve("l0wk", [128, 8, DH], BF16, 33824)
            P.carve("l0wv", [128, 8, DH], BF16, 34848)
            P.carve("l0wo", [DH, D], BF16, 35872)
            P.carve("l0da", [100, D], BF16, 37920)
            P.carve("l0uaT", [50, D], BF16, 39968)
            P.carve("l0ldT", [100, 4], BF16, 42016)
            P.carve("l0luT", [4, 50], BF16, 42080)
            P.tmpa = P.etmp[0:64, 0:512]
            P.tmpr = P.etmp[0:64, 512:1024]
            P.dcol = P.xsc32[:, 504:512]
        if want_l1:
            P.carve("l1rw", [128, 8, NE], BF16, 108928)
            P.carve("l1rb", [1, NE], BF16, 109952)
            P.sb("b1s", [128, NE, 2], F32)
            P.carve("b2s", [64, D], BF16, 106880)
            P.sb("sb1s", [128, 2, 16], F32)
            P.sb("sb2s", [128, 2, 8], F32)
            P.sb("sb2sum", [128, 8], F32)
            P.sb("l1bo", [128, 8], F32)

        for c8 in range(8):
            S.dma("S", P.xst[:, c8, :], H["x_in"][c8])
        S.dma("G", P.idn[:], H["c_idn"][:])
        S.dma("G", P.tri[:], H["c_tri"][:])
        S.dma("S", P.pmask[:], H["c_pmask"][:])
        S.dma("S", P.hmask[:], H["c_hmask"][:])
        if want_l0:
            S.dma("S", P.s_nodemask[:], H["c_nodemask"][:])
            S.dma("G", P.s_projmask[:], H["c_projmask"][:])
            S.dma("G", P.l0wq[:], H["l0_wq"].rearrange("c p n -> p c n"))
            S.dma("G", P.l0wk[:], H["l0_wk"].rearrange("c p n -> p c n"))
            S.dma("G", P.l0wv[:], H["l0_wv"].rearrange("c p n -> p c n"))
            S.dma("G", P.l0wo[:], H["l0_wo"][:])
            for nmx in ("UL", "VR", "UR", "VL"):
                for l in range(4):
                    S.dma("S", getattr(P, f"l0{nmx}T{l}")[:], H[f"l0_{nmx}T{l}"][:])
            S.dma("S", P.l0b[:], H["l0_b"][:])
            S.dma("G", P.l0ldT[:], H["l0_ldT"][:])
            S.dma("G", P.l0luT[:], H["l0_luT"][:])
            S.dma("G", P.l0da[:], H["l0_da"][:])
            S.dma("G", P.l0uaT[:], H["l0_uaT"][:])
        if want_l1:
            S.dma("G", P.l1rw[:], H["l1_rw"].rearrange("c p n -> p c n"))
            S.dma("G", P.l1rb[:], H["l1_rb"][:])
            S.dma("S", P.b1s[:], H["l1_b1"][:])
            S.dma("S", P.sb1s[:], H["l1_sb1"][:])
            S.dma("S", P.sb2s[:], H["l1_sb2"][:])
            S.dma("S", P.l1bo[:], H["l1_bo"][:])
        x_ld = P.sX.at()
        w_ld = P.sW.at()

        def prep_ones(e):
            e.memset(P.ones1[:], 1.0)
            e.memset(P.ones1f[:], 1.0)
            e.memset(P.onesc[:], 1.0)
            return e.memset(P.onesk[:], 1.0)

        S.op("V", prep_ones, [], P.sV)

        xn_ok = phase_ln(P, S, x_ld)
        S.flush()

        if want_l0:
            last = build_l0(P, S, H, cc0_in, cc0_out, xn_ok, w_ld, x_ld)
        else:
            last = xn_ok
        if want_l1:
            S.op("V", lambda e: e.tensor_tensor(P.sb2sum[:], P.sb2s[:, 0, :],
                                                P.sb2s[:, 1, :], ALU.add),
                 [x_ld], P.sV)
            if want_l0:
                xn1_ok = phase_ln(P, S, last)
                S.flush()
            else:
                xn1_ok = last
            last = build_l1(P, S, H, cc1_in, cc1_out, cw_dram, xn1_ok, x_ld,
                            wq_bf, wo_bf)

        for c8 in range(8):
            S.dma("S", x_out[c8], P.xst[:, c8, :], [last], P.sO)
        for t in taps:
            S.dma("S", tap_out[t], getattr(P, t)[:], [last], P.sO)
        fin = P.sO.at()
        S.op("S", lambda e: e.nop(), [fin])
        S.flush()

    return nc



def prep_core_inputs(inputs, cid, part):
    t = lambda a: np.ascontiguousarray(np.asarray(a, np.float32).T)
    m = dict(host_consts())
    m["c_pmask"] = np.full((128, 1), 1.0 if cid % 2 else 0.0, np.float32)
    x = np.asarray(inputs["L0_attn_x"], np.float32).reshape(16384, D)
    shard = x[cid * NT:(cid + 1) * NT]
    m["x_in"] = np.ascontiguousarray(shard.T).reshape(8, 128, NT)
    if part in ("l0", "fused"):
        m["l0_wq"] = np.asarray(inputs["L0_attn_Wq"], np.float32).reshape(8, 128, DH)
        m["l0_wk"] = np.asarray(inputs["L0_attn_Wk"], np.float32).reshape(8, 128, DH)
        m["l0_wv"] = np.asarray(inputs["L0_attn_Wv"], np.float32).reshape(8, 128, DH)
        m["l0_wo"] = np.ascontiguousarray(np.asarray(inputs["L0_attn_Wo"], np.float32))
        for nmx in ("UL", "VR", "UR", "VL"):
            for l in range(4):
                m[f"l0_{nmx}T{l}"] = t(inputs[f"L0_attn_{nmx}{l}"])
        m["l0_WT"] = t(inputs["L0_ffn_W"]).reshape(8, 128, D)
        m["l0_b"] = np.ascontiguousarray(
            np.asarray(inputs["L0_ffn_b"], np.float32).reshape(8, 128).T)
        m["l0_ldT"] = t(inputs["L0_ffn_lora_down"])
        m["l0_luT"] = t(inputs["L0_ffn_lora_up"])
        m["l0_da"] = np.ascontiguousarray(np.asarray(inputs["L0_ffn_down_aux"], np.float32))
        m["l0_uaT"] = t(inputs["L0_ffn_up_aux"])
    if part in ("l1", "fused"):
        for w in ("Wq", "Wk", "Wv", "Wo"):
            m[f"l1_{w.lower()}T"] = t(inputs[f"L1_attn_{w}"]).reshape(8, 128, D)
        m["l1_bo"] = np.ascontiguousarray(
            np.asarray(inputs["L1_attn_bo"], np.float32).reshape(8, 128).T)
        m["l1_rw"] = np.asarray(inputs["L1_ffn_router_w"], np.float32).reshape(8, 128, NE)
        m["l1_rb"] = np.asarray(inputs["L1_ffn_router_b"], np.float32).reshape(1, NE)
        m["l1_w1"] = np.asarray(inputs["L1_ffn_w1"], np.float32).reshape(16, 32, 128, EH)
        m["l1_w2"] = np.asarray(inputs["L1_ffn_w2"], np.float32).reshape(16, 8, 128, D)
        m["l1_b1"] = np.ascontiguousarray(
            np.asarray(inputs["L1_ffn_b1"], np.float32).reshape(NE, 2, 128)
            .transpose(2, 0, 1))
        m["l1_b2"] = np.asarray(inputs["L1_ffn_b2"], np.float32)
        m["l1_sw1"] = np.asarray(inputs["L1_ffn_sw1"], np.float32).reshape(2, 8, 128, SH)
        m["l1_sw2"] = np.asarray(inputs["L1_ffn_sw2"], np.float32).reshape(2, 16, 128, D)
        m["l1_sb1"] = np.ascontiguousarray(
            np.asarray(inputs["L1_ffn_sb1"], np.float32).reshape(2, 16, 128)
            .transpose(2, 0, 1))
        m["l1_sb2"] = np.ascontiguousarray(
            np.asarray(inputs["L1_ffn_sb2"], np.float32).reshape(2, 8, 128)
            .transpose(2, 0, 1))
    return m


def run_part(inputs, part, taps=(), x_override=None):
    nc = build_program(part, taps)
    in_maps = [prep_core_inputs(inputs, cid, part) for cid in range(8)]
    if x_override is not None:
        for cid in range(8):
            in_maps[cid]["x_in"] = np.ascontiguousarray(x_override[cid])
    return run_bass_kernel_spmd(nc, in_maps, core_ids=list(range(8)))


def kernel(**inputs):
    res = run_part(inputs, "fused")
    shards = [r["x_out"].reshape(D, NT).T for r in res.results]
    return np.concatenate(shards, axis=0).reshape(4, 4096, D)
